# version 15
# baseline (speedup 1.0000x reference)
import math
from contextlib import ExitStack

import numpy as np
import ml_dtypes

import concourse.bass as bass
import concourse.mybir as mybir
from concourse.bass_utils import run_bass_kernel_spmd

F32 = mybir.dt.float32
BF16 = mybir.dt.bfloat16
AF = mybir.ActivationFunctionType
ALU = mybir.AluOpType
AX = mybir.AxisListType

N_CORES = 8
B = 2
SEQ = 4096
D = 4096
NCH = 32
SLOC = 512
TLOC = 1024
YST = 1032
NEG = -1e9
RS128 = 1.0 / math.sqrt(128.0)
RS64 = 0.125

SUBLAYERS = list(range(1, 9))
MODE = "fused"
DEBUG_TAP = None
DEBUG_RES = {}


def _tile_lhsT(WT):
    Dk, O = WT.shape
    return np.ascontiguousarray(
        WT.reshape(Dk // 128, 128, O // 128, 128).transpose(1, 2, 0, 3))


def _tile_rhs(W):
    Dk, O = W.shape
    return np.ascontiguousarray(W.reshape(Dk // 128, 128, O).transpose(1, 0, 2))


def _bias_col(b):
    return np.ascontiguousarray(np.asarray(b, np.float32).reshape(-1, 128).T)


def prep_host(inputs):
    f32 = np.float32
    perm = np.asarray(inputs["L0_ffn_perm"]).astype(np.int64)
    inv = np.argsort(perm)
    sh = {}

    x = np.asarray(inputs["L0_attn_a_x"], dtype=f32)
    xp = np.ascontiguousarray(x[:, :, perm])
    per_core = []
    for c in range(N_CORES):
        xc = xp[:, SLOC * c: SLOC * (c + 1), :]
        xf = np.ascontiguousarray(xc.transpose(2, 0, 1).reshape(NCH, 128, TLOC))
        per_core.append({"x_in": xf})

    sh["c_ones_f"] = np.ones((128, 128), f32)
    sh["c_negones"] = np.full((1, 128), -1.0, f32)
    sh["c_onescol"] = np.ones((128, 1), f32)
    sh["c_ident"] = np.eye(128, dtype=f32)
    idx = np.arange(128)
    sh["c_mask_diag"] = np.where(idx[:, None] <= idx[None, :], 0.0, NEG).astype(f32)
    sh["c_mask_tail"] = np.where(idx[:, None] > idx[None, :], 0.0, NEG).astype(f32)
    PT = np.zeros((128, 128), f32)
    for e in range(64):
        PT[e + 64, e] = -1.0
        PT[e, e + 64] = 1.0
    sh["c_rotPT"] = PT
    E = np.zeros((8, 128), f32)
    for e in range(8):
        E[e, 16 * e: 16 * (e + 1)] = 1.0
    sh["c_expE"] = E

    half = np.arange(0, 128, 2, dtype=f32)
    inv_freq = 1.0 / (10000.0 ** (half / 128.0))
    t = np.arange(SEQ, dtype=f32)
    emb = np.concatenate([np.outer(t, inv_freq)] * 2, axis=1)
    cosT = np.cos(emb).T.astype(f32)
    sinT = np.sin(emb).T.astype(f32)
    for c in range(N_CORES):
        per_core[c]["c_cosT"] = np.ascontiguousarray(cosT[:, SLOC * c: SLOC * (c + 1)])
        per_core[c]["c_sinT"] = np.ascontiguousarray(sinT[:, SLOC * c: SLOC * (c + 1)])
        vcol = np.zeros((128, 12), f32)
        for j in range(12):
            if c + j // 4 - 2 < 0:
                vcol[:, j] = NEG
        per_core[c]["b0_vcol"] = vcol
        hv = np.zeros((128, 2), f32)
        hv[:, 0] = 1.0 if c > 0 else 0.0
        hv[:, 1] = 1.0 if c < N_CORES - 1 else 0.0
        per_core[c]["b1_hv"] = hv

    for nm in ("Wq", "Wk", "Wv", "Wo"):
        W = np.asarray(inputs[f"L0_attn_b_{nm}"], f32)
        WT = np.ascontiguousarray(W[perm, :].T) if nm == "Wo" else \
            np.ascontiguousarray(W[:, perm].T)
        sh[f"b0_{nm}T"] = _tile_lhsT(WT)
    up = np.asarray(inputs["L0_ffn_up"], f32)
    down = np.asarray(inputs["L0_ffn_down"], f32)
    sh["f0_up"] = np.ascontiguousarray(
        up.reshape(16, 2, 128, 8, 128).transpose(0, 2, 1, 3, 4).reshape(16, 128, 2048))
    sh["f0_down"] = np.ascontiguousarray(
        down.reshape(16, 8, 128, 2, 128).transpose(0, 2, 1, 3, 4).reshape(16, 128, 2048))
    for nm in ("Wq", "Wk", "Wv", "Wo"):
        W = np.asarray(inputs[f"L1_attn_a_{nm}"], f32)
        WT = np.ascontiguousarray(W[perm, :].T) if nm == "Wo" else \
            np.ascontiguousarray(W[:, perm].T)
        sh[f"a1_{nm}T"] = _tile_lhsT(WT)
    Wb = np.asarray(inputs["L1_ffn_W_base"], f32)
    sh["f1_WbT"] = _tile_lhsT(np.ascontiguousarray(Wb[perm][:, perm].T))
    sh["f1_bias"] = _bias_col(np.asarray(inputs["L1_ffn_b_base"], f32)[perm])
    Wr = np.asarray(inputs["L1_ffn_W_router"], f32)
    WrT = np.ascontiguousarray(Wr[:, perm].T)
    sh["f1_WrT"] = np.ascontiguousarray(WrT.reshape(NCH, 128, 8).transpose(1, 0, 2))
    A = np.asarray(inputs["L1_ffn_lora_A"], f32)
    A2T = np.ascontiguousarray(A.transpose(2, 0, 1).reshape(D, 128))[perm, :]
    sh["f1_A2T"] = np.ascontiguousarray(A2T.reshape(NCH, 128, 128).transpose(1, 0, 2))
    Bm = np.asarray(inputs["L1_ffn_lora_B"], f32)
    B2 = np.ascontiguousarray(Bm.transpose(0, 2, 1).reshape(128, D))[:, perm]
    sh["f1_B2"] = np.ascontiguousarray(B2.reshape(128, NCH, 128))
    cw = np.asarray(inputs["L1_attn_b_conv_w"], f32)[:, perm, :]
    sh["b1_conv"] = np.ascontiguousarray(
        cw.reshape(3, NCH, 128, 24, 128).transpose(0, 2, 3, 1, 4))
    sh["b1_convb"] = _bias_col(inputs["L1_attn_b_conv_b"])
    for nm in ("wq", "wk", "wv"):
        W2 = np.asarray(inputs[f"L1_attn_b_{nm}"], f32).reshape(1024, 1024)
        if nm == "wv":
            sh["b1_wv_rhs"] = _tile_rhs(W2)
        else:
            sh[f"b1_{nm}"] = _tile_lhsT(W2)
        sh[f"b1_b{nm[1]}"] = _bias_col(
            np.asarray(inputs[f"L1_attn_b_b{nm[1]}"], f32).reshape(1024))
    sh["b1_bv_row"] = np.ascontiguousarray(
        np.asarray(inputs["L1_attn_b_bv"], f32).reshape(1, 1024))
    wo = np.asarray(inputs["L1_attn_b_wo"], f32).reshape(1024, 1024)
    sh["b1_wo"] = _tile_lhsT(wo)
    sh["b1_bo"] = _bias_col(inputs["L1_attn_b_bo"])
    pw = np.asarray(inputs["L1_attn_b_proj_w"], f32)
    sh["b1_proj"] = _tile_lhsT(np.ascontiguousarray(pw[:, perm]))
    sh["b1_projb"] = _bias_col(np.asarray(inputs["L1_attn_b_proj_b"], f32)[perm])
    return sh, per_core, inv



ENGS = ("tensor", "vector", "scalar", "gpsimd", "sync")


class SemT:
    __slots__ = ("h", "n", "name")

    def __init__(self, h, name):
        self.h, self.n, self.name = h, 0, name


class Ctx:
    def __init__(self, nc, stack):
        self.nc = nc
        self.stack = stack
        self.ops = {e: [] for e in ENGS}
        self.sems = []
        self._semi = 0
        self.es = {e: self.new_sem("es_" + e) for e in ENGS}
        self.pid = {}
        self.need_pid = set()

    def new_sem(self, name=None):
        name = f"{name}_{self._semi}" if name else f"s{self._semi}"
        self._semi += 1
        h = self.stack.enter_context(self.nc.semaphore(name))
        s = SemT(h, name)
        self.sems.append(s)
        return s

    def emit(self, eng, build, waits=(), inc=None, inc_by=1):
        waits = [(s.h, v) for (s, v) in waits if v > 0]
        after = None
        ih = None
        if inc is not None:
            inc.n += inc_by
            after = inc.n
            ih = inc.h

        def op(e, _w=waits, _b=build, _ih=ih, _iv=inc_by):
            for (h, v) in _w:
                e.wait_ge(h, v)
            ins = _b(e) if _b is not None else None
            if _ih is not None:
                if ins is None:
                    ins = e.nop()
                ins.then_inc(_ih, _iv)
        self.ops[eng].append(op)
        return after

    def vN(self):
        return self.es["vector"].n

    def barrier(self):
        vals = [(s, s.n) for s in self.sems if s.n > 0]
        for e in ENGS:
            self.emit(e, None, waits=vals)

    def finalize(self, block):
        for e in ENGS:
            ops = self.ops[e]
            sec = getattr(block, e)
            need = e in self.need_pid

            def runner(eng, _ops=ops, _e=e, _need=need):
                if _need:
                    self.pid[_e] = self.nc.partition_id()
                for f in _ops:
                    f(eng)
            sec(runner)


class Slot:
    __slots__ = ("ap", "sem", "free_waits")

    def __init__(self, ap, sem):
        self.ap, self.sem, self.free_waits = ap, sem, []


class Ring:
    def __init__(self, ctx, aps, name):
        self.slots = [Slot(ap, ctx.new_sem(f"{name}{i}")) for i, ap in enumerate(aps)]
        self.i = 0

    def next(self):
        s = self.slots[self.i % len(self.slots)]
        self.i += 1
        return s


def dma(ctx, queue, dst, src, slot_sem=None, waits=()):
    if slot_sem is not None:
        return ctx.emit(queue, lambda e, _d=dst, _s=src: e.dma_start(out=_d, in_=_s),
                        waits=waits, inc=slot_sem, inc_by=16)
    ctx.emit(queue, lambda e, _d=dst, _s=src: e.dma_start(out=_d, in_=_s), waits=waits)
    return None


def precast_w(pr, wname, wshape):
    if not hasattr(pr, "precast"):
        pr.precast = {}
    if wname in pr.precast:
        return pr.precast[wname]
    ctx = pr.ctx
    w = pr.din(wname, wshape)
    n_oc, n_cc = wshape[1], wshape[2]
    g = n_cc * 128
    scr = pr.dint(wname + "_bf", [128, n_oc * g], BF16)
    sem = ctx.new_sem(wname + "_pc")
    flat = w.ap().rearrange("p c j k -> p (c j k)")
    for oc in range(n_oc):
        dma(ctx, "gpsimd", scr.ap()[:, oc * g:(oc + 1) * g],
            flat[:, oc * g:(oc + 1) * g], slot_sem=sem)
    pr.precast[wname] = (scr, sem, sem.n)
    return pr.precast[wname]



class Prog:
    def __init__(self, nc, stack):
        self.nc = nc
        self.stack = stack
        self.ctx = Ctx(nc, stack)
        self.dram = {}
        self.input_names = set()

        def sb(name, shape, dt):
            return stack.enter_context(nc.sbuf_tensor(name, shape, dt))
        self.Y = sb("Y", [128, NCH * YST], BF16)
        self.BIG = sb("BIGB", [128, NCH * 1024], BF16)
        self.AUX = sb("AUX", [128, 8192], BF16)
        self.NB = sb("NB", [128, 1024], BF16)
        self.BRr = sb("BRr", [128, 1024], BF16)
        self.XS = [sb(f"XS{i}", [128, 1024], F32) for i in range(2)]
        self.WG = [sb(f"WG{i}", [128, 4096], BF16) for i in range(2)]
        self.SCR = [sb(f"SCR{i}", [128, 512], F32) for i in range(3)]
        self.PTB = [sb(f"PTB{i}", [128, 512], BF16) for i in range(2)]
        self.KSLB = sb("KSLB", [128, 4096], BF16)
        self.VTB = sb("VTB", [128, 4096], BF16)
        self.ROWS = sb("ROWS", [128, 1024], F32)
        self.STAT = sb("STAT", [128, 160], F32)
        self.WR = sb("WR", [128, 256], BF16)
        self.KNM = sb("KNM", [1, 128], F32)
        self.ones_f = sb("ones_f", [128, 128], F32)
        self.ones_b = sb("ones_b", [1, 128], BF16)
        self.negones_b = sb("negones_b", [1, 128], BF16)
        self.onescol_f = sb("onescol_f", [128, 1], F32)
        self.onescol_b = sb("onescol_b", [128, 1], BF16)
        self.ident_f = sb("ident_f", [128, 128], F32)
        self.mask_diag = sb("mask_diag", [128, 128], BF16)
        self.mask_tail = sb("mask_tail", [128, 128], BF16)
        self.rotPT = sb("rotPT", [128, 128], BF16)
        self.expE = sb("expE", [8, 128], BF16)
        self.cosT = sb("cosT", [128, 512], BF16)
        self.sinT = sb("sinT", [128, 512], BF16)
        self.vcolB = sb("vcolB", [128, 12], F32)
        self.hvB = sb("hvB", [128, 2], F32)
        self.biasC = {}
        for nm, w in (("f1_bias", 32), ("b1_convb", 24), ("b1_bq", 8),
                      ("b1_bk", 8), ("b1_bo", 8), ("b1_projb", 32)):
            self.biasC[nm] = sb("bc_" + nm, [128, w], F32)
        self.W8T = self.KSLB
        self.HSB = self.VTB
        self.HSB2 = self.VTB
        self.PS = [stack.enter_context(nc.psum_tensor(f"PS{i}", [128, 512], F32))
                   for i in range(8)]
        self.bank_free = [[] for _ in range(8)]

        self.r_x = Ring(self.ctx, [t[:, :] for t in self.XS], "rx")
        self.r_wg = Ring(self.ctx, [t[:, :] for t in self.WG], "rwg")
        self.s_store = self.ctx.new_sem("st")
        self.x_src = None
        self.x_dst = None

    def din(self, name, shape, dt=F32):
        if name not in self.dram:
            self.dram[name] = self.nc.dram_tensor(name, list(shape), dt,
                                                  kind="ExternalInput")
            self.input_names.add(name)
        return self.dram[name]

    def dint(self, name, shape, dt=BF16):
        if name not in self.dram:
            self.dram[name] = self.nc.dram_tensor(name, list(shape), dt, kind="Internal")
        return self.dram[name]

    def load_consts(self):
        ctx = self.ctx
        sem = ctx.new_sem("consts")
        pairs = [
            ("c_ones_f", self.ones_f[:, :], (128, 128)),
            ("c_onescol", self.onescol_f[:, :], (128, 1)),
            ("c_ident", self.ident_f[:, :], (128, 128)),
            ("c_mask_diag", self.mask_diag[:, :], (128, 128)),
            ("c_mask_tail", self.mask_tail[:, :], (128, 128)),
            ("c_rotPT", self.rotPT[:, :], (128, 128)),
            ("c_expE", self.expE[0:8, :], (8, 128)),
            ("c_cosT", self.cosT[:, :], (128, 512)),
            ("c_sinT", self.sinT[:, :], (128, 512)),
            ("b0_vcol", self.vcolB[:, :], (128, 12)),
            ("b1_hv", self.hvB[:, :], (128, 2)),
            ("c_negones", self.negones_b[0:1, :], (1, 128)),
        ]
        for nm, dst, shp in pairs:
            t = self.din(nm, shp)
            dma(ctx, "gpsimd", dst, t.ap(), slot_sem=sem)
        for nm, buf in self.biasC.items():
            t = self.din(nm, (128, buf.shape[1]))
            dma(ctx, "gpsimd", buf[:, :], t.ap(), slot_sem=sem)
        ctx.emit("vector", lambda e: e.tensor_copy(self.ones_b[0:1, :], self.ones_f[0:1, :]),
                 waits=[(sem, sem.n)], inc=ctx.es["vector"])
        ctx.emit("vector", lambda e: e.tensor_copy(self.onescol_b[:, :], self.onescol_f[:, :]),
                 inc=ctx.es["vector"])
        ctx.barrier()

    @staticmethod
    def col_n(ch, b):
        return ch * 1024 + b * 512

    @staticmethod
    def col_p(ch, b):
        return ch * YST + 1 + b * 514

    def pass_A(self, padded=False):
        ctx = self.ctx
        col = self.col_p if padded else self.col_n
        sq_free = [[], []]
        for ch in range(NCH):
            sl = self.r_x.next()
            c = dma(ctx, "sync", sl.ap, self.x_src.ap()[ch], slot_sem=sl.sem,
                    waits=sl.free_waits)
            casts = []
            for b in range(2):
                ycols = self.Y[:, col(ch, b): col(ch, b) + 512]
                cv = ctx.emit("scalar",
                              lambda e, _y=ycols, _x=sl.ap[:, b * 512:(b + 1) * 512]:
                              e.copy(_y, _x),
                              waits=[(sl.sem, c)] if b == 0 else [],
                              inc=ctx.es["scalar"])
                casts.append((ycols, cv))
            sl.free_waits = [(ctx.es["scalar"], casts[-1][1])]
            for b in range(2):
                ycols, cv = casts[b]
                si = (2 * ch + b) % 2
                sqa = self.AUX[:, si * 512: si * 512 + 512]
                cs = ctx.emit("scalar",
                              lambda e, _o=sqa, _y=ycols: e.activation(_o, _y, AF.Square),
                              waits=sq_free[si], inc=ctx.es["scalar"])
                ctx.emit("tensor",
                         lambda e, _p=self.PS[b][0:1, :], _r=ycols, _s=(ch == 0),
                                _e=(ch == NCH - 1):
                         e.matmul(_p, self.onescol_b[:, 0:1], _r, start=_s, stop=_e),
                         waits=[(ctx.es["scalar"], cv)], inc=ctx.es["tensor"])
                last = ctx.emit("tensor",
                                lambda e, _p=self.PS[2 + b][0:1, :], _r=sqa, _s=(ch == 0),
                                       _e=(ch == NCH - 1):
                                e.matmul(_p, self.onescol_b[:, 0:1], _r, start=_s, stop=_e),
                                waits=[(ctx.es["scalar"], cs)], inc=ctx.es["tensor"])
                sq_free[si] = [(ctx.es["tensor"], last)]
        self._rows_and_broadcasts()
        ctx.barrier()

    def _rows_and_broadcasts(self):
        ctx = self.ctx
        R = self.ROWS
        S0 = self.SCR[0]
        tN = ctx.es["tensor"].n
        prevv = 0
        for th in range(2):
            c = ctx.emit("vector", lambda e, _p=self.PS[th][0:1, :]:
                     e.tensor_copy(R[0:1, 0:512], _p),
                     waits=[(ctx.es["tensor"], tN), (ctx.es["vector"], prevv)],
                     inc=ctx.es["vector"])
            c = ctx.emit("vector", lambda e, _p=self.PS[2 + th][0:1, :]:
                     e.tensor_copy(S0[0:1, 0:512], _p),
                     waits=[(ctx.es["vector"], c)], inc=ctx.es["vector"])
            c = ctx.emit("vector", lambda e: e.tensor_scalar(R[0:1, 0:512], R[0:1, 0:512],
                                                             1.0 / D, None, ALU.mult),
                     waits=[(ctx.es["vector"], c)], inc=ctx.es["vector"])
            c = ctx.emit("vector", lambda e: e.tensor_tensor(R[0:1, 512:1024], R[0:1, 0:512],
                                                             R[0:1, 0:512], ALU.mult),
                     waits=[(ctx.es["vector"], c)], inc=ctx.es["vector"])
            c = ctx.emit("vector", lambda e: e.tensor_scalar(S0[0:1, 0:512], S0[0:1, 0:512],
                                                             1.0 / D, None, ALU.mult),
                     waits=[(ctx.es["vector"], c)], inc=ctx.es["vector"])
            c = ctx.emit("vector", lambda e: e.tensor_tensor(S0[0:1, 0:512], S0[0:1, 0:512],
                                                             R[0:1, 512:1024], ALU.subtract),
                     waits=[(ctx.es["vector"], c)], inc=ctx.es["vector"])
            cvar = ctx.emit("vector", lambda e: e.tensor_scalar(S0[0:1, 0:512],
                                                                S0[0:1, 0:512],
                                                                1e-5, None, ALU.add),
                            waits=[(ctx.es["vector"], c)], inc=ctx.es["vector"])
            csd = ctx.emit("scalar", lambda e: e.activation(R[0:1, 512:1024],
                                                            S0[0:1, 0:512], AF.Sqrt),
                           waits=[(ctx.es["vector"], cvar)], inc=ctx.es["scalar"])
            crs = ctx.emit("vector", lambda e: e.reciprocal(S0[0:1, 0:512],
                                                            R[0:1, 512:1024]),
                           waits=[(ctx.es["scalar"], csd)], inc=ctx.es["vector"])
            cnm = ctx.emit("vector", lambda e: e.tensor_scalar(R[0:1, 0:512],
                                                               R[0:1, 0:512], -1.0,
                                                               None, ALU.mult),
                           waits=[(ctx.es["vector"], crs)], inc=ctx.es["vector"])
            m1 = ctx.emit("tensor",
                          lambda e: e.matmul(self.PS[4][:, :], self.ones_f[0:1, :],
                                             R[0:1, 0:512], start=True, stop=True),
                          waits=[(ctx.es["vector"], cnm)] + self.bank_free[4],
                          inc=ctx.es["tensor"])
            c1 = ctx.emit("vector",
                          lambda e, _d=self.NB[:, th * 512:(th + 1) * 512]:
                          e.tensor_copy(_d, self.PS[4][:, :]),
                          waits=[(ctx.es["tensor"], m1)], inc=ctx.es["vector"])
            self.bank_free[4] = [(ctx.es["vector"], c1)]
            m2 = ctx.emit("tensor",
                          lambda e: e.matmul(self.PS[5][:, :], self.ones_f[0:1, :],
                                             S0[0:1, 0:512], start=True, stop=True),
                          waits=[(ctx.es["vector"], crs)] + self.bank_free[5],
                          inc=ctx.es["tensor"])
            c2 = ctx.emit("vector",
                          lambda e, _d=self.BRr[:, th * 512:(th + 1) * 512]:
                          e.tensor_copy(_d, self.PS[5][:, :]),
                          waits=[(ctx.es["tensor"], m2)], inc=ctx.es["vector"])
            self.bank_free[5] = [(ctx.es["vector"], c2)]
            prevv = ctx.es["vector"].n

    def pass_B(self, padded=False):
        ctx = self.ctx
        col = self.col_p if padded else self.col_n
        slot_w = {0: [], 1: []}
        for ch in range(NCH):
            for b in range(2):
                ycols = self.Y[:, col(ch, b): col(ch, b) + 512]
                si = (2 * ch + b) % 2
                tt = self.SCR[si][:, :]
                ca = ctx.emit("gpsimd",
                              lambda e, _t=tt, _s=ycols, _n=self.NB[:, b * 512:(b + 1) * 512]:
                              e.tensor_tensor(_t, _s, _n, ALU.add),
                              waits=slot_w[si], inc=ctx.es["gpsimd"])
                cv = ctx.emit("vector",
                              lambda e, _d=ycols, _t=tt, _r=self.BRr[:, b * 512:(b + 1) * 512]:
                              e.tensor_tensor(_d, _t, _r, ALU.mult),
                              waits=[(ctx.es["gpsimd"], ca)], inc=ctx.es["vector"])
                slot_w[si] = [(ctx.es["vector"], cv)]
        ctx.barrier()

    def emit_chunk_stats(self, ch, slot_ap, dep, padded_next):
        ctx = self.ctx
        col = self.col_p if padded_next else self.col_n
        casts = []
        for b in range(2):
            ycols = self.Y[:, col(ch, b): col(ch, b) + 512]
            cv = ctx.emit("scalar",
                          lambda e, _y=ycols, _x=slot_ap[:, b * 512:(b + 1) * 512]:
                          e.copy(_y, _x),
                          waits=[dep] if b == 0 else [],
                          inc=ctx.es["scalar"])
            casts.append((ycols, cv))
        for b in range(2):
            ycols, cv = casts[b]
            si = (2 * ch + b) % 2
            sqa = self.AUX[:, si * 512: si * 512 + 512]
            cs = ctx.emit("scalar",
                          lambda e, _o=sqa, _y=ycols: e.activation(_o, _y, AF.Square),
                          waits=self._sq_free[si], inc=ctx.es["scalar"])
            ctx.emit("tensor",
                     lambda e, _p=self.PS[b][0:1, :], _r=ycols, _s=(ch == 0),
                            _e=(ch == NCH - 1):
                     e.matmul(_p, self.onescol_b[:, 0:1], _r, start=_s, stop=_e),
                     waits=[(ctx.es["scalar"], cv)] +
                           (self.bank_free[b] if ch == 0 else []),
                     inc=ctx.es["tensor"])
            last = ctx.emit("tensor",
                            lambda e, _p=self.PS[2 + b][0:1, :], _r=sqa, _s=(ch == 0),
                                   _e=(ch == NCH - 1):
                            e.matmul(_p, self.onescol_b[:, 0:1], _r, start=_s, stop=_e),
                            waits=[(ctx.es["scalar"], cs)] +
                                  (self.bank_free[2 + b] if ch == 0 else []),
                            inc=ctx.es["tensor"])
            self._sq_free[si] = [(ctx.es["tensor"], last)]
            if ch == NCH - 1:
                self.bank_free[b] = []
                self.bank_free[2 + b] = []
        return (ctx.es["scalar"], casts[-1][1])

    def pass_D(self, delta_buf, dstride=1024, stats_next=None):
        ctx = self.ctx
        self._sq_free = [[], []]
        for ch in range(NCH):
            sl = self.r_x.next()
            c = dma(ctx, "sync", sl.ap, self.x_src.ap()[ch], slot_sem=sl.sem,
                    waits=sl.free_waits)
            dcols = delta_buf[:, ch * dstride: ch * dstride + 1024]
            cv = ctx.emit("vector",
                          lambda e, _x=sl.ap, _d=dcols: e.tensor_tensor(_x, _x, _d, ALU.add),
                          waits=[(sl.sem, c)], inc=ctx.es["vector"])
            cs = dma(ctx, "sync", self.x_dst.ap()[ch], sl.ap, slot_sem=self.s_store,
                     waits=[(ctx.es["vector"], cv)])
            sl.free_waits = [(self.s_store, cs)]
            if stats_next is not None:
                cst = self.emit_chunk_stats(ch, sl.ap, (ctx.es["vector"], cv),
                                            stats_next == "b1")
                sl.free_waits = sl.free_waits + [cst]
        ctx.barrier()

    def pass_A_tail(self):
        self._rows_and_broadcasts()
        self.ctx.barrier()



def stream_proj(pr, wname, wshape, n_oc, n_cc, rhs_fn, out_write,
                bank_base=0, bank_n=8, extra_accum=None, gran_sel=None,
                src_bf=None):
    ctx = pr.ctx
    if src_bf is None:
        w = pr.din(wname, wshape)
    ppr = bank_n // 2
    for oc in range(n_oc):
        sl = pr.r_wg.next()
        if src_bf is not None:
            scr_, csem_, cval_ = src_bf
            g_ = n_cc * 128
            c = dma(ctx, "sync" if oc % 2 == 0 else "scalar",
                    sl.ap[:, 0: n_cc * 128],
                    scr_.ap()[:, oc * g_:(oc + 1) * g_], slot_sem=sl.sem,
                    waits=list(sl.free_waits) + ([(csem_, cval_)] if oc < 2 else []))
        else:
            src = w.ap()[:, oc] if gran_sel is None else gran_sel(w, oc)
            c = dma(ctx, "gpsimd", sl.ap[:, 0: n_cc * 128],
                    src.rearrange("p c j -> p (c j)"), slot_sem=sl.sem,
                    waits=sl.free_waits)
        banks = [bank_base + (oc % ppr) * 2 + th for th in range(2)]
        last = None
        for cc in range(n_cc):
            tile = sl.ap[:, cc * 128:(cc + 1) * 128]
            for th in range(2):
                bw = []
                if cc == 0:
                    bw = list(pr.bank_free[banks[th]])
                    if th == 0:
                        bw = bw + [(sl.sem, c)]
                last = ctx.emit(
                    "tensor",
                    lambda e, _p=pr.PS[banks[th]][:, :], _t=tile, _r=rhs_fn(cc, th),
                           _s=(cc == 0), _e=(cc == n_cc - 1 and extra_accum is None):
                    e.matmul(_p, _t, _r, start=_s, stop=_e),
                    waits=bw, inc=ctx.es["tensor"])
        sl.free_waits = [(ctx.es["tensor"], last)]
        for th in range(2):
            cnt = last
            if extra_accum is not None:
                cnt = extra_accum(oc, th, pr.PS[banks[th]], cnt)
        for th in range(2):
            cnt = ctx.es["tensor"].n
            pr.bank_free[banks[th]] = [out_write(oc, th, pr.PS[banks[th]], cnt)]


def delta_writer(pr, buf, bias=None, stride=1024, col_off=0):
    ctx = pr.ctx

    def wr(oc, th, psum, dep):
        cols = buf[:, col_off + oc * stride + th * 512:
                   col_off + oc * stride + th * 512 + 512]
        if bias is not None:
            cc = ctx.emit("vector",
                          lambda e, _d=cols, _p=psum[:, :], _b=bias[:, oc:oc + 1]:
                          e.tensor_scalar(_d, _p, _b, None, ALU.add),
                          waits=[(ctx.es["tensor"], dep)], inc=ctx.es["vector"])
            return (ctx.es["vector"], cc)
        cc = ctx.emit("vector",
                      lambda e, _d=cols, _p=psum[:, :]: e.tensor_copy(_d, _p),
                      waits=[(ctx.es["tensor"], dep)], inc=ctx.es["vector"])
        return (ctx.es["vector"], cc)
    return wr


def y_rhs(pr):
    return lambda cc, th: pr.Y[:, cc * 1024 + th * 512: cc * 1024 + th * 512 + 512]


def big_rhs(pr):
    return lambda cc, th: pr.BIG[:, cc * 1024 + th * 512: cc * 1024 + th * 512 + 512]



def ph_attn_a(pr, stats_next=None):
    ctx = pr.ctx
    ar_in = pr.dint("aa_in", [128, 96], F32)
    ar_out = pr.dint("aa_out", [128, 96], F32)
    ST = pr.STAT
    red_done = {0: [], 1: [], 2: []}
    for ch in range(NCH):
        y0 = pr.Y[:, ch * 1024: ch * 1024 + 512]
        y1 = pr.Y[:, ch * 1024 + 512: ch * 1024 + 1024]
        for pi, (a, b_, colo) in enumerate(((y0, y0, 0), (y0, y1, 32), (y1, y1, 64))):
            prod = pr.SCR[pi][:, :]
            cg = ctx.emit("gpsimd",
                          lambda e, _p=prod, _a=a, _b=b_:
                          e.tensor_tensor(_p, _a, _b, ALU.mult),
                          waits=red_done[pi], inc=ctx.es["gpsimd"])
            cv = ctx.emit("vector",
                          lambda e, _d=ST[:, colo + ch: colo + ch + 1], _p=prod:
                          e.tensor_reduce(_d, _p, AX.X, ALU.add),
                          waits=[(ctx.es["gpsimd"], cg)], inc=ctx.es["vector"])
            red_done[pi] = [(ctx.es["vector"], cv)]
    cv = ctx.es["vector"].n
    cst = dma(ctx, "sync", ar_in.ap(), ST[:, 0:96], slot_sem=pr.s_store,
              waits=[(ctx.es["vector"], cv)])
    sc = ctx.new_sem("aa_cc")
    ctx.emit("gpsimd",
             lambda e: e.collective_compute(
                 "AllReduce", ALU.add, replica_groups=[list(range(N_CORES))],
                 ins=[ar_in.ap()], outs=[ar_out.ap()]),
             waits=[(pr.s_store, cst)], inc=sc)
    sld = ctx.new_sem("aa_ld")
    cld = dma(ctx, "gpsimd", ST[:, 0:96], ar_out.ap(), slot_sem=sld, waits=[(sc, 1)])
    ctx.emit("vector", lambda e: e.tensor_tensor(ST[:, 96:128], ST[:, 0:32],
                                                 ST[:, 32:64], ALU.subtract),
             waits=[(sld, cld)], inc=ctx.es["vector"])
    c2 = ctx.emit("vector", lambda e: e.tensor_tensor(ST[:, 128:160], ST[:, 32:64],
                                                      ST[:, 64:96], ALU.subtract),
                  inc=ctx.es["vector"])
    ctx.emit("scalar", lambda e: e.activation(ST[:, 96:128], ST[:, 96:128], AF.Sigmoid,
                                              scale=1.0 / 64.0),
             waits=[(ctx.es["vector"], c2)], inc=ctx.es["scalar"])
    ctx.emit("scalar", lambda e: e.activation(ST[:, 128:160], ST[:, 128:160], AF.Sigmoid,
                                              scale=1.0 / 64.0), inc=ctx.es["scalar"])
    ctx.barrier()
    q0 = ST[:, 96:128]
    q1 = ST[:, 128:160]
    pr._sq_free = [[], []]
    gprev = []
    for ch in range(NCH):
        sl = pr.r_x.next()
        c = dma(ctx, "sync", sl.ap, pr.x_src.ap()[ch], slot_sem=sl.sem,
                waits=sl.free_waits)
        y0 = pr.Y[:, ch * 1024: ch * 1024 + 512]
        y1 = pr.Y[:, ch * 1024 + 512: ch * 1024 + 1024]
        x0 = sl.ap[:, 0:512]
        x1 = sl.ap[:, 512:1024]
        t0, t1, t2 = pr.SCR[0][:, :], pr.SCR[1][:, :], pr.SCR[2][:, :]
        p0c = q0[:, ch: ch + 1]
        p1c = q1[:, ch: ch + 1]
        a1 = ctx.emit("vector", lambda e, _t=t0, _y=y0, _p=p0c:
                      e.tensor_scalar(_t, _y, _p, None, ALU.mult), waits=gprev,
                      inc=ctx.es["vector"])
        a2 = ctx.emit("vector", lambda e, _t=t1, _y=y1, _p=p0c:
                      e.tensor_scalar(_t, _y, _p, -1.0, ALU.mult, op1=ALU.mult),
                      inc=ctx.es["vector"])
        a3 = ctx.emit("vector", lambda e, _t=t2, _y=y0, _p=p1c:
                      e.tensor_scalar(_t, _y, _p, None, ALU.mult), inc=ctx.es["vector"])
        ctx.emit("vector", lambda e, _x=x0, _y=y0: e.tensor_tensor(_x, _x, _y, ALU.add),
                 waits=[(sl.sem, c)], inc=ctx.es["vector"])
        ctx.emit("vector", lambda e, _x=x0, _y=y1: e.tensor_tensor(_x, _x, _y, ALU.add),
                 inc=ctx.es["vector"])
        c3 = ctx.emit("vector", lambda e, _x=x0, _t=t0: e.tensor_tensor(_x, _x, _t, ALU.add),
                      inc=ctx.es["vector"])
        ctx.emit("vector", lambda e, _x=x0, _t=t1: e.tensor_tensor(_x, _x, _t, ALU.add),
                 inc=ctx.es["vector"])
        cg1 = ctx.emit("gpsimd", lambda e, _x=x1, _y=y1:
                       e.tensor_scalar(_x, _y, 2.0, None, ALU.mult, accum_out=None)
                       if False else e.tensor_tensor(_x, _x, _y, ALU.add),
                       waits=[(sl.sem, c)], inc=ctx.es["gpsimd"])
        cg2 = ctx.emit("gpsimd", lambda e, _x=x1, _y=y1: e.tensor_tensor(_x, _x, _y, ALU.add),
                       inc=ctx.es["gpsimd"])
        c7 = ctx.emit("vector", lambda e, _x=x1, _t=t2: e.tensor_tensor(_x, _x, _t, ALU.add),
                      waits=[(ctx.es["gpsimd"], cg2)], inc=ctx.es["vector"])
        a4 = ctx.emit("vector", lambda e, _t=t0, _y=y1, _p=p1c:
                      e.tensor_scalar(_t, _y, _p, None, ALU.mult),
                      waits=[(ctx.es["vector"], c3)], inc=ctx.es["vector"])
        cl = ctx.emit("vector", lambda e, _x=x1, _t=t0:
                      e.tensor_tensor(_x, _x, _t, ALU.subtract),
                      waits=[(ctx.es["vector"], a4)], inc=ctx.es["vector"])
        gprev = [(ctx.es["vector"], cl)]
        cs = dma(ctx, "sync", pr.x_dst.ap()[ch], sl.ap, slot_sem=pr.s_store,
                 waits=[(ctx.es["vector"], cl)])
        sl.free_waits = [(pr.s_store, cs)]
        if stats_next is not None:
            cst = pr.emit_chunk_stats(ch, sl.ap, (ctx.es["vector"], cl),
                                      stats_next == "b1")
            sl.free_waits = sl.free_waits + [cst]
    ctx.barrier()



def ph_ffn0(pr):
    ctx = pr.ctx
    precast_w(pr, "b0_WvT", (128, NCH, NCH, 128))
    up = pr.din("f0_up", (16, 128, 2048))
    down = pr.din("f0_down", (16, 128, 2048))
    for h in range(16):
        slu = pr.r_wg.next()
        cu = dma(ctx, "gpsimd", slu.ap[:, 0:2048], up.ap()[h], slot_sem=slu.sem,
                 waits=slu.free_waits)
        sld = pr.r_wg.next()
        cd = dma(ctx, "gpsimd", sld.ap[:, 0:2048], down.ap()[h], slot_sem=sld.sem,
                 waits=sld.free_waits)
        last_up = None
        zlast = None
        for mc in range(8):
            bks = [(mc % 2) * 2, (mc % 2) * 2 + 1]
            for jc in range(2):
                tile = slu.ap[:, jc * 1024 + mc * 128: jc * 1024 + mc * 128 + 128]
                for th in range(2):
                    bw = []
                    if jc == 0:
                        bw = list(pr.bank_free[bks[th]])
                        if th == 0 and mc == 0:
                            bw = bw + [(slu.sem, cu)]
                    ych = 2 * h + jc
                    last_up = ctx.emit(
                        "tensor",
                        lambda e, _p=pr.PS[bks[th]][:, :], _t=tile,
                               _r=pr.Y[:, ych * 1024 + th * 512: ych * 1024 + th * 512 + 512],
                               _s=(jc == 0), _e=(jc == 1):
                        e.matmul(_p, _t, _r, start=_s, stop=_e),
                        waits=bw, inc=ctx.es["tensor"])
            for th in range(2):
                zc = pr.AUX[:, mc * 1024 + th * 512: mc * 1024 + th * 512 + 512]
                cz = ctx.emit("scalar",
                              lambda e, _z=zc, _p=pr.PS[bks[th]][:, :]:
                              e.activation(_z, _p, AF.Gelu),
                              waits=[(ctx.es["tensor"], last_up)], inc=ctx.es["scalar"])
                zlast = cz
                pr.bank_free[bks[th]] = [(ctx.es["scalar"], cz)]
        last_dn = None
        for djc in range(2):
            bks = [4 + djc * 2, 5 + djc * 2]
            for mc in range(8):
                tile = sld.ap[:, mc * 256 + djc * 128: mc * 256 + djc * 128 + 128]
                for th in range(2):
                    bw = []
                    if mc == 0:
                        bw = list(pr.bank_free[bks[th]])
                        if th == 0:
                            bw = bw + [(ctx.es["scalar"], zlast)]
                            if djc == 0:
                                bw = bw + [(sld.sem, cd)]
                    last_dn = ctx.emit(
                        "tensor",
                        lambda e, _p=pr.PS[bks[th]][:, :], _t=tile,
                               _r=pr.AUX[:, mc * 1024 + th * 512: mc * 1024 + th * 512 + 512],
                               _s=(mc == 0), _e=(mc == 7):
                        e.matmul(_p, _t, _r, start=_s, stop=_e),
                        waits=bw, inc=ctx.es["tensor"])
            for th in range(2):
                och = 2 * h + djc
                dcols = pr.BIG[:, och * 1024 + th * 512: och * 1024 + th * 512 + 512]
                cvv = ctx.emit("vector",
                               lambda e, _d=dcols, _p=pr.PS[bks[th]][:, :]:
                               e.tensor_copy(_d, _p),
                               waits=[(ctx.es["tensor"], last_dn)], inc=ctx.es["vector"])
                pr.bank_free[bks[th]] = [(ctx.es["vector"], cvv)]
        slu.free_waits = [(ctx.es["tensor"], last_up)]
        sld.free_waits = [(ctx.es["tensor"], last_dn)]
    ctx.barrier()



def ph_ffn1(pr):
    ctx = pr.ctx
    wr = pr.din("f1_WrT", (128, NCH, 8))
    a2 = pr.din("f1_A2T", (128, NCH, 128))
    b2 = pr.din("f1_B2", (128, NCH, 128))
    sem_w = ctx.new_sem("f1w")
    dma(ctx, "gpsimd", pr.WR[:, 0:256], wr.ap().rearrange("p c j -> p (c j)"),
        slot_sem=sem_w)
    dma(ctx, "gpsimd", pr.AUX[:, 0:4096], a2.ap().rearrange("p c j -> p (c j)"),
        slot_sem=sem_w)
    dma(ctx, "gpsimd", pr.AUX[:, 4096:8192], b2.ap().rearrange("p c j -> p (c j)"),
        slot_sem=sem_w)
    prev_ev = 0
    for th in range(2):
        for cc in range(NCH):
            bw = []
            if cc == 0:
                bw = [(sem_w, 16)] if th == 0 else [(ctx.es["vector"], prev_ev)]
            ctx.emit("tensor",
                     lambda e, _t=pr.WR[:, cc * 8:(cc + 1) * 8],
                            _r=pr.Y[:, cc * 1024 + th * 512: cc * 1024 + th * 512 + 512],
                            _s=(cc == 0), _e=(cc == NCH - 1):
                     e.matmul(pr.PS[6][0:8, :], _t, _r, start=_s, stop=_e),
                     waits=bw, inc=ctx.es["tensor"])
        prev_ev = ctx.emit("vector",
                           lambda e, _d=pr.ROWS[0:8, th * 512:(th + 1) * 512]:
                           e.tensor_copy(_d, pr.PS[6][0:8, :]),
                           waits=[(ctx.es["tensor"], ctx.es["tensor"].n)],
                           inc=ctx.es["vector"])
    lg_ev = prev_ev
    h_mm = None
    for th in range(2):
        for cc in range(NCH):
            bw = []
            if cc == 0:
                bw = list(pr.bank_free[3 + th])
                if th == 0:
                    bw = bw + [(sem_w, 32)]
            h_mm = ctx.emit("tensor",
                            lambda e, _p=pr.PS[3 + th][:, :],
                                   _t=pr.AUX[:, cc * 128:(cc + 1) * 128],
                                   _r=pr.Y[:, cc * 1024 + th * 512: cc * 1024 + th * 512 + 512],
                                   _s=(cc == 0), _e=(cc == NCH - 1):
                            e.matmul(_p, _t, _r, start=_s, stop=_e),
                            waits=bw, inc=ctx.es["tensor"])
    ps7_free = []
    for tt in range(8):
        ctr = ctx.emit("tensor",
                       lambda e, _i=pr.ROWS[0:8, tt * 128:(tt + 1) * 128]:
                       e.transpose(pr.PS[7][:, 0:8], _i, pr.ident_f[0:8, 0:8]),
                       waits=[(ctx.es["vector"], lg_ev)] + ps7_free,
                       inc=ctx.es["tensor"])
        lg = pr.STAT[:, tt * 8: tt * 8 + 8]
        cv0 = ctx.emit("vector", lambda e, _d=lg: e.tensor_copy(_d, pr.PS[7][:, 0:8]),
                       waits=[(ctx.es["tensor"], ctr)], inc=ctx.es["vector"])
        nm1 = pr.STAT[:, 64 + tt * 2: 64 + tt * 2 + 1]
        nm2 = pr.STAT[:, 64 + tt * 2 + 1: 64 + tt * 2 + 2]
        g1 = pr.SCR[0][:, 0:8]
        g2 = pr.SCR[1][:, 0:8]
        g3 = pr.SCR[2][:, 0:8]
        c = ctx.emit("vector", lambda e, _d=nm1, _l=lg:
                     e.tensor_reduce(_d, _l, AX.X, ALU.max, negate=True),
                     waits=[(ctx.es["vector"], cv0)], inc=ctx.es["vector"])
        c = ctx.emit("vector", lambda e, _d=g1, _l=lg, _n=nm1:
                     e.tensor_scalar(_d, _l, _n, None, ALU.add),
                     waits=[(ctx.es["vector"], c)], inc=ctx.es["vector"])
        c = ctx.emit("vector", lambda e, _d=g2, _g=g1:
                     e.tensor_scalar(_d, _g, 0.0, NEG, ALU.is_ge, op1=ALU.mult),
                     waits=[(ctx.es["vector"], c)], inc=ctx.es["vector"])
        c = ctx.emit("vector", lambda e, _d=g3, _l=lg, _g=g2:
                     e.tensor_tensor(_d, _l, _g, ALU.add),
                     waits=[(ctx.es["vector"], c)], inc=ctx.es["vector"])
        c = ctx.emit("vector", lambda e, _d=nm2, _g=g3:
                     e.tensor_reduce(_d, _g, AX.X, ALU.max, negate=True),
                     waits=[(ctx.es["vector"], c)], inc=ctx.es["vector"])
        cp8 = ctx.emit("scalar", lambda e, _d=g3, _g=g1: e.activation(_d, _g, AF.Exp),
                       waits=[(ctx.es["vector"], c)], inc=ctx.es["scalar"])
        c = ctx.emit("vector", lambda e, _d=g1, _l=lg, _n=nm2:
                     e.tensor_scalar(_d, _l, _n, None, ALU.add),
                     waits=[(ctx.es["vector"], c), (ctx.es["scalar"], cp8)],
                     inc=ctx.es["vector"])
        c = ctx.emit("vector", lambda e, _d=g1:
                     e.tensor_scalar(_d, _d, 0.0, None, ALU.is_ge),
                     waits=[(ctx.es["vector"], c)], inc=ctx.es["vector"])
        c = ctx.emit("vector", lambda e, _d=g1, _p=g3:
                     e.tensor_tensor(_d, _p, _d, ALU.mult),
                     waits=[(ctx.es["vector"], c)], inc=ctx.es["vector"])
        c = ctx.emit("vector", lambda e, _d=nm1, _g=g1:
                     e.tensor_reduce(_d, _g, AX.X, ALU.add),
                     waits=[(ctx.es["vector"], c)], inc=ctx.es["vector"])
        c = ctx.emit("vector", lambda e, _d=nm2, _n=nm1: e.reciprocal(_d, _n),
                     waits=[(ctx.es["vector"], c)], inc=ctx.es["vector"])
        cwf = ctx.emit("vector", lambda e, _d=g2, _g=g1, _n=nm2:
                       e.tensor_scalar(_d, _g, _n, 2.0, ALU.mult, op1=ALU.mult),
                       waits=[(ctx.es["vector"], c)], inc=ctx.es["vector"])
        ctr2 = ctx.emit("tensor",
                        lambda e, _i=g2: e.transpose(pr.PS[7][0:8, 0:128], _i,
                                                     pr.ident_f[:, :]),
                        waits=[(ctx.es["vector"], cwf)], inc=ctx.es["tensor"])
        cwe = ctx.emit("vector",
                       lambda e, _d=pr.KSLB[0:8, 1024 + tt * 128: 1024 + (tt + 1) * 128]:
                       e.tensor_copy(_d, pr.PS[7][0:8, 0:128]),
                       waits=[(ctx.es["tensor"], ctr2)], inc=ctx.es["vector"])
        ps7_free = [(ctx.es["vector"], cwe)]
    wt_done = ctx.es["vector"].n
    prev = 0
    for th in range(2):
        cm = ctx.emit("tensor",
                      lambda e, _r=pr.KSLB[0:8, 1024 + th * 512: 1024 + (th + 1) * 512]:
                      e.matmul(pr.PS[5][:, :], pr.expE[0:8, :], _r, start=True, stop=True),
                      waits=[(ctx.es["vector"], wt_done if th == 0 else prev)] +
                            (pr.bank_free[5] if th == 0 else []),
                      inc=ctx.es["tensor"])
        cev = ctx.emit("vector",
                       lambda e, _d=pr.PTB[th][:, :]: e.tensor_copy(_d, pr.PS[5][:, :]),
                       waits=[(ctx.es["tensor"], cm)], inc=ctx.es["vector"])
        prev = cev
        cg = ctx.emit("vector",
                      lambda e, _d=pr.KSLB[:, th * 512:(th + 1) * 512],
                             _h=pr.PS[3 + th][:, :], _w=pr.PTB[th][:, :]:
                      e.tensor_tensor(_d, _h, _w, ALU.mult),
                      waits=[(ctx.es["tensor"], h_mm), (ctx.es["vector"], cev)],
                      inc=ctx.es["vector"])
        pr.bank_free[3 + th] = [(ctx.es["vector"], cg)]
        pr.bank_free[5] = [(ctx.es["vector"], cev)]
    ctx.barrier()
    if DEBUG_TAP == "f1_gates":
        dbg1 = pr.nc.dram_tensor("dbg_stat", [128, 160], F32, kind="ExternalOutput")
        dbg2 = pr.nc.dram_tensor("dbg_kslb", [128, 2048], BF16, kind="ExternalOutput")
        dbg3 = pr.nc.dram_tensor("dbg_rows", [8, 1024], F32, kind="ExternalOutput")
        dma(ctx, "sync", dbg1.ap(), pr.STAT[:, :], slot_sem=pr.s_store)
        dma(ctx, "sync", dbg2.ap(), pr.KSLB[:, 0:2048], slot_sem=pr.s_store)
        dma(ctx, "sync", dbg3.ap(), pr.ROWS[0:8, :], slot_sem=pr.s_store)
        ctx.barrier()

    def extra(oc, th, psum, cnt):
        return ctx.emit("tensor",
                        lambda e, _p=psum[:, :],
                               _t=pr.AUX[:, 4096 + oc * 128: 4096 + oc * 128 + 128],
                               _r=pr.KSLB[:, th * 512:(th + 1) * 512]:
                        e.matmul(_p, _t, _r, start=False, stop=True),
                        inc=ctx.es["tensor"])
    stream_proj(pr, "f1_WbT", (128, NCH, NCH, 128), NCH, NCH, y_rhs(pr),
                delta_writer(pr, pr.BIG, bias=pr.biasC["f1_bias"]),
                extra_accum=extra)
    ctx.barrier()



def k_writer(pr, kv_loc, rot):
    ctx = pr.ctx
    st = {"ktmp_free": {0: [], 1: []}, "sq_free": {0: [], 1: []},
          "krot_free": [], "scr_free": []}

    def wr(oc, th, psum, dep):
        ktmp = pr.AUX[:, th * 512: th * 512 + 512]
        sq = pr.AUX[:, 1024 + th * 512: 1024 + th * 512 + 512]
        c1 = ctx.emit("scalar", lambda e, _d=ktmp, _p=psum[:, :]: e.copy(_d, _p),
                      waits=[(ctx.es["tensor"], dep)] + st["ktmp_free"][th],
                      inc=ctx.es["scalar"])
        c2 = ctx.emit("scalar", lambda e, _d=sq, _p=psum[:, :]:
                      e.activation(_d, _p, AF.Square),
                      waits=st["sq_free"][th], inc=ctx.es["scalar"])
        bank = 6 + th
        cm = ctx.emit("tensor",
                      lambda e, _p=pr.PS[bank][0:1, :], _r=sq:
                      e.matmul(_p, pr.onescol_b[:, 0:1], _r, start=True, stop=True),
                      waits=[(ctx.es["scalar"], c2)] + pr.bank_free[bank],
                      inc=ctx.es["tensor"])
        st["sq_free"][th] = [(ctx.es["tensor"], cm)]
        cr = ctx.emit("vector",
                      lambda e, _d=pr.KNM[0:1, oc * 2 + th: oc * 2 + th + 1],
                             _p=pr.PS[bank][0:1, :]:
                      e.tensor_reduce(_d, _p, AX.X, ALU.max),
                      waits=[(ctx.es["tensor"], cm)], inc=ctx.es["vector"])
        pr.bank_free[bank] = [(ctx.es["vector"], cr)]
        if th == 0:
            st["c1a"] = c1
            return (ctx.es["scalar"], c1)
        if not rot:
            cs = dma(ctx, "sync", kv_loc.ap()[oc * 128:(oc + 1) * 128, :],
                     pr.AUX[:, 0:1024], slot_sem=pr.s_store,
                     waits=[(ctx.es["scalar"], max(c1, st["c1a"]))])
            st["ktmp_free"][0] = [(pr.s_store, cs)]
            st["ktmp_free"][1] = [(pr.s_store, cs)]
            return (ctx.es["scalar"], c1)
        lastv = None
        kt_fr = {0: [], 1: []}
        for b in range(2):
            cmm = ctx.emit("tensor",
                           lambda e, _r=pr.AUX[:, b * 512: b * 512 + 512]:
                           e.matmul(pr.PS[5][:, :], pr.rotPT[:, :], _r,
                                    start=True, stop=True),
                           waits=[(ctx.es["scalar"], max(c1, st["c1a"]))] +
                                 pr.bank_free[5],
                           inc=ctx.es["tensor"])
            cg = ctx.emit("gpsimd",
                          lambda e, _a=pr.AUX[:, b * 512: b * 512 + 512]:
                          e.tensor_tensor(pr.SCR[0][:, :], _a, pr.cosT[:, :], ALU.mult),
                          waits=st["scr_free"] + [(ctx.es["scalar"], max(c1, st["c1a"]))],
                          inc=ctx.es["gpsimd"])
            cv2 = ctx.emit("vector",
                           lambda e: e.tensor_tensor(pr.SCR[1][:, :], pr.PS[5][:, :],
                                                     pr.sinT[:, :], ALU.mult),
                           waits=[(ctx.es["tensor"], cmm)] + st["scr_free"],
                           inc=ctx.es["vector"])
            pr.bank_free[5] = [(ctx.es["vector"], cv2)]
            lastv = ctx.emit("vector",
                             lambda e, _d=pr.VTB[:, b * 512: b * 512 + 512]:
                             e.tensor_tensor(_d, pr.SCR[0][:, :], pr.SCR[1][:, :],
                                             ALU.add),
                             waits=[(ctx.es["gpsimd"], cg), (ctx.es["vector"], cv2)] +
                                   st["krot_free"],
                             inc=ctx.es["vector"])
            st["scr_free"] = [(ctx.es["vector"], lastv)]
            kt_fr[b] = [(ctx.es["tensor"], cmm), (ctx.es["gpsimd"], cg)]
        cs = dma(ctx, "sync", kv_loc.ap()[oc * 128:(oc + 1) * 128, :],
                 pr.VTB[:, 0:1024], slot_sem=pr.s_store,
                 waits=[(ctx.es["vector"], lastv)])
        st["krot_free"] = [(pr.s_store, cs)]
        st["ktmp_free"][0] = kt_fr[0]
        st["ktmp_free"][1] = kt_fr[1]
        return (ctx.es["vector"], lastv)
    return wr


def v_writer(pr, kv_loc, row_base):
    ctx = pr.ctx
    st = {"aux_free": []}

    def wr(oc, th, psum, dep):
        vst = pr.AUX[:, 3072 + th * 512: 3072 + th * 512 + 512]
        c1 = ctx.emit("scalar", lambda e, _d=vst, _p=psum[:, :]: e.copy(_d, _p),
                      waits=[(ctx.es["tensor"], dep)] + st["aux_free"],
                      inc=ctx.es["scalar"])
        if th == 0:
            st["c1a"] = c1
        else:
            cs = dma(ctx, "sync",
                     kv_loc.ap()[row_base + oc * 128: row_base + (oc + 1) * 128, :],
                     pr.AUX[:, 3072:4096], slot_sem=pr.s_store,
                     waits=[(ctx.es["scalar"], max(c1, st["c1a"]))])
            st["aux_free"] = [(pr.s_store, cs)]
        return (ctx.es["scalar"], c1)
    return wr



def attn_units(pr, kv_all, n_kt, k_rows_fn, v_rows_fn, q_cols_fn, mtab_fn,
               knm_idx_fn, out_cols_fn, scale, units, group_fn=None,
               k_ready=(), v_ready=(), post_bias=None):
    ctx = pr.ctx
    sem_k = ctx.new_sem("atk")
    sem_v = ctx.new_sem("atv")
    n_sl = (n_kt + 3) // 4
    LA = 4 if mtab_fn is None else 3
    SB = [0, 1, 6, 7]
    n_pt = 6 if mtab_fn is None else 4
    ctx.emit("vector", lambda e: e.tensor_scalar(pr.KNM[0:1, 64:128],
                                                 pr.KNM[0:1, 0:64], 0.5, None, ALU.mult),
             waits=[(ctx.es["vector"], ctx.vN())] + list(k_ready),
             inc=ctx.es["vector"])
    bias_done = []
    ptb0_free = []
    for u_i, u in enumerate(units):
        qc = q_cols_fn(u)
        csq = ctx.emit("scalar",
                       lambda e, _q=qc: e.activation(pr.PTB[0][:, :], _q, AF.Square),
                       waits=ptb0_free, inc=ctx.es["scalar"])
        cqn = ctx.emit("tensor",
                       lambda e: e.matmul(pr.PS[4][0:1, :], pr.onescol_b[:, 0:1],
                                          pr.PTB[0][:, :], start=True, stop=True),
                       waits=[(ctx.es["scalar"], csq)] + pr.bank_free[4],
                       inc=ctx.es["tensor"])
        ptb0_free = [(ctx.es["tensor"], cqn)]
        c = ctx.emit("vector",
                     lambda e: e.tensor_reduce(pr.ROWS[0:1, 512:513], pr.PS[4][0:1, :],
                                               AX.X, ALU.max),
                     waits=[(ctx.es["tensor"], cqn), (ctx.es["vector"], ctx.vN())],
                     inc=ctx.es["vector"])
        ki = knm_idx_fn(u)
        c = ctx.emit("vector",
                     lambda e: e.tensor_scalar(pr.ROWS[0:1, 512:513],
                                               pr.ROWS[0:1, 512:513], 0.5, None, ALU.mult),
                     waits=[(ctx.es["vector"], c)], inc=ctx.es["vector"])
        c = ctx.emit("vector",
                     lambda e, _k=pr.KNM[0:1, 64 + ki: 64 + ki + 1]:
                     e.tensor_scalar(pr.ROWS[0:1, 512:513], pr.ROWS[0:1, 512:513],
                                     _k, None, ALU.add),
                     waits=[(ctx.es["vector"], c)], inc=ctx.es["vector"])
        cbu = ctx.emit("vector",
                       lambda e, _s=-scale:
                       e.tensor_scalar(pr.ROWS[0:1, 512:513], pr.ROWS[0:1, 512:513],
                                       _s, None, ALU.mult),
                       waits=[(ctx.es["vector"], c)], inc=ctx.es["vector"])
        cbm = ctx.emit("tensor",
                       lambda e: e.matmul(pr.PS[4][:, 0:1], pr.ones_f[0:1, :],
                                          pr.ROWS[0:1, 512:513], start=True, stop=True),
                       waits=[(ctx.es["vector"], cbu)], inc=ctx.es["tensor"])
        cbe = ctx.emit("vector",
                       lambda e, _d=pr.STAT[:, u_i: u_i + 1]:
                       e.tensor_copy(_d, pr.PS[4][:, 0:1]),
                       waits=[(ctx.es["tensor"], cbm)], inc=ctx.es["vector"])
        pr.bank_free[4] = [(ctx.es["vector"], cbe)]
        bias_done.append((ctx.es["vector"], cbe))
    if post_bias is not None:
        post_bias()
    scr_free = {i: [] for i in range(3)}
    ptb_free = {i: [] for i in range(n_pt)}
    quad_free = {0: list(ptb0_free), 1: []}
    ks_mm = {}
    vt_pv = {}
    ck_all = cv_all = 0
    nq = n_kt // 4
    for gi, u in enumerate(units):
        new_group = (group_fn is None or gi == 0 or
                     group_fn(u) != group_fn(units[gi - 1]))
        if new_group:
            for sl in range(n_sl):
                def mk_k(e, _u=u, _sl=sl):
                    r, c0 = k_rows_fn(_u, _sl)
                    kt_ = kv_all[0] if isinstance(kv_all, tuple) else kv_all
                    return e.dma_start(out=pr.KSLB[:, _sl * 512:(_sl + 1) * 512],
                                       in_=kt_.ap()[bass.ds(r, 128), c0:c0 + 512])
                ctx.emit("sync", mk_k,
                         waits=[(ctx.es["tensor"], ks_mm.get(sl, 0))] + list(k_ready),
                         inc=sem_k, inc_by=16)
            for j in range(n_kt):
                def mk_v(e, _u=u, _j=j):
                    r, c0 = v_rows_fn(_u, _j)
                    vt_ = kv_all[1] if isinstance(kv_all, tuple) else kv_all
                    return e.dma_start(out=pr.VTB[:, _j * 128:(_j + 1) * 128],
                                       in_=vt_.ap()[bass.ds(r, 128), c0:c0 + 128],
                                       transpose=True)
                ctx.emit("sync", mk_v,
                         waits=[(ctx.es["tensor"], vt_pv.get(j, 0))] + list(v_ready),
                         inc=sem_v, inc_by=16)
            ck_all = sem_k.n
            cv_all = sem_v.n
        qc = q_cols_fn(u)
        bcol = pr.STAT[:, gi: gi + 1]
        bwait = bias_done[gi]
        otb = 2 if gi % 2 == 0 else 5
        exp_of = {}
        ca_of = {}
        st_ca = {}
        pv_last = None
        cs_last = None

        def pt_ap(j):
            sl_ = j % n_pt
            return pr.AUX[:, sl_ * 512: sl_ * 512 + 512]

        def emit_pv(j):
            nonlocal pv_last, cs_last
            pv_last = ctx.emit("tensor",
                               lambda e, _t=pr.VTB[:, j * 128:(j + 1) * 128],
                                      _r=pt_ap(j), _s=(j == 0), _e=(j == n_kt - 1),
                                      _o=pr.PS[otb][:, :]:
                               e.matmul(_o, _t, _r, start=_s, stop=_e),
                               waits=[exp_of[j]] +
                                     ([(sem_v, cv_all)] + pr.bank_free[otb]
                                      if j == 0 else []),
                               inc=ctx.es["tensor"])
            cs_last = ctx.emit("tensor",
                               lambda e, _r=pt_ap(j), _s=(j == 0), _e=(j == n_kt - 1):
                               e.matmul(pr.PS[3][0:1, :], pr.onescol_b[:, 0:1], _r,
                                        start=_s, stop=_e),
                               waits=(pr.bank_free[3] if j == 0 else []),
                               inc=ctx.es["tensor"])
            ptb_free[j % n_pt] = [(ctx.es["tensor"], cs_last)]
            vt_pv[j] = pv_last

        for j in range(n_kt):
            bk = SB[j % 4]
            cmm = ctx.emit("tensor",
                           lambda e, _pp=pr.PS[bk][:, :],
                                  _t=pr.KSLB[:, (j // 4) * 512 + (j % 4) * 128:
                                             (j // 4) * 512 + (j % 4) * 128 + 128],
                                  _q=qc:
                           e.matmul(_pp, _t, _q, start=True, stop=True),
                           waits=([(sem_k, ck_all)] if j == 0 else []) + pr.bank_free[bk],
                           inc=ctx.es["tensor"])
            ks_mm[j // 4] = cmm
            mcols = mtab_fn(u, j) if mtab_fn is not None else None
            if mcols is not None:
                sslot = j % 3
                cadd = ctx.emit("vector",
                                lambda e, _d=pr.SCR[sslot][:, :], _p=pr.PS[bk][:, :],
                                       _m=mcols:
                                e.tensor_tensor(_d, _p, _m, ALU.add),
                                waits=[(ctx.es["tensor"], cmm)] + scr_free[sslot],
                                inc=ctx.es["vector"])
                pr.bank_free[bk] = [(ctx.es["vector"], cadd)]
                cexp = ctx.emit("scalar",
                                lambda e, _d=pt_ap(j), _s=pr.SCR[sslot][:, :], _b=bcol:
                                e.activation(_d, _s, AF.Exp, bias=_b, scale=scale),
                                waits=[(ctx.es["vector"], cadd)] +
                                      ([bwait] if j == 0 else []) + ptb_free[j % n_pt],
                                inc=ctx.es["scalar"])
                scr_free[sslot] = [(ctx.es["scalar"], cexp)]
            else:
                cexp = ctx.emit("scalar",
                                lambda e, _d=pt_ap(j), _p=pr.PS[bk][:, :], _b=bcol:
                                e.activation(_d, _p, AF.Exp, bias=_b, scale=scale),
                                waits=[(ctx.es["tensor"], cmm)] +
                                      ([bwait] if j == 0 else []) + ptb_free[j % n_pt],
                                inc=ctx.es["scalar"])
                pr.bank_free[bk] = [(ctx.es["scalar"], cexp)]
            exp_of[j] = (ctx.es["scalar"], cexp)
            if j >= LA:
                emit_pv(j - LA)
        for j in range(max(0, n_kt - LA), n_kt):
            emit_pv(j)
        ce1 = ctx.emit("vector",
                       lambda e: e.tensor_copy(pr.ROWS[0:1, 0:512], pr.PS[3][0:1, :]),
                       waits=[(ctx.es["tensor"], cs_last)], inc=ctx.es["vector"])
        cb2 = ctx.emit("tensor",
                       lambda e: e.matmul(pr.PS[3][:, :], pr.ones_f[0:1, :],
                                          pr.ROWS[0:1, 0:512], start=True, stop=True),
                       waits=[(ctx.es["vector"], ce1)], inc=ctx.es["tensor"])
        ce2 = ctx.emit("vector",
                       lambda e: e.tensor_copy(pr.SCR[0][:, :], pr.PS[3][:, :]),
                       waits=[(ctx.es["tensor"], cb2)] + scr_free[0],
                       inc=ctx.es["vector"])
        pr.bank_free[3] = [(ctx.es["vector"], ce2)]
        cr1 = ctx.emit("vector",
                       lambda e: e.reciprocal(pr.SCR[1][:, :], pr.SCR[0][:, :]),
                       waits=[(ctx.es["vector"], ce2)] + scr_free[1],
                       inc=ctx.es["vector"])
        cno = ctx.emit("vector",
                       lambda e, _d=out_cols_fn(u), _o=pr.PS[otb][:, :]:
                       e.tensor_tensor(_d, _o, pr.SCR[1][:, :], ALU.mult),
                       waits=[(ctx.es["vector"], cr1)], inc=ctx.es["vector"])
        pr.bank_free[otb] = [(ctx.es["vector"], cno)]
        scr_free[0] = [(ctx.es["vector"], cr1)]
        scr_free[1] = [(ctx.es["vector"], cno)]



def ph_b0(pr):
    ctx = pr.ctx
    kv_loc = pr.dint("b0_kvloc", [8192, 1024], BF16)
    kv_allK = pr.dint("b0_kvallk", [32768, 1024], BF16)
    kv_allV = pr.dint("b0_kvallv", [32768, 1024], BF16)
    kv_win = pr.dint("b0_kvwin", [24576, 1024], BF16)
    knm_in = pr.dint("b0_knin", [1, 64], F32)
    knm_out = pr.dint("b0_knout", [1, 64], F32)
    ctx.need_pid.add("gpsimd")
    vbf = precast_w(pr, "b0_WvT", (128, NCH, NCH, 128))
    precast_w(pr, "a1_WvT", (128, 8, NCH, 128))

    stream_proj(pr, "b0_WqT", (128, NCH, NCH, 128), NCH, NCH, y_rhs(pr),
                delta_writer(pr, pr.BIG))
    ctx.barrier()
    scr_free = []
    for ch in range(NCH):
        for b in range(2):
            bk = 6 + (2 * ch + b) % 2
            qcols = pr.BIG[:, ch * 1024 + b * 512: ch * 1024 + b * 512 + 512]
            cmm = ctx.emit("tensor",
                           lambda e, _p=pr.PS[bk][:, :], _q=qcols:
                           e.matmul(_p, pr.rotPT[:, :], _q, start=True, stop=True),
                           waits=pr.bank_free[bk], inc=ctx.es["tensor"])
            cg = ctx.emit("gpsimd",
                          lambda e, _q=qcols:
                          e.tensor_tensor(pr.SCR[0][:, :], _q, pr.cosT[:, :], ALU.mult),
                          waits=scr_free, inc=ctx.es["gpsimd"])
            cv2 = ctx.emit("vector",
                           lambda e, _p=pr.PS[bk][:, :]:
                           e.tensor_tensor(pr.SCR[1][:, :], _p, pr.sinT[:, :], ALU.mult),
                           waits=[(ctx.es["tensor"], cmm)] + scr_free,
                           inc=ctx.es["vector"])
            pr.bank_free[bk] = [(ctx.es["vector"], cv2)]
            cvw = ctx.emit("vector",
                           lambda e, _d=qcols:
                           e.tensor_tensor(_d, pr.SCR[0][:, :], pr.SCR[1][:, :], ALU.add),
                           waits=[(ctx.es["gpsimd"], cg)], inc=ctx.es["vector"])
            scr_free = [(ctx.es["vector"], cvw)]
    ctx.barrier()
    for j in range(12):
        for i in range(4):
            d = i + 8 - j
            cols = pr.AUX[:, 2048 + j * 512 + i * 128: 2048 + j * 512 + i * 128 + 128]
            if d == 0:
                ctx.emit("gpsimd", lambda e, _d=cols: e.tensor_copy(_d, pr.mask_diag[:, :]),
                         inc=ctx.es["gpsimd"])
            elif d == 8:
                ctx.emit("gpsimd", lambda e, _d=cols: e.tensor_copy(_d, pr.mask_tail[:, :]),
                         inc=ctx.es["gpsimd"])
            elif 1 <= d <= 7:
                ctx.emit("gpsimd", lambda e, _d=cols: e.memset(_d, 0.0),
                         inc=ctx.es["gpsimd"])
            else:
                ctx.emit("gpsimd", lambda e, _d=cols: e.memset(_d, NEG),
                         inc=ctx.es["gpsimd"])
        ctx.emit("gpsimd",
                 lambda e, _d=pr.AUX[:, 2048 + j * 512: 2048 + j * 512 + 512],
                        _v=pr.vcolB[:, j: j + 1]:
                 e.tensor_scalar(_d, _d, _v, None, ALU.add),
                 inc=ctx.es["gpsimd"])
    stream_proj(pr, "b0_WkT", (128, NCH, NCH, 128), NCH, NCH, y_rhs(pr),
                k_writer(pr, kv_loc, rot=True), bank_n=4)
    ctx.barrier()
    dma(ctx, "sync", knm_in.ap(), pr.KNM[0:1, 0:64], slot_sem=pr.s_store)
    sc1 = ctx.new_sem("b0_agk")
    ctx.emit("gpsimd",
             lambda e: e.collective_compute(
                 "AllGather", ALU.bypass, replica_groups=[list(range(N_CORES))],
                 ins=[kv_loc.ap()[0:4096, :]], outs=[kv_allK.ap()]),
             waits=[(pr.s_store, pr.s_store.n)], inc=sc1)
    sc2 = ctx.new_sem("b0_knm")
    ctx.emit("gpsimd",
             lambda e: e.collective_compute(
                 "AllReduce", ALU.max, replica_groups=[list(range(N_CORES))],
                 ins=[knm_in.ap()], outs=[knm_out.ap()]),
             inc=sc2)
    sld = ctx.new_sem("b0_knl")
    dma(ctx, "gpsimd", pr.KNM[0:1, 0:64], knm_out.ap(), slot_sem=sld, waits=[(sc2, 1)])
    kv8k = kv_allK.ap().rearrange("(c r) w -> c r w", c=8)
    swin = ctx.new_sem("b0_wink")

    for dlt in range(3):
        def mk_win(e, _d=dlt):
            src = kv8k[bass.ds((ctx.pid["gpsimd"] + (_d + 6)) % 8, 1), :, :]
            return e.dma_start(out=kv_win.ap()[_d * 4096:(_d + 1) * 4096, :],
                               in_=src.rearrange("o r w -> (o r) w"))
        ctx.emit("gpsimd", mk_win, waits=[(sc1, 1)] if dlt == 0 else [],
                 inc=swin, inc_by=16)
    stream_proj(pr, "b0_WvT", (128, NCH, NCH, 128), NCH, NCH, y_rhs(pr),
                v_writer(pr, kv_loc, 4096), bank_n=4, src_bf=vbf)
    ctx.barrier()
    scv = ctx.new_sem("b0_agv")
    ctx.emit("gpsimd",
             lambda e: e.collective_compute(
                 "AllGather", ALU.bypass, replica_groups=[list(range(N_CORES))],
                 ins=[kv_loc.ap()[4096:8192, :]], outs=[kv_allV.ap()]),
             waits=[(pr.s_store, pr.s_store.n)], inc=scv)
    kv8v = kv_allV.ap().rearrange("(c r) w -> c r w", c=8)
    for dlt in range(3):
        def mk_winv(e, _d=dlt):
            src = kv8v[bass.ds((ctx.pid["gpsimd"] + (_d + 6)) % 8, 1), :, :]
            return e.dma_start(out=kv_win.ap()[12288 + _d * 4096: 12288 + (_d + 1) * 4096, :],
                               in_=src.rearrange("o r w -> (o r) w"))
        ctx.emit("gpsimd", mk_winv, waits=[(scv, 1)] if dlt == 0 else [],
                 inc=swin, inc_by=16)
    win_k_ready = [(swin, 48), (sld, 16)]
    win_v_ready = [(swin, 96)]

    units = [(b, h) for b in range(2) for h in range(NCH)]

    def k_rows(u, sl):
        return (sl * 4096 + u[1] * 128, u[0] * 512)

    def v_rows(u, j):
        return (12288 + (j // 4) * 4096 + u[1] * 128,
                u[0] * 512 + (j % 4) * 128)

    attn_units(pr, kv_win, 12, k_rows, v_rows,
               q_cols_fn=lambda u: pr.BIG[:, u[1] * 1024 + u[0] * 512:
                                          u[1] * 1024 + u[0] * 512 + 512],
               mtab_fn=lambda u, j: pr.AUX[:, 2048 + j * 512: 2048 + j * 512 + 512],
               knm_idx_fn=lambda u: u[1] * 2 + u[0],
               out_cols_fn=lambda u: pr.BIG[:, u[1] * 1024 + u[0] * 512:
                                            u[1] * 1024 + u[0] * 512 + 512],
               scale=RS128, units=units,
               k_ready=win_k_ready, v_ready=win_v_ready)
    ctx.barrier()
    stream_proj(pr, "b0_WoT", (128, NCH, NCH, 128), NCH, NCH, big_rhs(pr),
                delta_writer(pr, pr.Y))
    ctx.barrier()



def ph_a1(pr):
    ctx = pr.ctx
    vbf = precast_w(pr, "a1_WvT", (128, 8, NCH, 128))
    kv_loc = pr.dint("a1_kvloc", [2048, 1024], BF16)
    kv_allK = pr.dint("a1_kvallk", [8192, 1024], BF16)
    kv_allV = pr.dint("a1_kvallv", [8192, 1024], BF16)
    knm_in = pr.dint("a1_knin", [1, 64], F32)
    knm_out = pr.dint("a1_knout", [1, 64], F32)

    stream_proj(pr, "a1_WqT", (128, NCH, NCH, 128), NCH, NCH, y_rhs(pr),
                delta_writer(pr, pr.BIG))
    ctx.barrier()
    stream_proj(pr, "a1_WkT", (128, 8, NCH, 128), 8, NCH, y_rhs(pr),
                k_writer(pr, kv_loc, rot=False), bank_n=4)
    ctx.barrier()
    dma(ctx, "sync", knm_in.ap(), pr.KNM[0:1, 0:64], slot_sem=pr.s_store)
    sc1 = ctx.new_sem("a1_agk")
    ctx.emit("gpsimd",
             lambda e: e.collective_compute(
                 "AllGather", ALU.bypass, replica_groups=[list(range(N_CORES))],
                 ins=[kv_loc.ap()[0:1024, :]], outs=[kv_allK.ap()]),
             waits=[(pr.s_store, pr.s_store.n)], inc=sc1)
    sc2 = ctx.new_sem("a1_knm")
    ctx.emit("gpsimd",
             lambda e: e.collective_compute(
                 "AllReduce", ALU.max, replica_groups=[list(range(N_CORES))],
                 ins=[knm_in.ap()], outs=[knm_out.ap()]),
             inc=sc2)
    sld = ctx.new_sem("a1_knl")
    dma(ctx, "gpsimd", pr.KNM[0:1, 0:64], knm_out.ap(), slot_sem=sld, waits=[(sc2, 1)])
    stream_proj(pr, "a1_WvT", (128, 8, NCH, 128), 8, NCH, y_rhs(pr),
                v_writer(pr, kv_loc, 1024), bank_n=4, src_bf=vbf)
    ctx.barrier()
    scv = ctx.new_sem("a1_agv")
    ctx.emit("gpsimd",
             lambda e: e.collective_compute(
                 "AllGather", ALU.bypass, replica_groups=[list(range(N_CORES))],
                 ins=[kv_loc.ap()[1024:2048, :]], outs=[kv_allV.ap()]),
             waits=[(pr.s_store, pr.s_store.n)], inc=scv)

    units = [(b, h) for b in range(2) for h in range(NCH)]
    attn_units(pr, (kv_allK, kv_allV), 32,
               k_rows_fn=lambda u, sl: (sl * 1024 + (u[1] // 4) * 128, u[0] * 512),
               v_rows_fn=lambda u, j: ((j // 4) * 1024 + (u[1] // 4) * 128,
                                       u[0] * 512 + (j % 4) * 128),
               q_cols_fn=lambda u: pr.BIG[:, u[1] * 1024 + u[0] * 512:
                                          u[1] * 1024 + u[0] * 512 + 512],
               mtab_fn=None,
               knm_idx_fn=lambda u: (u[1] // 4) * 2 + u[0],
               out_cols_fn=lambda u: pr.BIG[:, u[1] * 1024 + u[0] * 512:
                                            u[1] * 1024 + u[0] * 512 + 512],
               scale=RS128, units=units, group_fn=lambda u: (u[0], u[1] // 4),
               k_ready=[(sc1, 1), (sld, 16)], v_ready=[(scv, 1)])
    ctx.barrier()
    if DEBUG_TAP == "a1_attn":
        dbg = pr.nc.dram_tensor("dbg_big", [128, NCH * 1024], BF16,
                                kind="ExternalOutput")
        dma(ctx, "sync", dbg.ap(), pr.BIG[:, :], slot_sem=pr.s_store)
        ctx.barrier()
    if DEBUG_TAP == "a1_q":
        dbg = pr.nc.dram_tensor("dbg_big", [128, NCH * 1024], BF16,
                                kind="ExternalOutput")
        dma(ctx, "sync", dbg.ap(), pr.BIG[:, :], slot_sem=pr.s_store)
        ctx.barrier()
    stream_proj(pr, "a1_WoT", (128, NCH, NCH, 128), NCH, NCH, big_rhs(pr),
                delta_writer(pr, pr.Y))
    ctx.barrier()



def ph_b1(pr):
    ctx = pr.ctx
    hl_loc = pr.dint("b1_hloc", [128, 128], BF16)
    hl_all = pr.dint("b1_hall", [1024, 128], BF16)
    ctx.need_pid.add("sync")
    ecols = [(1,), (512,), (515,), (1026,)]
    cvh = None
    for e_i, (off,) in enumerate(ecols):
        src = pr.Y[:, off: off + (NCH - 1) * YST + 1: YST]
        cvh = ctx.emit("vector",
                       lambda e, _d=pr.VTB[:, 128 + e_i * 32: 128 + (e_i + 1) * 32], _s=src:
                       e.tensor_copy(_d, _s), inc=ctx.es["vector"])
    dma(ctx, "sync", hl_loc.ap(), pr.VTB[:, 128:256], slot_sem=pr.s_store,
        waits=[(ctx.es["vector"], cvh)])
    sc1 = ctx.new_sem("b1_ag")
    ctx.emit("gpsimd",
             lambda e: e.collective_compute(
                 "AllGather", ALU.bypass, replica_groups=[list(range(N_CORES))],
                 ins=[hl_loc.ap()], outs=[hl_all.ap()]),
             waits=[(pr.s_store, pr.s_store.n)], inc=sc1)
    sem_h = ctx.new_sem("b1_hl")
    ch1 = ctx.emit("sync",
                   lambda e: e.dma_start(
                       out=pr.VTB[:, 128:256],
                       in_=hl_all.ap()[bass.ds((ctx.pid["sync"] + 7) % 8 * 128, 128), :]),
                   waits=[(sc1, 1)], inc=sem_h, inc_by=16)
    ch2 = ctx.emit("sync",
                   lambda e: e.dma_start(
                       out=pr.VTB[:, 256:384],
                       in_=hl_all.ap()[bass.ds((ctx.pid["sync"] + 1) % 8 * 128, 128), :]),
                   inc=sem_h, inc_by=16)
    cl = ctx.emit("gpsimd",
                  lambda e: e.tensor_scalar(pr.VTB[:, 128:256], pr.VTB[:, 128:256],
                                            pr.hvB[:, 0:1], None, ALU.mult),
                  waits=[(sem_h, ch1)], inc=ctx.es["gpsimd"])
    cr = ctx.emit("gpsimd",
                  lambda e: e.tensor_scalar(pr.VTB[:, 256:384], pr.VTB[:, 256:384],
                                            pr.hvB[:, 1:2], None, ALU.mult),
                  waits=[(sem_h, ch2)], inc=ctx.es["gpsimd"])
    fills = [(128, 1, 0), (128, 3, 514), (256, 0, 513), (256, 2, 1027)]
    for bo, eidx, coloff in fills:
        ctx.emit("vector",
                 lambda e, _s=pr.VTB[:, bo + eidx * 32: bo + (eidx + 1) * 32],
                        _d=pr.Y[:, coloff: coloff + (NCH - 1) * YST + 1: YST]:
                 e.tensor_copy(_d, _s),
                 waits=[(ctx.es["gpsimd"], max(cl, cr))], inc=ctx.es["vector"])
    ctx.barrier()
    conv = pr.din("b1_conv", (3, 128, 24, NCH, 128))
    for oc in range(24):
        banks = [(oc % 2) * 2, (oc % 2) * 2 + 1]
        last = None
        for k in range(3):
            sl = pr.r_wg.next()
            c = dma(ctx, "gpsimd", sl.ap[:, 0: NCH * 128],
                    conv.ap()[k][:, oc].rearrange("p c j -> p (c j)"),
                    slot_sem=sl.sem, waits=sl.free_waits)
            for cc in range(NCH):
                tile = sl.ap[:, cc * 128:(cc + 1) * 128]
                for b in range(2):
                    bw = []
                    if cc == 0 and b == 0:
                        bw = [(sl.sem, c)]
                        if k == 0:
                            bw = bw + list(pr.bank_free[banks[0]]) + \
                                list(pr.bank_free[banks[1]])
                    rhs = pr.Y[:, cc * YST + k + b * 514: cc * YST + k + b * 514 + 512]
                    last = ctx.emit(
                        "tensor",
                        lambda e, _p=pr.PS[banks[b]][:, :], _t=tile, _r=rhs,
                               _s=(k == 0 and cc == 0), _e=(k == 2 and cc == NCH - 1):
                        e.matmul(_p, _t, _r, start=_s, stop=_e),
                        waits=bw, inc=ctx.es["tensor"])
            sl.free_waits = [(ctx.es["tensor"], last)]
        for b in range(2):
            dcols = pr.BIG[:, oc * 1024 + b * 512: oc * 1024 + b * 512 + 512]
            ce = ctx.emit("vector",
                          lambda e, _d=dcols, _p=pr.PS[banks[b]][:, :],
                                 _bc=pr.biasC["b1_convb"][:, oc:oc + 1]:
                          e.tensor_scalar(_d, _p, _bc, None, ALU.add),
                          waits=[(ctx.es["tensor"], last)], inc=ctx.es["vector"])
            pr.bank_free[banks[b]] = [(ctx.es["vector"], ce)]
    ctx.barrier()
    def qkv_rhs(base):
        return lambda cc, th: pr.BIG[:, (base + cc) * 1024 + th * 512:
                                     (base + cc) * 1024 + th * 512 + 512]
    stream_proj(pr, "b1_wq", (128, 8, 8, 128), 8, 8, qkv_rhs(0),
                delta_writer(pr, pr.Y, bias=pr.biasC["b1_bq"]), bank_n=4)
    stream_proj(pr, "b1_wk", (128, 8, 8, 128), 8, 8, qkv_rhs(8),
                delta_writer(pr, pr.Y, bias=pr.biasC["b1_bk"], col_off=8192),
                bank_n=4, bank_base=4)
    ctx.barrier()
    wv = pr.din("b1_wv_rhs", (128, 8, 1024))
    bvr = pr.din("b1_bv_row", (1, 1024))
    semv = ctx.new_sem("b1_wv")
    dma(ctx, "sync", pr.ROWS[0:1, :], bvr.ap(), slot_sem=semv)
    dma(ctx, "gpsimd", pr.AUX[:, 0:8192], wv.ap().rearrange("p c j -> p (c j)"),
        slot_sem=semv)
    prev = 0
    for th in range(2):
        cb = ctx.emit("tensor",
                      lambda e, _r=pr.ROWS[0:1, th * 512:(th + 1) * 512]:
                      e.matmul(pr.PS[6][:, :], pr.ones_f[0:1, :], _r,
                               start=True, stop=True),
                      waits=(pr.bank_free[6] + [(semv, 32)] if th == 0 else
                             [(ctx.es["vector"], prev)]),
                      inc=ctx.es["tensor"])
        prev = ctx.emit("vector",
                        lambda e, _d=pr.NB[:, th * 512:(th + 1) * 512]:
                        e.tensor_copy(_d, pr.PS[6][:, :]),
                        waits=[(ctx.es["tensor"], cb)], inc=ctx.es["vector"])
    cbv_done = prev
    pr.bank_free[6] = [(ctx.es["vector"], cbv_done)]
    for tt in range(8):
        banks = [(tt % 2) * 2, (tt % 2) * 2 + 1]
        last = None
        for wc in range(8):
            tile = pr.BIG[:, (16 + wc) * 1024 + tt * 128: (16 + wc) * 1024 + tt * 128 + 128]
            for th in range(2):
                bw = []
                if wc == 0:
                    bw = list(pr.bank_free[banks[th]])
                    if th == 0 and tt == 0:
                        bw = bw + [(semv, 32)]
                last = ctx.emit(
                    "tensor",
                    lambda e, _p=pr.PS[banks[th]][:, :], _t=tile,
                           _r=pr.AUX[:, wc * 1024 + th * 512: wc * 1024 + th * 512 + 512],
                           _s=(wc == 0), _e=(wc == 7):
                    e.matmul(_p, _t, _r, start=_s, stop=_e),
                    waits=bw, inc=ctx.es["tensor"])
        for th in range(2):
            dcols = pr.Y[:, 16384 + tt * 1024 + th * 512:
                         16384 + tt * 1024 + th * 512 + 512]
            cvv = ctx.emit("vector",
                           lambda e, _d=dcols, _p=pr.PS[banks[th]][:, :],
                                  _b=pr.NB[:, th * 512:(th + 1) * 512]:
                           e.tensor_tensor(_d, _p, _b, ALU.add),
                           waits=[(ctx.es["tensor"], last),
                                  (ctx.es["vector"], cbv_done)],
                           inc=ctx.es["vector"])
            pr.bank_free[banks[th]] = [(ctx.es["vector"], cvv)]
    ctx.barrier()
    ctx = pr.ctx
    gcol = pr.STAT[:, 0:1]
    rmax = pr.ROWS[0:1, 512:513]
    rtmp = pr.ROWS[0:1, 513:514]
    cz = ctx.emit("gpsimd", lambda e: e.memset(rmax, 0.0), inc=ctx.es["gpsimd"])
    cprev = 0
    sqf = []
    for idx in range(16):
        for th in range(2):
            ysl = pr.Y[:, idx * 1024 + th * 512: idx * 1024 + th * 512 + 512]
            csq = ctx.emit("scalar",
                           lambda e, _y=ysl: e.activation(pr.PTB[0][:, :], _y,
                                                          AF.Square),
                           waits=sqf, inc=ctx.es["scalar"])
            cqn = ctx.emit("tensor",
                           lambda e: e.matmul(pr.PS[4][0:1, :], pr.onescol_b[:, 0:1],
                                              pr.PTB[0][:, :], start=True, stop=True),
                           waits=[(ctx.es["scalar"], csq)] + pr.bank_free[4],
                           inc=ctx.es["tensor"])
            sqf = [(ctx.es["tensor"], cqn)]
            cr = ctx.emit("vector",
                          lambda e: e.tensor_reduce(rtmp, pr.PS[4][0:1, :], AX.X,
                                                    ALU.max),
                          waits=[(ctx.es["tensor"], cqn), (ctx.es["vector"], cprev),
                                 (ctx.es["gpsimd"], cz)],
                          inc=ctx.es["vector"])
            pr.bank_free[4] = [(ctx.es["vector"], cr)]
            cprev = ctx.emit("vector",
                             lambda e: e.tensor_tensor(rmax, rmax, rtmp, ALU.max),
                             waits=[(ctx.es["vector"], cr)], inc=ctx.es["vector"])
    cbu = ctx.emit("vector",
                   lambda e: e.tensor_scalar(rmax, rmax, -RS64, None, ALU.mult),
                   waits=[(ctx.es["vector"], cprev)], inc=ctx.es["vector"])
    cbm = ctx.emit("tensor",
                   lambda e: e.matmul(pr.PS[4][:, 0:1], pr.ones_f[0:1, :], rmax,
                                      start=True, stop=True),
                   waits=[(ctx.es["vector"], cbu)] + pr.bank_free[4],
                   inc=ctx.es["tensor"])
    cbe = ctx.emit("vector", lambda e: e.tensor_copy(gcol, pr.PS[4][:, 0:1]),
                   waits=[(ctx.es["tensor"], cbm)], inc=ctx.es["vector"])
    pr.bank_free[4] = [(ctx.es["vector"], cbe)]
    bias_ready = (ctx.es["vector"], cbe)
    SBK = [0, 1, 6, 7]
    scr_lo = {i: [] for i in range(3)}
    scr_hi = {i: [] for i in range(3)}
    pt_free = {i: [] for i in range(8)}
    row_free = {0: [], 1: []}
    ui = 0
    for b in range(2):
        for blk in range(4):
            tti = b * 4 + blk
            for h in range(16):
                hc = h // 2
                pb = (h % 2) * 64
                base_q = hc * 1024 + b * 512 + blk * 128
                qt = pr.Y[pb:pb + 64, base_q: base_q + 128]
                kt = pr.Y[pb:pb + 64, 8192 + base_q: 8192 + base_q + 128]
                vt = pr.Y[:, 16384 + tti * 1024 + h * 64:
                          16384 + tti * 1024 + h * 64 + 64]
                sb = SBK[ui % 4]
                otb = 2 if ui % 2 == 0 else 5
                csb = 3 if ui % 2 == 0 else 4
                ptc = pr.AUX[:, (ui % 8) * 128: (ui % 8) * 128 + 128]
                ss = ui % 3
                rrow = pr.ROWS[0:1, (ui % 2) * 128: (ui % 2) * 128 + 128]
                cmm = ctx.emit("tensor",
                               lambda e, _p=pr.PS[sb][:, 0:128], _k=kt, _q=qt:
                               e.matmul(_p, _k, _q, start=True, stop=True),
                               waits=pr.bank_free[sb], inc=ctx.es["tensor"])
                cadd = ctx.emit("vector",
                                lambda e, _d=pr.SCR[ss][:, 0:128],
                                       _p=pr.PS[sb][:, 0:128]:
                                e.tensor_tensor(_d, _p, pr.mask_diag[:, :], ALU.add),
                                waits=[(ctx.es["tensor"], cmm)] + scr_lo[ss],
                                inc=ctx.es["vector"])
                pr.bank_free[sb] = [(ctx.es["vector"], cadd)]
                cexp = ctx.emit("scalar",
                                lambda e, _d=ptc, _s=pr.SCR[ss][:, 0:128]:
                                e.activation(_d, _s, AF.Exp, bias=gcol, scale=RS64),
                                waits=[(ctx.es["vector"], cadd), bias_ready] +
                                      pt_free[ui % 8],
                                inc=ctx.es["scalar"])
                scr_lo[ss] = [(ctx.es["scalar"], cexp)]
                cpv = ctx.emit("tensor",
                               lambda e, _o=pr.PS[otb][0:64, 0:128], _v=vt, _r=ptc:
                               e.matmul(_o, _v, _r, start=True, stop=True),
                               waits=[(ctx.es["scalar"], cexp)] + pr.bank_free[otb],
                               inc=ctx.es["tensor"])
                ccs = ctx.emit("tensor",
                               lambda e, _p=pr.PS[csb][0:1, 0:128], _r=ptc:
                               e.matmul(_p, pr.onescol_b[:, 0:1], _r,
                                        start=True, stop=True),
                               waits=pr.bank_free[csb], inc=ctx.es["tensor"])
                pt_free[ui % 8] = [(ctx.es["tensor"], ccs)]
                ce1 = ctx.emit("vector",
                               lambda e, _r=rrow, _p=pr.PS[csb][0:1, 0:128]:
                               e.tensor_copy(_r, _p),
                               waits=[(ctx.es["tensor"], ccs)] + row_free[ui % 2],
                               inc=ctx.es["vector"])
                cb3 = ctx.emit("tensor",
                               lambda e, _p=pr.PS[csb][:, 0:128], _r=rrow:
                               e.matmul(_p, pr.ones_f[0:1, :], _r,
                                        start=True, stop=True),
                               waits=[(ctx.es["vector"], ce1)], inc=ctx.es["tensor"])
                row_free[ui % 2] = [(ctx.es["tensor"], cb3)]
                cln = ctx.emit("scalar",
                               lambda e, _d=pr.SCR[ss][0:64, 128:256],
                                      _p=pr.PS[csb][0:64, 0:128]:
                               e.activation(_d, _p, AF.Ln),
                               waits=[(ctx.es["tensor"], cb3)] + scr_hi[ss],
                               inc=ctx.es["scalar"])
                pr.bank_free[csb] = [(ctx.es["scalar"], cln)]
                crcp = ctx.emit("scalar",
                                lambda e, _d=pr.SCR[ss][0:64, 256:384],
                                       _s=pr.SCR[ss][0:64, 128:256]:
                                e.activation(_d, _s, AF.Exp, scale=-1.0),
                                inc=ctx.es["scalar"])
                ono = pr.BIG[pb:pb + 64, base_q: base_q + 128]
                cdv = ctx.emit("vector",
                               lambda e, _d=ono, _o=pr.PS[otb][0:64, 0:128],
                                      _r=pr.SCR[ss][0:64, 256:384]:
                               e.tensor_tensor(_d, _o, _r, ALU.mult),
                               waits=[(ctx.es["scalar"], crcp),
                                      (ctx.es["tensor"], cpv)],
                               inc=ctx.es["vector"])
                pr.bank_free[otb] = [(ctx.es["vector"], cdv)]
                scr_hi[ss] = [(ctx.es["vector"], cdv)]
                ui += 1
    ctx.barrier()
    stream_proj(pr, "b1_wo", (128, 8, 8, 128), 8, 8, big_rhs(pr),
                delta_writer(pr, pr.AUX, bias=pr.biasC["b1_bo"]), bank_n=4)
    ctx.barrier()
    stream_proj(pr, "b1_proj", (128, NCH, 8, 128), NCH, 8,
                lambda cc, th: pr.AUX[:, cc * 1024 + th * 512: cc * 1024 + th * 512 + 512],
                delta_writer(pr, pr.Y, bias=pr.biasC["b1_projb"]))
    ctx.barrier()



PHASES = {1: "attn_a", 2: "ffn0", 3: "b0", 4: "ffn0",
          5: "a1", 6: "ffn1", 7: "b1", 8: "ffn1"}


def emit_sublayer(pr, k, first, stats_next):
    kind = PHASES[k]
    if first:
        pr.pass_A(padded=(kind == "b1"))
    else:
        pr.pass_A_tail()
    pr.pass_B(padded=(kind == "b1"))
    if kind == "attn_a":
        ph_attn_a(pr, stats_next)
        return
    if kind == "ffn0":
        ph_ffn0(pr)
        pr.pass_D(pr.BIG, stats_next=stats_next)
    elif kind == "ffn1":
        ph_ffn1(pr)
        pr.pass_D(pr.BIG, stats_next=stats_next)
    elif kind == "b0":
        ph_b0(pr)
        pr.pass_D(pr.Y, stats_next=stats_next)
    elif kind == "a1":
        ph_a1(pr)
        pr.pass_D(pr.Y, stats_next=stats_next)
    elif kind == "b1":
        ph_b1(pr)
        pr.pass_D(pr.Y, stats_next=stats_next)


def build_launch(sublayers, mode):
    nc = bass.Bass("TRN2", target_bir_lowering=False, debug=False)
    with ExitStack() as stack:
        pr = Prog(nc, stack)
        x_in = pr.din("x_in", (NCH, 128, TLOC))
        x_out = nc.dram_tensor("x_out", [NCH, 128, TLOC], F32, kind="ExternalOutput")
        pr.load_consts()
        if mode == "fused" and len(sublayers) > 1:
            xm = pr.dint("x_master", (NCH, 128, TLOC), F32)
            dma(pr.ctx, "sync", xm.ap()[:, :, :], x_in.ap()[:, :, :],
                slot_sem=pr.s_store)
            pr.ctx.barrier()
            for i, k in enumerate(sublayers):
                pr.x_src = xm
                pr.x_dst = xm if i < len(sublayers) - 1 else x_out
                nxt = PHASES[sublayers[i + 1]] if i + 1 < len(sublayers) else None
                emit_sublayer(pr, k, i == 0, nxt)
        else:
            for k in sublayers:
                pr.x_src = x_in
                pr.x_dst = x_out
                emit_sublayer(pr, k, True, None)
        for e in ENGS:
            pr.ctx.emit(e, None, waits=[(pr.s_store, pr.s_store.n)])
        block = stack.enter_context(nc.Block())
        pr.ctx.finalize(block)
        names = set(pr.input_names)
    return nc, names


def kernel(**inputs):
    sh, per_core, inv = prep_host(inputs)
    groups = [SUBLAYERS] if MODE == "fused" else [[k] for k in SUBLAYERS]
    xcur = [pc["x_in"] for pc in per_core]
    for group in groups:
        nc, input_names = build_launch(group, MODE)
        in_maps = []
        for c in range(N_CORES):
            m = {}
            for nm in input_names:
                if nm == "x_in":
                    m[nm] = xcur[c]
                elif nm in per_core[c]:
                    m[nm] = per_core[c][nm]
                else:
                    m[nm] = sh[nm]
            in_maps.append(m)
        res = run_bass_kernel_spmd(nc, in_maps, core_ids=list(range(N_CORES)))
        for key in res.results[0]:
            if key.startswith("dbg"):
                DEBUG_RES[key] = [np.asarray(r[key]) for r in res.results]
        xcur = [r["x_out"] for r in res.results]

    out = np.empty((B, SEQ, D), np.float32)
    for c in range(N_CORES):
        xf = xcur[c].reshape(D, B, SLOC)
        out[:, SLOC * c: SLOC * (c + 1), :] = xf.transpose(1, 2, 0)
    return np.ascontiguousarray(out[:, :, inv])


# version 16
# speedup vs baseline: 1.0348x; 1.0348x over previous
import math
from contextlib import ExitStack

import numpy as np
import ml_dtypes

import concourse.bass as bass
import concourse.mybir as mybir
from concourse.bass_utils import run_bass_kernel_spmd

F32 = mybir.dt.float32
BF16 = mybir.dt.bfloat16
AF = mybir.ActivationFunctionType
ALU = mybir.AluOpType
AX = mybir.AxisListType

N_CORES = 8
B = 2
SEQ = 4096
D = 4096
NCH = 32
SLOC = 512
TLOC = 1024
YST = 1032
NEG = -1e9
RS128 = 1.0 / math.sqrt(128.0)
RS64 = 0.125

SUBLAYERS = list(range(1, 9))
MODE = "fused"
DEBUG_TAP = None
DEBUG_RES = {}


def _tile_lhsT(WT):
    Dk, O = WT.shape
    return np.ascontiguousarray(
        WT.reshape(Dk // 128, 128, O // 128, 128).transpose(1, 2, 0, 3))


def _tile_rhs(W):
    Dk, O = W.shape
    return np.ascontiguousarray(W.reshape(Dk // 128, 128, O).transpose(1, 0, 2))


def _bias_col(b):
    return np.ascontiguousarray(np.asarray(b, np.float32).reshape(-1, 128).T)


def prep_host(inputs):
    f32 = np.float32
    perm = np.asarray(inputs["L0_ffn_perm"]).astype(np.int64)
    inv = np.argsort(perm)
    sh = {}

    x = np.asarray(inputs["L0_attn_a_x"], dtype=f32)
    xp = np.ascontiguousarray(x[:, :, perm])
    per_core = []
    for c in range(N_CORES):
        xc = xp[:, SLOC * c: SLOC * (c + 1), :]
        xf = np.ascontiguousarray(xc.transpose(2, 0, 1).reshape(NCH, 128, TLOC))
        per_core.append({"x_in": xf})

    sh["c_ones_f"] = np.ones((128, 128), f32)
    sh["c_negones"] = np.full((1, 128), -1.0, f32)
    sh["c_onescol"] = np.ones((128, 1), f32)
    sh["c_ident"] = np.eye(128, dtype=f32)
    idx = np.arange(128)
    sh["c_mask_diag"] = np.where(idx[:, None] <= idx[None, :], 0.0, NEG).astype(f32)
    sh["c_mask_tail"] = np.where(idx[:, None] > idx[None, :], 0.0, NEG).astype(f32)
    PT = np.zeros((128, 128), f32)
    for e in range(64):
        PT[e + 64, e] = -1.0
        PT[e, e + 64] = 1.0
    sh["c_rotPT"] = PT
    E = np.zeros((8, 128), f32)
    for e in range(8):
        E[e, 16 * e: 16 * (e + 1)] = 1.0
    sh["c_expE"] = E

    half = np.arange(0, 128, 2, dtype=f32)
    inv_freq = 1.0 / (10000.0 ** (half / 128.0))
    t = np.arange(SEQ, dtype=f32)
    emb = np.concatenate([np.outer(t, inv_freq)] * 2, axis=1)
    cosT = np.cos(emb).T.astype(f32)
    sinT = np.sin(emb).T.astype(f32)
    for c in range(N_CORES):
        per_core[c]["c_cosT"] = np.ascontiguousarray(cosT[:, SLOC * c: SLOC * (c + 1)])
        per_core[c]["c_sinT"] = np.ascontiguousarray(sinT[:, SLOC * c: SLOC * (c + 1)])
        vcol = np.zeros((128, 12), f32)
        for j in range(12):
            if c + j // 4 - 2 < 0:
                vcol[:, j] = NEG
        per_core[c]["b0_vcol"] = vcol
        hv = np.zeros((128, 2), f32)
        hv[:, 0] = 1.0 if c > 0 else 0.0
        hv[:, 1] = 1.0 if c < N_CORES - 1 else 0.0
        per_core[c]["b1_hv"] = hv

    for nm in ("Wq", "Wk", "Wv", "Wo"):
        W = np.asarray(inputs[f"L0_attn_b_{nm}"], f32)
        WT = np.ascontiguousarray(W[perm, :].T) if nm == "Wo" else \
            np.ascontiguousarray(W[:, perm].T)
        sh[f"b0_{nm}T"] = _tile_lhsT(WT)
    up = np.asarray(inputs["L0_ffn_up"], f32)
    down = np.asarray(inputs["L0_ffn_down"], f32)
    sh["f0_up"] = np.ascontiguousarray(
        up.reshape(16, 2, 128, 8, 128).transpose(0, 2, 1, 3, 4).reshape(16, 128, 2048))
    sh["f0_down"] = np.ascontiguousarray(
        down.reshape(16, 8, 128, 2, 128).transpose(0, 2, 1, 3, 4).reshape(16, 128, 2048))
    for nm in ("Wq", "Wk", "Wv", "Wo"):
        W = np.asarray(inputs[f"L1_attn_a_{nm}"], f32)
        WT = np.ascontiguousarray(W[perm, :].T) if nm == "Wo" else \
            np.ascontiguousarray(W[:, perm].T)
        sh[f"a1_{nm}T"] = _tile_lhsT(WT)
    Wb = np.asarray(inputs["L1_ffn_W_base"], f32)
    sh["f1_WbT"] = _tile_lhsT(np.ascontiguousarray(Wb[perm][:, perm].T))
    sh["f1_bias"] = _bias_col(np.asarray(inputs["L1_ffn_b_base"], f32)[perm])
    Wr = np.asarray(inputs["L1_ffn_W_router"], f32)
    WrT = np.ascontiguousarray(Wr[:, perm].T)
    sh["f1_WrT"] = np.ascontiguousarray(WrT.reshape(NCH, 128, 8).transpose(1, 0, 2))
    A = np.asarray(inputs["L1_ffn_lora_A"], f32)
    A2T = np.ascontiguousarray(A.transpose(2, 0, 1).reshape(D, 128))[perm, :]
    sh["f1_A2T"] = np.ascontiguousarray(A2T.reshape(NCH, 128, 128).transpose(1, 0, 2))
    Bm = np.asarray(inputs["L1_ffn_lora_B"], f32)
    B2 = np.ascontiguousarray(Bm.transpose(0, 2, 1).reshape(128, D))[:, perm]
    sh["f1_B2"] = np.ascontiguousarray(B2.reshape(128, NCH, 128))
    cw = np.asarray(inputs["L1_attn_b_conv_w"], f32)[:, perm, :]
    sh["b1_conv"] = np.ascontiguousarray(
        cw.reshape(3, NCH, 128, 24, 128).transpose(0, 2, 3, 1, 4))
    sh["b1_convb"] = _bias_col(inputs["L1_attn_b_conv_b"])
    for nm in ("wq", "wk", "wv"):
        W2 = np.asarray(inputs[f"L1_attn_b_{nm}"], f32).reshape(1024, 1024)
        if nm == "wv":
            sh["b1_wv_rhs"] = _tile_rhs(W2)
        else:
            sh[f"b1_{nm}"] = _tile_lhsT(W2)
        sh[f"b1_b{nm[1]}"] = _bias_col(
            np.asarray(inputs[f"L1_attn_b_b{nm[1]}"], f32).reshape(1024))
    sh["b1_bv_row"] = np.ascontiguousarray(
        np.asarray(inputs["L1_attn_b_bv"], f32).reshape(1, 1024))
    wo = np.asarray(inputs["L1_attn_b_wo"], f32).reshape(1024, 1024)
    sh["b1_wo"] = _tile_lhsT(wo)
    sh["b1_bo"] = _bias_col(inputs["L1_attn_b_bo"])
    pw = np.asarray(inputs["L1_attn_b_proj_w"], f32)
    sh["b1_proj"] = _tile_lhsT(np.ascontiguousarray(pw[:, perm]))
    sh["b1_projb"] = _bias_col(np.asarray(inputs["L1_attn_b_proj_b"], f32)[perm])
    return sh, per_core, inv



ENGS = ("tensor", "vector", "scalar", "gpsimd", "sync")


class SemT:
    __slots__ = ("h", "n", "name")

    def __init__(self, h, name):
        self.h, self.n, self.name = h, 0, name


class Ctx:
    def __init__(self, nc, stack):
        self.nc = nc
        self.stack = stack
        self.ops = {e: [] for e in ENGS}
        self.sems = []
        self._semi = 0
        self.es = {e: self.new_sem("es_" + e) for e in ENGS}
        self.pid = {}
        self.need_pid = set()

    def new_sem(self, name=None):
        name = f"{name}_{self._semi}" if name else f"s{self._semi}"
        self._semi += 1
        h = self.stack.enter_context(self.nc.semaphore(name))
        s = SemT(h, name)
        self.sems.append(s)
        return s

    def emit(self, eng, build, waits=(), inc=None, inc_by=1):
        waits = [(s.h, v) for (s, v) in waits if v > 0]
        after = None
        ih = None
        if inc is not None:
            inc.n += inc_by
            after = inc.n
            ih = inc.h

        def op(e, _w=waits, _b=build, _ih=ih, _iv=inc_by):
            for (h, v) in _w:
                e.wait_ge(h, v)
            ins = _b(e) if _b is not None else None
            if _ih is not None:
                if ins is None:
                    ins = e.nop()
                ins.then_inc(_ih, _iv)
        self.ops[eng].append(op)
        return after

    def vN(self):
        return self.es["vector"].n

    def barrier(self):
        vals = [(s, s.n) for s in self.sems if s.n > 0]
        for e in ENGS:
            self.emit(e, None, waits=vals)

    def finalize(self, block):
        for e in ENGS:
            ops = self.ops[e]
            sec = getattr(block, e)
            need = e in self.need_pid

            def runner(eng, _ops=ops, _e=e, _need=need):
                if _need:
                    self.pid[_e] = self.nc.partition_id()
                for f in _ops:
                    f(eng)
            sec(runner)


class Slot:
    __slots__ = ("ap", "sem", "free_waits")

    def __init__(self, ap, sem):
        self.ap, self.sem, self.free_waits = ap, sem, []


class Ring:
    def __init__(self, ctx, aps, name):
        self.slots = [Slot(ap, ctx.new_sem(f"{name}{i}")) for i, ap in enumerate(aps)]
        self.i = 0

    def next(self):
        s = self.slots[self.i % len(self.slots)]
        self.i += 1
        return s


def dma(ctx, queue, dst, src, slot_sem=None, waits=()):
    if slot_sem is not None:
        return ctx.emit(queue, lambda e, _d=dst, _s=src: e.dma_start(out=_d, in_=_s),
                        waits=waits, inc=slot_sem, inc_by=16)
    ctx.emit(queue, lambda e, _d=dst, _s=src: e.dma_start(out=_d, in_=_s), waits=waits)
    return None



class Prog:
    def __init__(self, nc, stack):
        self.nc = nc
        self.stack = stack
        self.ctx = Ctx(nc, stack)
        self.dram = {}
        self.input_names = set()

        def sb(name, shape, dt):
            return stack.enter_context(nc.sbuf_tensor(name, shape, dt))
        self.Y = sb("Y", [128, NCH * YST], BF16)
        self.BIG = sb("BIGB", [128, NCH * 1024], BF16)
        self.AUX = sb("AUX", [128, 8192], BF16)
        self.NB = sb("NB", [128, 1024], BF16)
        self.BRr = sb("BRr", [128, 1024], BF16)
        self.XS = [sb(f"XS{i}", [128, 1024], F32) for i in range(2)]
        self.WG = [sb(f"WG{i}", [128, 4096], BF16) for i in range(2)]
        self.SCR = [sb(f"SCR{i}", [128, 512], F32) for i in range(3)]
        self.PTB = [sb(f"PTB{i}", [128, 512], BF16) for i in range(2)]
        self.KSLB = sb("KSLB", [128, 4096], BF16)
        self.VTB = sb("VTB", [128, 4096], BF16)
        self.ROWS = sb("ROWS", [128, 1024], F32)
        self.STAT = sb("STAT", [128, 160], F32)
        self.WR = sb("WR", [128, 256], BF16)
        self.KNM = sb("KNM", [1, 128], F32)
        self.ones_f = sb("ones_f", [128, 128], F32)
        self.ones_b = sb("ones_b", [1, 128], BF16)
        self.negones_b = sb("negones_b", [1, 128], BF16)
        self.onescol_f = sb("onescol_f", [128, 1], F32)
        self.onescol_b = sb("onescol_b", [128, 1], BF16)
        self.ident_f = sb("ident_f", [128, 128], F32)
        self.mask_diag = sb("mask_diag", [128, 128], BF16)
        self.mask_tail = sb("mask_tail", [128, 128], BF16)
        self.rotPT = sb("rotPT", [128, 128], BF16)
        self.expE = sb("expE", [8, 128], BF16)
        self.cosT = sb("cosT", [128, 512], BF16)
        self.sinT = sb("sinT", [128, 512], BF16)
        self.vcolB = sb("vcolB", [128, 12], F32)
        self.hvB = sb("hvB", [128, 2], F32)
        self.biasC = {}
        for nm, w in (("f1_bias", 32), ("b1_convb", 24), ("b1_bq", 8),
                      ("b1_bk", 8), ("b1_bo", 8), ("b1_projb", 32)):
            self.biasC[nm] = sb("bc_" + nm, [128, w], F32)
        self.W8T = self.KSLB
        self.HSB = self.VTB
        self.HSB2 = self.VTB
        self.PS = [stack.enter_context(nc.psum_tensor(f"PS{i}", [128, 512], F32))
                   for i in range(8)]
        self.bank_free = [[] for _ in range(8)]

        self.r_x = Ring(self.ctx, [t[:, :] for t in self.XS], "rx")
        self.r_wg = Ring(self.ctx, [t[:, :] for t in self.WG], "rwg")
        self.s_store = self.ctx.new_sem("st")
        self.x_src = None
        self.x_dst = None

    def din(self, name, shape, dt=F32):
        if name not in self.dram:
            self.dram[name] = self.nc.dram_tensor(name, list(shape), dt,
                                                  kind="ExternalInput")
            self.input_names.add(name)
        return self.dram[name]

    def dint(self, name, shape, dt=BF16):
        if name not in self.dram:
            self.dram[name] = self.nc.dram_tensor(name, list(shape), dt, kind="Internal")
        return self.dram[name]

    def load_consts(self):
        ctx = self.ctx
        sem = ctx.new_sem("consts")
        pairs = [
            ("c_ones_f", self.ones_f[:, :], (128, 128)),
            ("c_onescol", self.onescol_f[:, :], (128, 1)),
            ("c_ident", self.ident_f[:, :], (128, 128)),
            ("c_mask_diag", self.mask_diag[:, :], (128, 128)),
            ("c_mask_tail", self.mask_tail[:, :], (128, 128)),
            ("c_rotPT", self.rotPT[:, :], (128, 128)),
            ("c_expE", self.expE[0:8, :], (8, 128)),
            ("c_cosT", self.cosT[:, :], (128, 512)),
            ("c_sinT", self.sinT[:, :], (128, 512)),
            ("b0_vcol", self.vcolB[:, :], (128, 12)),
            ("b1_hv", self.hvB[:, :], (128, 2)),
            ("c_negones", self.negones_b[0:1, :], (1, 128)),
        ]
        for nm, dst, shp in pairs:
            t = self.din(nm, shp)
            dma(ctx, "gpsimd", dst, t.ap(), slot_sem=sem)
        for nm, buf in self.biasC.items():
            t = self.din(nm, (128, buf.shape[1]))
            dma(ctx, "gpsimd", buf[:, :], t.ap(), slot_sem=sem)
        ctx.emit("vector", lambda e: e.tensor_copy(self.ones_b[0:1, :], self.ones_f[0:1, :]),
                 waits=[(sem, sem.n)], inc=ctx.es["vector"])
        ctx.emit("vector", lambda e: e.tensor_copy(self.onescol_b[:, :], self.onescol_f[:, :]),
                 inc=ctx.es["vector"])
        ctx.barrier()

    @staticmethod
    def col_n(ch, b):
        return ch * 1024 + b * 512

    @staticmethod
    def col_p(ch, b):
        return ch * YST + 1 + b * 514

    def pass_A(self, padded=False):
        ctx = self.ctx
        col = self.col_p if padded else self.col_n
        sq_free = [[], []]
        for ch in range(NCH):
            sl = self.r_x.next()
            c = dma(ctx, "sync", sl.ap, self.x_src.ap()[ch], slot_sem=sl.sem,
                    waits=sl.free_waits)
            casts = []
            for b in range(2):
                ycols = self.Y[:, col(ch, b): col(ch, b) + 512]
                cv = ctx.emit("scalar",
                              lambda e, _y=ycols, _x=sl.ap[:, b * 512:(b + 1) * 512]:
                              e.copy(_y, _x),
                              waits=[(sl.sem, c)] if b == 0 else [],
                              inc=ctx.es["scalar"])
                casts.append((ycols, cv))
            sl.free_waits = [(ctx.es["scalar"], casts[-1][1])]
            for b in range(2):
                ycols, cv = casts[b]
                si = (2 * ch + b) % 2
                sqa = self.AUX[:, si * 512: si * 512 + 512]
                cs = ctx.emit("scalar",
                              lambda e, _o=sqa, _y=ycols: e.activation(_o, _y, AF.Square),
                              waits=sq_free[si], inc=ctx.es["scalar"])
                ctx.emit("tensor",
                         lambda e, _p=self.PS[b][0:1, :], _r=ycols, _s=(ch == 0),
                                _e=(ch == NCH - 1):
                         e.matmul(_p, self.onescol_b[:, 0:1], _r, start=_s, stop=_e),
                         waits=[(ctx.es["scalar"], cv)], inc=ctx.es["tensor"])
                last = ctx.emit("tensor",
                                lambda e, _p=self.PS[2 + b][0:1, :], _r=sqa, _s=(ch == 0),
                                       _e=(ch == NCH - 1):
                                e.matmul(_p, self.onescol_b[:, 0:1], _r, start=_s, stop=_e),
                                waits=[(ctx.es["scalar"], cs)], inc=ctx.es["tensor"])
                sq_free[si] = [(ctx.es["tensor"], last)]
        self._rows_and_broadcasts()
        ctx.barrier()

    def _rows_and_broadcasts(self):
        ctx = self.ctx
        R = self.ROWS
        S0 = self.SCR[0]
        tN = ctx.es["tensor"].n
        prevv = 0
        for th in range(2):
            c = ctx.emit("vector", lambda e, _p=self.PS[th][0:1, :]:
                     e.tensor_copy(R[0:1, 0:512], _p),
                     waits=[(ctx.es["tensor"], tN), (ctx.es["vector"], prevv)],
                     inc=ctx.es["vector"])
            c = ctx.emit("vector", lambda e, _p=self.PS[2 + th][0:1, :]:
                     e.tensor_copy(S0[0:1, 0:512], _p),
                     waits=[(ctx.es["vector"], c)], inc=ctx.es["vector"])
            c = ctx.emit("vector", lambda e: e.tensor_scalar(R[0:1, 0:512], R[0:1, 0:512],
                                                             1.0 / D, None, ALU.mult),
                     waits=[(ctx.es["vector"], c)], inc=ctx.es["vector"])
            c = ctx.emit("vector", lambda e: e.tensor_tensor(R[0:1, 512:1024], R[0:1, 0:512],
                                                             R[0:1, 0:512], ALU.mult),
                     waits=[(ctx.es["vector"], c)], inc=ctx.es["vector"])
            c = ctx.emit("vector", lambda e: e.tensor_scalar(S0[0:1, 0:512], S0[0:1, 0:512],
                                                             1.0 / D, None, ALU.mult),
                     waits=[(ctx.es["vector"], c)], inc=ctx.es["vector"])
            c = ctx.emit("vector", lambda e: e.tensor_tensor(S0[0:1, 0:512], S0[0:1, 0:512],
                                                             R[0:1, 512:1024], ALU.subtract),
                     waits=[(ctx.es["vector"], c)], inc=ctx.es["vector"])
            cvar = ctx.emit("vector", lambda e: e.tensor_scalar(S0[0:1, 0:512],
                                                                S0[0:1, 0:512],
                                                                1e-5, None, ALU.add),
                            waits=[(ctx.es["vector"], c)], inc=ctx.es["vector"])
            csd = ctx.emit("scalar", lambda e: e.activation(R[0:1, 512:1024],
                                                            S0[0:1, 0:512], AF.Sqrt),
                           waits=[(ctx.es["vector"], cvar)], inc=ctx.es["scalar"])
            crs = ctx.emit("vector", lambda e: e.reciprocal(S0[0:1, 0:512],
                                                            R[0:1, 512:1024]),
                           waits=[(ctx.es["scalar"], csd)], inc=ctx.es["vector"])
            cnm = ctx.emit("vector", lambda e: e.tensor_scalar(R[0:1, 0:512],
                                                               R[0:1, 0:512], -1.0,
                                                               None, ALU.mult),
                           waits=[(ctx.es["vector"], crs)], inc=ctx.es["vector"])
            m1 = ctx.emit("tensor",
                          lambda e: e.matmul(self.PS[4][:, :], self.ones_f[0:1, :],
                                             R[0:1, 0:512], start=True, stop=True),
                          waits=[(ctx.es["vector"], cnm)] + self.bank_free[4],
                          inc=ctx.es["tensor"])
            c1 = ctx.emit("vector",
                          lambda e, _d=self.NB[:, th * 512:(th + 1) * 512]:
                          e.tensor_copy(_d, self.PS[4][:, :]),
                          waits=[(ctx.es["tensor"], m1)], inc=ctx.es["vector"])
            self.bank_free[4] = [(ctx.es["vector"], c1)]
            m2 = ctx.emit("tensor",
                          lambda e: e.matmul(self.PS[5][:, :], self.ones_f[0:1, :],
                                             S0[0:1, 0:512], start=True, stop=True),
                          waits=[(ctx.es["vector"], crs)] + self.bank_free[5],
                          inc=ctx.es["tensor"])
            c2 = ctx.emit("vector",
                          lambda e, _d=self.BRr[:, th * 512:(th + 1) * 512]:
                          e.tensor_copy(_d, self.PS[5][:, :]),
                          waits=[(ctx.es["tensor"], m2)], inc=ctx.es["vector"])
            self.bank_free[5] = [(ctx.es["vector"], c2)]
            prevv = ctx.es["vector"].n

    def pass_B(self, padded=False):
        ctx = self.ctx
        col = self.col_p if padded else self.col_n
        slot_w = {0: [], 1: []}
        for ch in range(NCH):
            for b in range(2):
                ycols = self.Y[:, col(ch, b): col(ch, b) + 512]
                si = (2 * ch + b) % 2
                tt = self.SCR[si][:, :]
                ca = ctx.emit("gpsimd",
                              lambda e, _t=tt, _s=ycols, _n=self.NB[:, b * 512:(b + 1) * 512]:
                              e.tensor_tensor(_t, _s, _n, ALU.add),
                              waits=slot_w[si], inc=ctx.es["gpsimd"])
                cv = ctx.emit("vector",
                              lambda e, _d=ycols, _t=tt, _r=self.BRr[:, b * 512:(b + 1) * 512]:
                              e.tensor_tensor(_d, _t, _r, ALU.mult),
                              waits=[(ctx.es["gpsimd"], ca)], inc=ctx.es["vector"])
                slot_w[si] = [(ctx.es["vector"], cv)]
        ctx.barrier()

    def emit_chunk_stats(self, ch, slot_ap, dep, padded_next):
        ctx = self.ctx
        col = self.col_p if padded_next else self.col_n
        casts = []
        for b in range(2):
            ycols = self.Y[:, col(ch, b): col(ch, b) + 512]
            cv = ctx.emit("scalar",
                          lambda e, _y=ycols, _x=slot_ap[:, b * 512:(b + 1) * 512]:
                          e.copy(_y, _x),
                          waits=[dep] if b == 0 else [],
                          inc=ctx.es["scalar"])
            casts.append((ycols, cv))
        for b in range(2):
            ycols, cv = casts[b]
            si = (2 * ch + b) % 2
            sqa = self.AUX[:, si * 512: si * 512 + 512]
            cs = ctx.emit("scalar",
                          lambda e, _o=sqa, _y=ycols: e.activation(_o, _y, AF.Square),
                          waits=self._sq_free[si], inc=ctx.es["scalar"])
            ctx.emit("tensor",
                     lambda e, _p=self.PS[b][0:1, :], _r=ycols, _s=(ch == 0),
                            _e=(ch == NCH - 1):
                     e.matmul(_p, self.onescol_b[:, 0:1], _r, start=_s, stop=_e),
                     waits=[(ctx.es["scalar"], cv)] +
                           (self.bank_free[b] if ch == 0 else []),
                     inc=ctx.es["tensor"])
            last = ctx.emit("tensor",
                            lambda e, _p=self.PS[2 + b][0:1, :], _r=sqa, _s=(ch == 0),
                                   _e=(ch == NCH - 1):
                            e.matmul(_p, self.onescol_b[:, 0:1], _r, start=_s, stop=_e),
                            waits=[(ctx.es["scalar"], cs)] +
                                  (self.bank_free[2 + b] if ch == 0 else []),
                            inc=ctx.es["tensor"])
            self._sq_free[si] = [(ctx.es["tensor"], last)]
            if ch == NCH - 1:
                self.bank_free[b] = []
                self.bank_free[2 + b] = []
        return (ctx.es["scalar"], casts[-1][1])

    def pass_D(self, delta_buf, dstride=1024, stats_next=None):
        ctx = self.ctx
        self._sq_free = [[], []]
        for ch in range(NCH):
            sl = self.r_x.next()
            c = dma(ctx, "sync", sl.ap, self.x_src.ap()[ch], slot_sem=sl.sem,
                    waits=sl.free_waits)
            dcols = delta_buf[:, ch * dstride: ch * dstride + 1024]
            cv = ctx.emit("vector",
                          lambda e, _x=sl.ap, _d=dcols: e.tensor_tensor(_x, _x, _d, ALU.add),
                          waits=[(sl.sem, c)], inc=ctx.es["vector"])
            cs = dma(ctx, "sync", self.x_dst.ap()[ch], sl.ap, slot_sem=self.s_store,
                     waits=[(ctx.es["vector"], cv)])
            sl.free_waits = [(self.s_store, cs)]
            if stats_next is not None:
                cst = self.emit_chunk_stats(ch, sl.ap, (ctx.es["vector"], cv),
                                            stats_next == "b1")
                sl.free_waits = sl.free_waits + [cst]
        ctx.barrier()

    def pass_A_tail(self):
        self._rows_and_broadcasts()
        self.ctx.barrier()



def stream_proj(pr, wname, wshape, n_oc, n_cc, rhs_fn, out_write,
                bank_base=0, bank_n=8, extra_accum=None, gran_sel=None):
    ctx = pr.ctx
    w = pr.din(wname, wshape)
    ppr = bank_n // 2
    for oc in range(n_oc):
        sl = pr.r_wg.next()
        src = w.ap()[:, oc] if gran_sel is None else gran_sel(w, oc)
        c = dma(ctx, "gpsimd", sl.ap[:, 0: n_cc * 128],
                src.rearrange("p c j -> p (c j)"), slot_sem=sl.sem,
                waits=sl.free_waits)
        banks = [bank_base + (oc % ppr) * 2 + th for th in range(2)]
        last = None
        for cc in range(n_cc):
            tile = sl.ap[:, cc * 128:(cc + 1) * 128]
            for th in range(2):
                bw = []
                if cc == 0:
                    bw = list(pr.bank_free[banks[th]])
                    if th == 0:
                        bw = bw + [(sl.sem, c)]
                last = ctx.emit(
                    "tensor",
                    lambda e, _p=pr.PS[banks[th]][:, :], _t=tile, _r=rhs_fn(cc, th),
                           _s=(cc == 0), _e=(cc == n_cc - 1 and extra_accum is None):
                    e.matmul(_p, _t, _r, start=_s, stop=_e),
                    waits=bw, inc=ctx.es["tensor"])
        sl.free_waits = [(ctx.es["tensor"], last)]
        for th in range(2):
            cnt = last
            if extra_accum is not None:
                cnt = extra_accum(oc, th, pr.PS[banks[th]], cnt)
        for th in range(2):
            cnt = ctx.es["tensor"].n
            pr.bank_free[banks[th]] = [out_write(oc, th, pr.PS[banks[th]], cnt)]


def delta_writer(pr, buf, bias=None, stride=1024, col_off=0):
    ctx = pr.ctx

    def wr(oc, th, psum, dep):
        cols = buf[:, col_off + oc * stride + th * 512:
                   col_off + oc * stride + th * 512 + 512]
        if bias is not None:
            cc = ctx.emit("vector",
                          lambda e, _d=cols, _p=psum[:, :], _b=bias[:, oc:oc + 1]:
                          e.tensor_scalar(_d, _p, _b, None, ALU.add),
                          waits=[(ctx.es["tensor"], dep)], inc=ctx.es["vector"])
            return (ctx.es["vector"], cc)
        cc = ctx.emit("vector",
                      lambda e, _d=cols, _p=psum[:, :]: e.tensor_copy(_d, _p),
                      waits=[(ctx.es["tensor"], dep)], inc=ctx.es["vector"])
        return (ctx.es["vector"], cc)
    return wr


def y_rhs(pr):
    return lambda cc, th: pr.Y[:, cc * 1024 + th * 512: cc * 1024 + th * 512 + 512]


def big_rhs(pr):
    return lambda cc, th: pr.BIG[:, cc * 1024 + th * 512: cc * 1024 + th * 512 + 512]



def ph_attn_a(pr, stats_next=None):
    ctx = pr.ctx
    ar_in = pr.dint("aa_in", [128, 96], F32)
    ar_out = pr.dint("aa_out", [128, 96], F32)
    ST = pr.STAT
    red_done = {0: [], 1: [], 2: []}
    for ch in range(NCH):
        y0 = pr.Y[:, ch * 1024: ch * 1024 + 512]
        y1 = pr.Y[:, ch * 1024 + 512: ch * 1024 + 1024]
        for pi, (a, b_, colo) in enumerate(((y0, y0, 0), (y0, y1, 32), (y1, y1, 64))):
            prod = pr.SCR[pi][:, :]
            cg = ctx.emit("gpsimd",
                          lambda e, _p=prod, _a=a, _b=b_:
                          e.tensor_tensor(_p, _a, _b, ALU.mult),
                          waits=red_done[pi], inc=ctx.es["gpsimd"])
            cv = ctx.emit("vector",
                          lambda e, _d=ST[:, colo + ch: colo + ch + 1], _p=prod:
                          e.tensor_reduce(_d, _p, AX.X, ALU.add),
                          waits=[(ctx.es["gpsimd"], cg)], inc=ctx.es["vector"])
            red_done[pi] = [(ctx.es["vector"], cv)]
    cv = ctx.es["vector"].n
    cst = dma(ctx, "sync", ar_in.ap(), ST[:, 0:96], slot_sem=pr.s_store,
              waits=[(ctx.es["vector"], cv)])
    sc = ctx.new_sem("aa_cc")
    ctx.emit("gpsimd",
             lambda e: e.collective_compute(
                 "AllReduce", ALU.add, replica_groups=[list(range(N_CORES))],
                 ins=[ar_in.ap()], outs=[ar_out.ap()]),
             waits=[(pr.s_store, cst)], inc=sc)
    sld = ctx.new_sem("aa_ld")
    cld = dma(ctx, "gpsimd", ST[:, 0:96], ar_out.ap(), slot_sem=sld, waits=[(sc, 1)])
    ctx.emit("vector", lambda e: e.tensor_tensor(ST[:, 96:128], ST[:, 0:32],
                                                 ST[:, 32:64], ALU.subtract),
             waits=[(sld, cld)], inc=ctx.es["vector"])
    c2 = ctx.emit("vector", lambda e: e.tensor_tensor(ST[:, 128:160], ST[:, 32:64],
                                                      ST[:, 64:96], ALU.subtract),
                  inc=ctx.es["vector"])
    ctx.emit("scalar", lambda e: e.activation(ST[:, 96:128], ST[:, 96:128], AF.Sigmoid,
                                              scale=1.0 / 64.0),
             waits=[(ctx.es["vector"], c2)], inc=ctx.es["scalar"])
    ctx.emit("scalar", lambda e: e.activation(ST[:, 128:160], ST[:, 128:160], AF.Sigmoid,
                                              scale=1.0 / 64.0), inc=ctx.es["scalar"])
    ctx.barrier()
    q0 = ST[:, 96:128]
    q1 = ST[:, 128:160]
    pr._sq_free = [[], []]
    gprev = []
    for ch in range(NCH):
        sl = pr.r_x.next()
        c = dma(ctx, "sync", sl.ap, pr.x_src.ap()[ch], slot_sem=sl.sem,
                waits=sl.free_waits)
        y0 = pr.Y[:, ch * 1024: ch * 1024 + 512]
        y1 = pr.Y[:, ch * 1024 + 512: ch * 1024 + 1024]
        x0 = sl.ap[:, 0:512]
        x1 = sl.ap[:, 512:1024]
        t0, t1, t2 = pr.SCR[0][:, :], pr.SCR[1][:, :], pr.SCR[2][:, :]
        p0c = q0[:, ch: ch + 1]
        p1c = q1[:, ch: ch + 1]
        a1 = ctx.emit("vector", lambda e, _t=t0, _y=y0, _p=p0c:
                      e.tensor_scalar(_t, _y, _p, None, ALU.mult), waits=gprev,
                      inc=ctx.es["vector"])
        a2 = ctx.emit("vector", lambda e, _t=t1, _y=y1, _p=p0c:
                      e.tensor_scalar(_t, _y, _p, -1.0, ALU.mult, op1=ALU.mult),
                      inc=ctx.es["vector"])
        a3 = ctx.emit("vector", lambda e, _t=t2, _y=y0, _p=p1c:
                      e.tensor_scalar(_t, _y, _p, None, ALU.mult), inc=ctx.es["vector"])
        ctx.emit("vector", lambda e, _x=x0, _y=y0: e.tensor_tensor(_x, _x, _y, ALU.add),
                 waits=[(sl.sem, c)], inc=ctx.es["vector"])
        ctx.emit("vector", lambda e, _x=x0, _y=y1: e.tensor_tensor(_x, _x, _y, ALU.add),
                 inc=ctx.es["vector"])
        c3 = ctx.emit("vector", lambda e, _x=x0, _t=t0: e.tensor_tensor(_x, _x, _t, ALU.add),
                      inc=ctx.es["vector"])
        ctx.emit("vector", lambda e, _x=x0, _t=t1: e.tensor_tensor(_x, _x, _t, ALU.add),
                 inc=ctx.es["vector"])
        cg1 = ctx.emit("gpsimd", lambda e, _x=x1, _y=y1:
                       e.tensor_scalar(_x, _y, 2.0, None, ALU.mult, accum_out=None)
                       if False else e.tensor_tensor(_x, _x, _y, ALU.add),
                       waits=[(sl.sem, c)], inc=ctx.es["gpsimd"])
        cg2 = ctx.emit("gpsimd", lambda e, _x=x1, _y=y1: e.tensor_tensor(_x, _x, _y, ALU.add),
                       inc=ctx.es["gpsimd"])
        c7 = ctx.emit("vector", lambda e, _x=x1, _t=t2: e.tensor_tensor(_x, _x, _t, ALU.add),
                      waits=[(ctx.es["gpsimd"], cg2)], inc=ctx.es["vector"])
        a4 = ctx.emit("vector", lambda e, _t=t0, _y=y1, _p=p1c:
                      e.tensor_scalar(_t, _y, _p, None, ALU.mult),
                      waits=[(ctx.es["vector"], c3)], inc=ctx.es["vector"])
        cl = ctx.emit("vector", lambda e, _x=x1, _t=t0:
                      e.tensor_tensor(_x, _x, _t, ALU.subtract),
                      waits=[(ctx.es["vector"], a4)], inc=ctx.es["vector"])
        gprev = [(ctx.es["vector"], cl)]
        cs = dma(ctx, "sync", pr.x_dst.ap()[ch], sl.ap, slot_sem=pr.s_store,
                 waits=[(ctx.es["vector"], cl)])
        sl.free_waits = [(pr.s_store, cs)]
        if stats_next is not None:
            cst = pr.emit_chunk_stats(ch, sl.ap, (ctx.es["vector"], cl),
                                      stats_next == "b1")
            sl.free_waits = sl.free_waits + [cst]
    ctx.barrier()



def ph_ffn0(pr):
    ctx = pr.ctx
    up = pr.din("f0_up", (16, 128, 2048))
    down = pr.din("f0_down", (16, 128, 2048))
    for h in range(16):
        slu = pr.r_wg.next()
        cu = dma(ctx, "gpsimd", slu.ap[:, 0:2048], up.ap()[h], slot_sem=slu.sem,
                 waits=slu.free_waits)
        sld = pr.r_wg.next()
        cd = dma(ctx, "gpsimd", sld.ap[:, 0:2048], down.ap()[h], slot_sem=sld.sem,
                 waits=sld.free_waits)
        last_up = None
        zlast = None
        for mc in range(8):
            bks = [(mc % 2) * 2, (mc % 2) * 2 + 1]
            for jc in range(2):
                tile = slu.ap[:, jc * 1024 + mc * 128: jc * 1024 + mc * 128 + 128]
                for th in range(2):
                    bw = []
                    if jc == 0:
                        bw = list(pr.bank_free[bks[th]])
                        if th == 0 and mc == 0:
                            bw = bw + [(slu.sem, cu)]
                    ych = 2 * h + jc
                    last_up = ctx.emit(
                        "tensor",
                        lambda e, _p=pr.PS[bks[th]][:, :], _t=tile,
                               _r=pr.Y[:, ych * 1024 + th * 512: ych * 1024 + th * 512 + 512],
                               _s=(jc == 0), _e=(jc == 1):
                        e.matmul(_p, _t, _r, start=_s, stop=_e),
                        waits=bw, inc=ctx.es["tensor"])
            for th in range(2):
                zc = pr.AUX[:, mc * 1024 + th * 512: mc * 1024 + th * 512 + 512]
                cz = ctx.emit("scalar",
                              lambda e, _z=zc, _p=pr.PS[bks[th]][:, :]:
                              e.activation(_z, _p, AF.Gelu),
                              waits=[(ctx.es["tensor"], last_up)], inc=ctx.es["scalar"])
                zlast = cz
                pr.bank_free[bks[th]] = [(ctx.es["scalar"], cz)]
        last_dn = None
        for djc in range(2):
            bks = [4 + djc * 2, 5 + djc * 2]
            for mc in range(8):
                tile = sld.ap[:, mc * 256 + djc * 128: mc * 256 + djc * 128 + 128]
                for th in range(2):
                    bw = []
                    if mc == 0:
                        bw = list(pr.bank_free[bks[th]])
                        if th == 0:
                            bw = bw + [(ctx.es["scalar"], zlast)]
                            if djc == 0:
                                bw = bw + [(sld.sem, cd)]
                    last_dn = ctx.emit(
                        "tensor",
                        lambda e, _p=pr.PS[bks[th]][:, :], _t=tile,
                               _r=pr.AUX[:, mc * 1024 + th * 512: mc * 1024 + th * 512 + 512],
                               _s=(mc == 0), _e=(mc == 7):
                        e.matmul(_p, _t, _r, start=_s, stop=_e),
                        waits=bw, inc=ctx.es["tensor"])
            for th in range(2):
                och = 2 * h + djc
                dcols = pr.BIG[:, och * 1024 + th * 512: och * 1024 + th * 512 + 512]
                cvv = ctx.emit("vector",
                               lambda e, _d=dcols, _p=pr.PS[bks[th]][:, :]:
                               e.tensor_copy(_d, _p),
                               waits=[(ctx.es["tensor"], last_dn)], inc=ctx.es["vector"])
                pr.bank_free[bks[th]] = [(ctx.es["vector"], cvv)]
        slu.free_waits = [(ctx.es["tensor"], last_up)]
        sld.free_waits = [(ctx.es["tensor"], last_dn)]
    ctx.barrier()



def ph_ffn1(pr):
    ctx = pr.ctx
    wr = pr.din("f1_WrT", (128, NCH, 8))
    a2 = pr.din("f1_A2T", (128, NCH, 128))
    b2 = pr.din("f1_B2", (128, NCH, 128))
    sem_w = ctx.new_sem("f1w")
    dma(ctx, "gpsimd", pr.WR[:, 0:256], wr.ap().rearrange("p c j -> p (c j)"),
        slot_sem=sem_w)
    dma(ctx, "gpsimd", pr.AUX[:, 0:4096], a2.ap().rearrange("p c j -> p (c j)"),
        slot_sem=sem_w)
    dma(ctx, "gpsimd", pr.AUX[:, 4096:8192], b2.ap().rearrange("p c j -> p (c j)"),
        slot_sem=sem_w)
    prev_ev = 0
    for th in range(2):
        for cc in range(NCH):
            bw = []
            if cc == 0:
                bw = [(sem_w, 16)] if th == 0 else [(ctx.es["vector"], prev_ev)]
            ctx.emit("tensor",
                     lambda e, _t=pr.WR[:, cc * 8:(cc + 1) * 8],
                            _r=pr.Y[:, cc * 1024 + th * 512: cc * 1024 + th * 512 + 512],
                            _s=(cc == 0), _e=(cc == NCH - 1):
                     e.matmul(pr.PS[6][0:8, :], _t, _r, start=_s, stop=_e),
                     waits=bw, inc=ctx.es["tensor"])
        prev_ev = ctx.emit("vector",
                           lambda e, _d=pr.ROWS[0:8, th * 512:(th + 1) * 512]:
                           e.tensor_copy(_d, pr.PS[6][0:8, :]),
                           waits=[(ctx.es["tensor"], ctx.es["tensor"].n)],
                           inc=ctx.es["vector"])
    lg_ev = prev_ev
    h_mm = None
    for th in range(2):
        for cc in range(NCH):
            bw = []
            if cc == 0:
                bw = list(pr.bank_free[3 + th])
                if th == 0:
                    bw = bw + [(sem_w, 32)]
            h_mm = ctx.emit("tensor",
                            lambda e, _p=pr.PS[3 + th][:, :],
                                   _t=pr.AUX[:, cc * 128:(cc + 1) * 128],
                                   _r=pr.Y[:, cc * 1024 + th * 512: cc * 1024 + th * 512 + 512],
                                   _s=(cc == 0), _e=(cc == NCH - 1):
                            e.matmul(_p, _t, _r, start=_s, stop=_e),
                            waits=bw, inc=ctx.es["tensor"])
    ps7_free = []
    for tt in range(8):
        ctr = ctx.emit("tensor",
                       lambda e, _i=pr.ROWS[0:8, tt * 128:(tt + 1) * 128]:
                       e.transpose(pr.PS[7][:, 0:8], _i, pr.ident_f[0:8, 0:8]),
                       waits=[(ctx.es["vector"], lg_ev)] + ps7_free,
                       inc=ctx.es["tensor"])
        lg = pr.STAT[:, tt * 8: tt * 8 + 8]
        cv0 = ctx.emit("vector", lambda e, _d=lg: e.tensor_copy(_d, pr.PS[7][:, 0:8]),
                       waits=[(ctx.es["tensor"], ctr)], inc=ctx.es["vector"])
        nm1 = pr.STAT[:, 64 + tt * 2: 64 + tt * 2 + 1]
        nm2 = pr.STAT[:, 64 + tt * 2 + 1: 64 + tt * 2 + 2]
        g1 = pr.SCR[0][:, 0:8]
        g2 = pr.SCR[1][:, 0:8]
        g3 = pr.SCR[2][:, 0:8]
        c = ctx.emit("vector", lambda e, _d=nm1, _l=lg:
                     e.tensor_reduce(_d, _l, AX.X, ALU.max, negate=True),
                     waits=[(ctx.es["vector"], cv0)], inc=ctx.es["vector"])
        c = ctx.emit("vector", lambda e, _d=g1, _l=lg, _n=nm1:
                     e.tensor_scalar(_d, _l, _n, None, ALU.add),
                     waits=[(ctx.es["vector"], c)], inc=ctx.es["vector"])
        c = ctx.emit("vector", lambda e, _d=g2, _g=g1:
                     e.tensor_scalar(_d, _g, 0.0, NEG, ALU.is_ge, op1=ALU.mult),
                     waits=[(ctx.es["vector"], c)], inc=ctx.es["vector"])
        c = ctx.emit("vector", lambda e, _d=g3, _l=lg, _g=g2:
                     e.tensor_tensor(_d, _l, _g, ALU.add),
                     waits=[(ctx.es["vector"], c)], inc=ctx.es["vector"])
        c = ctx.emit("vector", lambda e, _d=nm2, _g=g3:
                     e.tensor_reduce(_d, _g, AX.X, ALU.max, negate=True),
                     waits=[(ctx.es["vector"], c)], inc=ctx.es["vector"])
        cp8 = ctx.emit("scalar", lambda e, _d=g3, _g=g1: e.activation(_d, _g, AF.Exp),
                       waits=[(ctx.es["vector"], c)], inc=ctx.es["scalar"])
        c = ctx.emit("vector", lambda e, _d=g1, _l=lg, _n=nm2:
                     e.tensor_scalar(_d, _l, _n, None, ALU.add),
                     waits=[(ctx.es["vector"], c), (ctx.es["scalar"], cp8)],
                     inc=ctx.es["vector"])
        c = ctx.emit("vector", lambda e, _d=g1:
                     e.tensor_scalar(_d, _d, 0.0, None, ALU.is_ge),
                     waits=[(ctx.es["vector"], c)], inc=ctx.es["vector"])
        c = ctx.emit("vector", lambda e, _d=g1, _p=g3:
                     e.tensor_tensor(_d, _p, _d, ALU.mult),
                     waits=[(ctx.es["vector"], c)], inc=ctx.es["vector"])
        c = ctx.emit("vector", lambda e, _d=nm1, _g=g1:
                     e.tensor_reduce(_d, _g, AX.X, ALU.add),
                     waits=[(ctx.es["vector"], c)], inc=ctx.es["vector"])
        c = ctx.emit("vector", lambda e, _d=nm2, _n=nm1: e.reciprocal(_d, _n),
                     waits=[(ctx.es["vector"], c)], inc=ctx.es["vector"])
        cwf = ctx.emit("vector", lambda e, _d=g2, _g=g1, _n=nm2:
                       e.tensor_scalar(_d, _g, _n, 2.0, ALU.mult, op1=ALU.mult),
                       waits=[(ctx.es["vector"], c)], inc=ctx.es["vector"])
        ctr2 = ctx.emit("tensor",
                        lambda e, _i=g2: e.transpose(pr.PS[7][0:8, 0:128], _i,
                                                     pr.ident_f[:, :]),
                        waits=[(ctx.es["vector"], cwf)], inc=ctx.es["tensor"])
        cwe = ctx.emit("vector",
                       lambda e, _d=pr.KSLB[0:8, 1024 + tt * 128: 1024 + (tt + 1) * 128]:
                       e.tensor_copy(_d, pr.PS[7][0:8, 0:128]),
                       waits=[(ctx.es["tensor"], ctr2)], inc=ctx.es["vector"])
        ps7_free = [(ctx.es["vector"], cwe)]
    wt_done = ctx.es["vector"].n
    prev = 0
    for th in range(2):
        cm = ctx.emit("tensor",
                      lambda e, _r=pr.KSLB[0:8, 1024 + th * 512: 1024 + (th + 1) * 512]:
                      e.matmul(pr.PS[5][:, :], pr.expE[0:8, :], _r, start=True, stop=True),
                      waits=[(ctx.es["vector"], wt_done if th == 0 else prev)] +
                            (pr.bank_free[5] if th == 0 else []),
                      inc=ctx.es["tensor"])
        cev = ctx.emit("vector",
                       lambda e, _d=pr.PTB[th][:, :]: e.tensor_copy(_d, pr.PS[5][:, :]),
                       waits=[(ctx.es["tensor"], cm)], inc=ctx.es["vector"])
        prev = cev
        cg = ctx.emit("vector",
                      lambda e, _d=pr.KSLB[:, th * 512:(th + 1) * 512],
                             _h=pr.PS[3 + th][:, :], _w=pr.PTB[th][:, :]:
                      e.tensor_tensor(_d, _h, _w, ALU.mult),
                      waits=[(ctx.es["tensor"], h_mm), (ctx.es["vector"], cev)],
                      inc=ctx.es["vector"])
        pr.bank_free[3 + th] = [(ctx.es["vector"], cg)]
        pr.bank_free[5] = [(ctx.es["vector"], cev)]
    ctx.barrier()
    if DEBUG_TAP == "f1_gates":
        dbg1 = pr.nc.dram_tensor("dbg_stat", [128, 160], F32, kind="ExternalOutput")
        dbg2 = pr.nc.dram_tensor("dbg_kslb", [128, 2048], BF16, kind="ExternalOutput")
        dbg3 = pr.nc.dram_tensor("dbg_rows", [8, 1024], F32, kind="ExternalOutput")
        dma(ctx, "sync", dbg1.ap(), pr.STAT[:, :], slot_sem=pr.s_store)
        dma(ctx, "sync", dbg2.ap(), pr.KSLB[:, 0:2048], slot_sem=pr.s_store)
        dma(ctx, "sync", dbg3.ap(), pr.ROWS[0:8, :], slot_sem=pr.s_store)
        ctx.barrier()

    def extra(oc, th, psum, cnt):
        return ctx.emit("tensor",
                        lambda e, _p=psum[:, :],
                               _t=pr.AUX[:, 4096 + oc * 128: 4096 + oc * 128 + 128],
                               _r=pr.KSLB[:, th * 512:(th + 1) * 512]:
                        e.matmul(_p, _t, _r, start=False, stop=True),
                        inc=ctx.es["tensor"])
    stream_proj(pr, "f1_WbT", (128, NCH, NCH, 128), NCH, NCH, y_rhs(pr),
                delta_writer(pr, pr.BIG, bias=pr.biasC["f1_bias"]),
                extra_accum=extra)
    ctx.barrier()



def k_writer(pr, kv_loc, rot):
    ctx = pr.ctx
    st = {"ktmp_free": {0: [], 1: []}, "sq_free": {0: [], 1: []},
          "krot_free": [], "scr_free": []}

    def wr(oc, th, psum, dep):
        ktmp = pr.AUX[:, th * 512: th * 512 + 512]
        sq = pr.AUX[:, 1024 + th * 512: 1024 + th * 512 + 512]
        c1 = ctx.emit("scalar", lambda e, _d=ktmp, _p=psum[:, :]: e.copy(_d, _p),
                      waits=[(ctx.es["tensor"], dep)] + st["ktmp_free"][th],
                      inc=ctx.es["scalar"])
        c2 = ctx.emit("scalar", lambda e, _d=sq, _p=psum[:, :]:
                      e.activation(_d, _p, AF.Square),
                      waits=st["sq_free"][th], inc=ctx.es["scalar"])
        bank = 6 + th
        cm = ctx.emit("tensor",
                      lambda e, _p=pr.PS[bank][0:1, :], _r=sq:
                      e.matmul(_p, pr.onescol_b[:, 0:1], _r, start=True, stop=True),
                      waits=[(ctx.es["scalar"], c2)] + pr.bank_free[bank],
                      inc=ctx.es["tensor"])
        st["sq_free"][th] = [(ctx.es["tensor"], cm)]
        cr = ctx.emit("vector",
                      lambda e, _d=pr.KNM[0:1, oc * 2 + th: oc * 2 + th + 1],
                             _p=pr.PS[bank][0:1, :]:
                      e.tensor_reduce(_d, _p, AX.X, ALU.max),
                      waits=[(ctx.es["tensor"], cm)], inc=ctx.es["vector"])
        pr.bank_free[bank] = [(ctx.es["vector"], cr)]
        if th == 0:
            st["c1a"] = c1
            return (ctx.es["scalar"], c1)
        if not rot:
            cs = dma(ctx, "sync", kv_loc.ap()[oc * 128:(oc + 1) * 128, :],
                     pr.AUX[:, 0:1024], slot_sem=pr.s_store,
                     waits=[(ctx.es["scalar"], max(c1, st["c1a"]))])
            st["ktmp_free"][0] = [(pr.s_store, cs)]
            st["ktmp_free"][1] = [(pr.s_store, cs)]
            return (ctx.es["scalar"], c1)
        lastv = None
        kt_fr = {0: [], 1: []}
        for b in range(2):
            cmm = ctx.emit("tensor",
                           lambda e, _r=pr.AUX[:, b * 512: b * 512 + 512]:
                           e.matmul(pr.PS[5][:, :], pr.rotPT[:, :], _r,
                                    start=True, stop=True),
                           waits=[(ctx.es["scalar"], max(c1, st["c1a"]))] +
                                 pr.bank_free[5],
                           inc=ctx.es["tensor"])
            cg = ctx.emit("gpsimd",
                          lambda e, _a=pr.AUX[:, b * 512: b * 512 + 512]:
                          e.tensor_tensor(pr.SCR[0][:, :], _a, pr.cosT[:, :], ALU.mult),
                          waits=st["scr_free"] + [(ctx.es["scalar"], max(c1, st["c1a"]))],
                          inc=ctx.es["gpsimd"])
            cv2 = ctx.emit("vector",
                           lambda e: e.tensor_tensor(pr.SCR[1][:, :], pr.PS[5][:, :],
                                                     pr.sinT[:, :], ALU.mult),
                           waits=[(ctx.es["tensor"], cmm)] + st["scr_free"],
                           inc=ctx.es["vector"])
            pr.bank_free[5] = [(ctx.es["vector"], cv2)]
            lastv = ctx.emit("vector",
                             lambda e, _d=pr.AUX[:, 2048 + b * 512: 2048 + b * 512 + 512]:
                             e.tensor_tensor(_d, pr.SCR[0][:, :], pr.SCR[1][:, :],
                                             ALU.add),
                             waits=[(ctx.es["gpsimd"], cg), (ctx.es["vector"], cv2)] +
                                   st["krot_free"],
                             inc=ctx.es["vector"])
            st["scr_free"] = [(ctx.es["vector"], lastv)]
            kt_fr[b] = [(ctx.es["tensor"], cmm), (ctx.es["gpsimd"], cg)]
        cs = dma(ctx, "sync", kv_loc.ap()[oc * 128:(oc + 1) * 128, :],
                 pr.AUX[:, 2048:3072], slot_sem=pr.s_store,
                 waits=[(ctx.es["vector"], lastv)])
        st["krot_free"] = [(pr.s_store, cs)]
        st["ktmp_free"][0] = kt_fr[0]
        st["ktmp_free"][1] = kt_fr[1]
        return (ctx.es["vector"], lastv)
    return wr


def v_writer(pr, kv_loc, row_base):
    ctx = pr.ctx
    st = {"aux_free": []}

    def wr(oc, th, psum, dep):
        vst = pr.AUX[:, 3072 + th * 512: 3072 + th * 512 + 512]
        c1 = ctx.emit("scalar", lambda e, _d=vst, _p=psum[:, :]: e.copy(_d, _p),
                      waits=[(ctx.es["tensor"], dep)] + st["aux_free"],
                      inc=ctx.es["scalar"])
        if th == 0:
            st["c1a"] = c1
        else:
            cs = dma(ctx, "sync",
                     kv_loc.ap()[row_base + oc * 128: row_base + (oc + 1) * 128, :],
                     pr.AUX[:, 3072:4096], slot_sem=pr.s_store,
                     waits=[(ctx.es["scalar"], max(c1, st["c1a"]))])
            st["aux_free"] = [(pr.s_store, cs)]
        return (ctx.es["scalar"], c1)
    return wr



def attn_units(pr, kv_all, n_kt, k_rows_fn, v_rows_fn, q_cols_fn, mtab_fn,
               knm_idx_fn, out_cols_fn, scale, units, group_fn=None,
               k_ready=(), v_ready=()):
    ctx = pr.ctx
    sem_k = ctx.new_sem("atk")
    sem_v = ctx.new_sem("atv")
    n_sl = (n_kt + 3) // 4
    LA = 6 if mtab_fn is None else 3
    SB = [0, 1, 6, 7]
    n_pt = 10 if mtab_fn is None else 4
    ctx.emit("vector", lambda e: e.tensor_scalar(pr.KNM[0:1, 64:128],
                                                 pr.KNM[0:1, 0:64], 0.5, None, ALU.mult),
             waits=[(ctx.es["vector"], ctx.vN())] + list(k_ready),
             inc=ctx.es["vector"])
    bias_done = []
    ptb0_free = []
    for u_i, u in enumerate(units):
        qc = q_cols_fn(u)
        csq = ctx.emit("scalar",
                       lambda e, _q=qc: e.activation(pr.PTB[0][:, :], _q, AF.Square),
                       waits=ptb0_free, inc=ctx.es["scalar"])
        cqn = ctx.emit("tensor",
                       lambda e: e.matmul(pr.PS[4][0:1, :], pr.onescol_b[:, 0:1],
                                          pr.PTB[0][:, :], start=True, stop=True),
                       waits=[(ctx.es["scalar"], csq)] + pr.bank_free[4],
                       inc=ctx.es["tensor"])
        ptb0_free = [(ctx.es["tensor"], cqn)]
        c = ctx.emit("vector",
                     lambda e: e.tensor_reduce(pr.ROWS[0:1, 512:513], pr.PS[4][0:1, :],
                                               AX.X, ALU.max),
                     waits=[(ctx.es["tensor"], cqn), (ctx.es["vector"], ctx.vN())],
                     inc=ctx.es["vector"])
        ki = knm_idx_fn(u)
        c = ctx.emit("vector",
                     lambda e: e.tensor_scalar(pr.ROWS[0:1, 512:513],
                                               pr.ROWS[0:1, 512:513], 0.5, None, ALU.mult),
                     waits=[(ctx.es["vector"], c)], inc=ctx.es["vector"])
        c = ctx.emit("vector",
                     lambda e, _k=pr.KNM[0:1, 64 + ki: 64 + ki + 1]:
                     e.tensor_scalar(pr.ROWS[0:1, 512:513], pr.ROWS[0:1, 512:513],
                                     _k, None, ALU.add),
                     waits=[(ctx.es["vector"], c)], inc=ctx.es["vector"])
        cbu = ctx.emit("vector",
                       lambda e, _s=-scale:
                       e.tensor_scalar(pr.ROWS[0:1, 512:513], pr.ROWS[0:1, 512:513],
                                       _s, None, ALU.mult),
                       waits=[(ctx.es["vector"], c)], inc=ctx.es["vector"])
        cbm = ctx.emit("tensor",
                       lambda e: e.matmul(pr.PS[4][:, 0:1], pr.ones_f[0:1, :],
                                          pr.ROWS[0:1, 512:513], start=True, stop=True),
                       waits=[(ctx.es["vector"], cbu)], inc=ctx.es["tensor"])
        cbe = ctx.emit("vector",
                       lambda e, _d=pr.STAT[:, u_i: u_i + 1]:
                       e.tensor_copy(_d, pr.PS[4][:, 0:1]),
                       waits=[(ctx.es["tensor"], cbm)], inc=ctx.es["vector"])
        pr.bank_free[4] = [(ctx.es["vector"], cbe)]
        bias_done.append((ctx.es["vector"], cbe))
    scr_free = {i: [] for i in range(3)}
    ptb_free = {i: [] for i in range(n_pt)}
    ks_mm = {}
    vt_pv = {}
    ck_all = cv_all = 0
    for gi, u in enumerate(units):
        new_group = (group_fn is None or gi == 0 or
                     group_fn(u) != group_fn(units[gi - 1]))
        if new_group:
            for sl in range(n_sl):
                def mk_k(e, _u=u, _sl=sl):
                    r, c0 = k_rows_fn(_u, _sl)
                    kt_ = kv_all[0] if isinstance(kv_all, tuple) else kv_all
                    return e.dma_start(out=pr.KSLB[:, _sl * 512:(_sl + 1) * 512],
                                       in_=kt_.ap()[bass.ds(r, 128), c0:c0 + 512])
                ctx.emit("sync", mk_k,
                         waits=[(ctx.es["tensor"], ks_mm.get(sl, 0))] + list(k_ready),
                         inc=sem_k, inc_by=16)
            for j in range(n_kt):
                def mk_v(e, _u=u, _j=j):
                    r, c0 = v_rows_fn(_u, _j)
                    vt_ = kv_all[1] if isinstance(kv_all, tuple) else kv_all
                    return e.dma_start(out=pr.VTB[:, _j * 128:(_j + 1) * 128],
                                       in_=vt_.ap()[bass.ds(r, 128), c0:c0 + 128],
                                       transpose=True)
                ctx.emit("sync", mk_v,
                         waits=[(ctx.es["tensor"], vt_pv.get(j, 0))] + list(v_ready),
                         inc=sem_v, inc_by=16)
            ck_all = sem_k.n
            cv_all = sem_v.n
        qc = q_cols_fn(u)
        bcol = pr.STAT[:, gi: gi + 1]
        bwait = bias_done[gi]
        otb = 2 if gi % 2 == 0 else 5
        exp_of = {}
        pv_last = None
        cs_last = None

        def pt_ap(j):
            sl_ = j % n_pt
            return pr.AUX[:, sl_ * 512: sl_ * 512 + 512]

        def emit_pv(j):
            nonlocal pv_last, cs_last
            pv_last = ctx.emit("tensor",
                               lambda e, _t=pr.VTB[:, j * 128:(j + 1) * 128],
                                      _r=pt_ap(j), _s=(j == 0), _e=(j == n_kt - 1),
                                      _o=pr.PS[otb][:, :]:
                               e.matmul(_o, _t, _r, start=_s, stop=_e),
                               waits=[exp_of[j]] +
                                     ([(sem_v, cv_all)] + pr.bank_free[otb]
                                      if j == 0 else []),
                               inc=ctx.es["tensor"])
            cs_last = ctx.emit("tensor",
                               lambda e, _r=pt_ap(j), _s=(j == 0), _e=(j == n_kt - 1):
                               e.matmul(pr.PS[3][0:1, :], pr.onescol_b[:, 0:1], _r,
                                        start=_s, stop=_e),
                               waits=(pr.bank_free[3] if j == 0 else []),
                               inc=ctx.es["tensor"])
            ptb_free[j % n_pt] = [(ctx.es["tensor"], cs_last)]
            vt_pv[j] = pv_last

        for j in range(n_kt):
            bk = SB[j % 4]
            cmm = ctx.emit("tensor",
                           lambda e, _pp=pr.PS[bk][:, :],
                                  _t=pr.KSLB[:, (j // 4) * 512 + (j % 4) * 128:
                                             (j // 4) * 512 + (j % 4) * 128 + 128],
                                  _q=qc:
                           e.matmul(_pp, _t, _q, start=True, stop=True),
                           waits=([(sem_k, ck_all)] if j == 0 else []) + pr.bank_free[bk],
                           inc=ctx.es["tensor"])
            ks_mm[j // 4] = cmm
            mcols = mtab_fn(u, j) if mtab_fn is not None else None
            if mcols is not None:
                sslot = j % 3
                cadd = ctx.emit("vector",
                                lambda e, _d=pr.SCR[sslot][:, :], _p=pr.PS[bk][:, :],
                                       _m=mcols:
                                e.tensor_tensor(_d, _p, _m, ALU.add),
                                waits=[(ctx.es["tensor"], cmm)] + scr_free[sslot],
                                inc=ctx.es["vector"])
                pr.bank_free[bk] = [(ctx.es["vector"], cadd)]
                cexp = ctx.emit("scalar",
                                lambda e, _d=pt_ap(j), _s=pr.SCR[sslot][:, :], _b=bcol:
                                e.activation(_d, _s, AF.Exp, bias=_b, scale=scale),
                                waits=[(ctx.es["vector"], cadd)] +
                                      ([bwait] if j == 0 else []) + ptb_free[j % n_pt],
                                inc=ctx.es["scalar"])
                scr_free[sslot] = [(ctx.es["scalar"], cexp)]
            else:
                cexp = ctx.emit("scalar",
                                lambda e, _d=pt_ap(j), _p=pr.PS[bk][:, :], _b=bcol:
                                e.activation(_d, _p, AF.Exp, bias=_b, scale=scale),
                                waits=[(ctx.es["tensor"], cmm)] +
                                      ([bwait] if j == 0 else []) + ptb_free[j % n_pt],
                                inc=ctx.es["scalar"])
                pr.bank_free[bk] = [(ctx.es["scalar"], cexp)]
            exp_of[j] = (ctx.es["scalar"], cexp)
            if j >= LA:
                emit_pv(j - LA)
        for j in range(max(0, n_kt - LA), n_kt):
            emit_pv(j)
        ce1 = ctx.emit("vector",
                       lambda e: e.tensor_copy(pr.ROWS[0:1, 0:512], pr.PS[3][0:1, :]),
                       waits=[(ctx.es["tensor"], cs_last)], inc=ctx.es["vector"])
        cb2 = ctx.emit("tensor",
                       lambda e: e.matmul(pr.PS[3][:, :], pr.ones_f[0:1, :],
                                          pr.ROWS[0:1, 0:512], start=True, stop=True),
                       waits=[(ctx.es["vector"], ce1)], inc=ctx.es["tensor"])
        ce2 = ctx.emit("vector",
                       lambda e: e.tensor_copy(pr.SCR[0][:, :], pr.PS[3][:, :]),
                       waits=[(ctx.es["tensor"], cb2)] + scr_free[0],
                       inc=ctx.es["vector"])
        pr.bank_free[3] = [(ctx.es["vector"], ce2)]
        cr1 = ctx.emit("vector",
                       lambda e: e.reciprocal(pr.SCR[1][:, :], pr.SCR[0][:, :]),
                       waits=[(ctx.es["vector"], ce2)] + scr_free[1],
                       inc=ctx.es["vector"])
        cno = ctx.emit("vector",
                       lambda e, _d=out_cols_fn(u), _o=pr.PS[otb][:, :]:
                       e.tensor_tensor(_d, _o, pr.SCR[1][:, :], ALU.mult),
                       waits=[(ctx.es["vector"], cr1)], inc=ctx.es["vector"])
        pr.bank_free[otb] = [(ctx.es["vector"], cno)]
        scr_free[0] = [(ctx.es["vector"], cr1)]
        scr_free[1] = [(ctx.es["vector"], cno)]



def ph_b0(pr):
    ctx = pr.ctx
    kv_loc = pr.dint("b0_kvloc", [8192, 1024], BF16)
    kv_allK = pr.dint("b0_kvallk", [32768, 1024], BF16)
    kv_allV = pr.dint("b0_kvallv", [32768, 1024], BF16)
    kv_win = pr.dint("b0_kvwin", [24576, 1024], BF16)
    knm_in = pr.dint("b0_knin", [1, 64], F32)
    knm_out = pr.dint("b0_knout", [1, 64], F32)
    ctx.need_pid.add("sync")

    stream_proj(pr, "b0_WqT", (128, NCH, NCH, 128), NCH, NCH, y_rhs(pr),
                delta_writer(pr, pr.BIG))
    ctx.barrier()
    scr_free = []
    for ch in range(NCH):
        for b in range(2):
            bk = 6 + (2 * ch + b) % 2
            qcols = pr.BIG[:, ch * 1024 + b * 512: ch * 1024 + b * 512 + 512]
            cmm = ctx.emit("tensor",
                           lambda e, _p=pr.PS[bk][:, :], _q=qcols:
                           e.matmul(_p, pr.rotPT[:, :], _q, start=True, stop=True),
                           waits=pr.bank_free[bk], inc=ctx.es["tensor"])
            cg = ctx.emit("gpsimd",
                          lambda e, _q=qcols:
                          e.tensor_tensor(pr.SCR[0][:, :], _q, pr.cosT[:, :], ALU.mult),
                          waits=scr_free, inc=ctx.es["gpsimd"])
            cv2 = ctx.emit("vector",
                           lambda e, _p=pr.PS[bk][:, :]:
                           e.tensor_tensor(pr.SCR[1][:, :], _p, pr.sinT[:, :], ALU.mult),
                           waits=[(ctx.es["tensor"], cmm)] + scr_free,
                           inc=ctx.es["vector"])
            pr.bank_free[bk] = [(ctx.es["vector"], cv2)]
            cvw = ctx.emit("vector",
                           lambda e, _d=qcols:
                           e.tensor_tensor(_d, pr.SCR[0][:, :], pr.SCR[1][:, :], ALU.add),
                           waits=[(ctx.es["gpsimd"], cg)], inc=ctx.es["vector"])
            scr_free = [(ctx.es["vector"], cvw)]
    ctx.barrier()
    stream_proj(pr, "b0_WkT", (128, NCH, NCH, 128), NCH, NCH, y_rhs(pr),
                k_writer(pr, kv_loc, rot=True), bank_n=4)
    ctx.barrier()
    dma(ctx, "sync", knm_in.ap(), pr.KNM[0:1, 0:64], slot_sem=pr.s_store)
    sc1 = ctx.new_sem("b0_agk")
    ctx.emit("gpsimd",
             lambda e: e.collective_compute(
                 "AllGather", ALU.bypass, replica_groups=[list(range(N_CORES))],
                 ins=[kv_loc.ap()[0:4096, :]], outs=[kv_allK.ap()]),
             waits=[(pr.s_store, pr.s_store.n)], inc=sc1)
    sc2 = ctx.new_sem("b0_knm")
    ctx.emit("gpsimd",
             lambda e: e.collective_compute(
                 "AllReduce", ALU.max, replica_groups=[list(range(N_CORES))],
                 ins=[knm_in.ap()], outs=[knm_out.ap()]),
             inc=sc2)
    sld = ctx.new_sem("b0_knl")
    dma(ctx, "gpsimd", pr.KNM[0:1, 0:64], knm_out.ap(), slot_sem=sld, waits=[(sc2, 1)])
    kv8k = kv_allK.ap().rearrange("(c r) w -> c r w", c=8)
    swin = ctx.new_sem("b0_wink")
    for dlt in range(3):
        def mk_win(e, _d=dlt):
            src = kv8k[bass.ds((ctx.pid["sync"] + (_d + 6)) % 8, 1), :, :]
            return e.dma_start(out=kv_win.ap()[_d * 4096:(_d + 1) * 4096, :],
                               in_=src.rearrange("o r w -> (o r) w"))
        ctx.emit("sync", mk_win, waits=[(sc1, 1)] if dlt == 0 else [],
                 inc=swin, inc_by=16)
    for j in range(12):
        for i in range(4):
            d = i + 8 - j
            cols = pr.AUX[:, 2048 + j * 512 + i * 128: 2048 + j * 512 + i * 128 + 128]
            if d == 0:
                ctx.emit("gpsimd", lambda e, _d=cols: e.tensor_copy(_d, pr.mask_diag[:, :]),
                         inc=ctx.es["gpsimd"])
            elif d == 8:
                ctx.emit("gpsimd", lambda e, _d=cols: e.tensor_copy(_d, pr.mask_tail[:, :]),
                         inc=ctx.es["gpsimd"])
            elif 1 <= d <= 7:
                ctx.emit("gpsimd", lambda e, _d=cols: e.memset(_d, 0.0),
                         inc=ctx.es["gpsimd"])
            else:
                ctx.emit("gpsimd", lambda e, _d=cols: e.memset(_d, NEG),
                         inc=ctx.es["gpsimd"])
        ctx.emit("gpsimd",
                 lambda e, _d=pr.AUX[:, 2048 + j * 512: 2048 + j * 512 + 512],
                        _v=pr.vcolB[:, j: j + 1]:
                 e.tensor_scalar(_d, _d, _v, None, ALU.add),
                 inc=ctx.es["gpsimd"])
    stream_proj(pr, "b0_WvT", (128, NCH, NCH, 128), NCH, NCH, y_rhs(pr),
                v_writer(pr, kv_loc, 4096), bank_n=4)
    ctx.barrier()
    scv = ctx.new_sem("b0_agv")
    ctx.emit("gpsimd",
             lambda e: e.collective_compute(
                 "AllGather", ALU.bypass, replica_groups=[list(range(N_CORES))],
                 ins=[kv_loc.ap()[4096:8192, :]], outs=[kv_allV.ap()]),
             waits=[(pr.s_store, pr.s_store.n)], inc=scv)
    kv8v = kv_allV.ap().rearrange("(c r) w -> c r w", c=8)
    for dlt in range(3):
        def mk_winv(e, _d=dlt):
            src = kv8v[bass.ds((ctx.pid["sync"] + (_d + 6)) % 8, 1), :, :]
            return e.dma_start(out=kv_win.ap()[12288 + _d * 4096: 12288 + (_d + 1) * 4096, :],
                               in_=src.rearrange("o r w -> (o r) w"))
        ctx.emit("sync", mk_winv, waits=[(scv, 1)] if dlt == 0 else [],
                 inc=swin, inc_by=16)
    win_k_ready = [(swin, 48), (sld, 16)]
    win_v_ready = [(swin, 96)]

    units = [(b, h) for b in range(2) for h in range(NCH)]

    def k_rows(u, sl):
        return (sl * 4096 + u[1] * 128, u[0] * 512)

    def v_rows(u, j):
        return (12288 + (j // 4) * 4096 + u[1] * 128,
                u[0] * 512 + (j % 4) * 128)

    attn_units(pr, kv_win, 12, k_rows, v_rows,
               q_cols_fn=lambda u: pr.BIG[:, u[1] * 1024 + u[0] * 512:
                                          u[1] * 1024 + u[0] * 512 + 512],
               mtab_fn=lambda u, j: pr.AUX[:, 2048 + j * 512: 2048 + j * 512 + 512],
               knm_idx_fn=lambda u: u[1] * 2 + u[0],
               out_cols_fn=lambda u: pr.BIG[:, u[1] * 1024 + u[0] * 512:
                                            u[1] * 1024 + u[0] * 512 + 512],
               scale=RS128, units=units,
               k_ready=win_k_ready, v_ready=win_v_ready)
    ctx.barrier()
    stream_proj(pr, "b0_WoT", (128, NCH, NCH, 128), NCH, NCH, big_rhs(pr),
                delta_writer(pr, pr.Y))
    ctx.barrier()



def ph_a1(pr):
    ctx = pr.ctx
    kv_loc = pr.dint("a1_kvloc", [2048, 1024], BF16)
    kv_allK = pr.dint("a1_kvallk", [8192, 1024], BF16)
    kv_allV = pr.dint("a1_kvallv", [8192, 1024], BF16)
    knm_in = pr.dint("a1_knin", [1, 64], F32)
    knm_out = pr.dint("a1_knout", [1, 64], F32)

    stream_proj(pr, "a1_WqT", (128, NCH, NCH, 128), NCH, NCH, y_rhs(pr),
                delta_writer(pr, pr.BIG))
    ctx.barrier()
    stream_proj(pr, "a1_WkT", (128, 8, NCH, 128), 8, NCH, y_rhs(pr),
                k_writer(pr, kv_loc, rot=False), bank_n=4)
    ctx.barrier()
    dma(ctx, "sync", knm_in.ap(), pr.KNM[0:1, 0:64], slot_sem=pr.s_store)
    sc1 = ctx.new_sem("a1_agk")
    ctx.emit("gpsimd",
             lambda e: e.collective_compute(
                 "AllGather", ALU.bypass, replica_groups=[list(range(N_CORES))],
                 ins=[kv_loc.ap()[0:1024, :]], outs=[kv_allK.ap()]),
             waits=[(pr.s_store, pr.s_store.n)], inc=sc1)
    sc2 = ctx.new_sem("a1_knm")
    ctx.emit("gpsimd",
             lambda e: e.collective_compute(
                 "AllReduce", ALU.max, replica_groups=[list(range(N_CORES))],
                 ins=[knm_in.ap()], outs=[knm_out.ap()]),
             inc=sc2)
    sld = ctx.new_sem("a1_knl")
    dma(ctx, "gpsimd", pr.KNM[0:1, 0:64], knm_out.ap(), slot_sem=sld, waits=[(sc2, 1)])
    stream_proj(pr, "a1_WvT", (128, 8, NCH, 128), 8, NCH, y_rhs(pr),
                v_writer(pr, kv_loc, 1024), bank_n=4)
    ctx.barrier()
    scv = ctx.new_sem("a1_agv")
    ctx.emit("gpsimd",
             lambda e: e.collective_compute(
                 "AllGather", ALU.bypass, replica_groups=[list(range(N_CORES))],
                 ins=[kv_loc.ap()[1024:2048, :]], outs=[kv_allV.ap()]),
             waits=[(pr.s_store, pr.s_store.n)], inc=scv)

    units = [(b, h) for b in range(2) for h in range(NCH)]
    attn_units(pr, (kv_allK, kv_allV), 32,
               k_rows_fn=lambda u, sl: (sl * 1024 + (u[1] // 4) * 128, u[0] * 512),
               v_rows_fn=lambda u, j: ((j // 4) * 1024 + (u[1] // 4) * 128,
                                       u[0] * 512 + (j % 4) * 128),
               q_cols_fn=lambda u: pr.BIG[:, u[1] * 1024 + u[0] * 512:
                                          u[1] * 1024 + u[0] * 512 + 512],
               mtab_fn=None,
               knm_idx_fn=lambda u: (u[1] // 4) * 2 + u[0],
               out_cols_fn=lambda u: pr.BIG[:, u[1] * 1024 + u[0] * 512:
                                            u[1] * 1024 + u[0] * 512 + 512],
               scale=RS128, units=units, group_fn=lambda u: (u[0], u[1] // 4),
               k_ready=[(sc1, 1), (sld, 16)], v_ready=[(scv, 1)])
    ctx.barrier()
    if DEBUG_TAP == "a1_attn":
        dbg = pr.nc.dram_tensor("dbg_big", [128, NCH * 1024], BF16,
                                kind="ExternalOutput")
        dma(ctx, "sync", dbg.ap(), pr.BIG[:, :], slot_sem=pr.s_store)
        ctx.barrier()
    if DEBUG_TAP == "a1_q":
        dbg = pr.nc.dram_tensor("dbg_big", [128, NCH * 1024], BF16,
                                kind="ExternalOutput")
        dma(ctx, "sync", dbg.ap(), pr.BIG[:, :], slot_sem=pr.s_store)
        ctx.barrier()
    stream_proj(pr, "a1_WoT", (128, NCH, NCH, 128), NCH, NCH, big_rhs(pr),
                delta_writer(pr, pr.Y))
    ctx.barrier()



def ph_b1(pr):
    ctx = pr.ctx
    hl_loc = pr.dint("b1_hloc", [128, 128], BF16)
    hl_all = pr.dint("b1_hall", [1024, 128], BF16)
    ctx.need_pid.add("sync")
    ecols = [(1,), (512,), (515,), (1026,)]
    cvh = None
    for e_i, (off,) in enumerate(ecols):
        src = pr.Y[:, off: off + (NCH - 1) * YST + 1: YST]
        cvh = ctx.emit("vector",
                       lambda e, _d=pr.VTB[:, 128 + e_i * 32: 128 + (e_i + 1) * 32], _s=src:
                       e.tensor_copy(_d, _s), inc=ctx.es["vector"])
    dma(ctx, "sync", hl_loc.ap(), pr.VTB[:, 128:256], slot_sem=pr.s_store,
        waits=[(ctx.es["vector"], cvh)])
    sc1 = ctx.new_sem("b1_ag")
    ctx.emit("gpsimd",
             lambda e: e.collective_compute(
                 "AllGather", ALU.bypass, replica_groups=[list(range(N_CORES))],
                 ins=[hl_loc.ap()], outs=[hl_all.ap()]),
             waits=[(pr.s_store, pr.s_store.n)], inc=sc1)
    sem_h = ctx.new_sem("b1_hl")
    ch1 = ctx.emit("sync",
                   lambda e: e.dma_start(
                       out=pr.VTB[:, 128:256],
                       in_=hl_all.ap()[bass.ds((ctx.pid["sync"] + 7) % 8 * 128, 128), :]),
                   waits=[(sc1, 1)], inc=sem_h, inc_by=16)
    ch2 = ctx.emit("sync",
                   lambda e: e.dma_start(
                       out=pr.VTB[:, 256:384],
                       in_=hl_all.ap()[bass.ds((ctx.pid["sync"] + 1) % 8 * 128, 128), :]),
                   inc=sem_h, inc_by=16)
    cl = ctx.emit("gpsimd",
                  lambda e: e.tensor_scalar(pr.VTB[:, 128:256], pr.VTB[:, 128:256],
                                            pr.hvB[:, 0:1], None, ALU.mult),
                  waits=[(sem_h, ch1)], inc=ctx.es["gpsimd"])
    cr = ctx.emit("gpsimd",
                  lambda e: e.tensor_scalar(pr.VTB[:, 256:384], pr.VTB[:, 256:384],
                                            pr.hvB[:, 1:2], None, ALU.mult),
                  waits=[(sem_h, ch2)], inc=ctx.es["gpsimd"])
    fills = [(128, 1, 0), (128, 3, 514), (256, 0, 513), (256, 2, 1027)]
    for bo, eidx, coloff in fills:
        ctx.emit("vector",
                 lambda e, _s=pr.VTB[:, bo + eidx * 32: bo + (eidx + 1) * 32],
                        _d=pr.Y[:, coloff: coloff + (NCH - 1) * YST + 1: YST]:
                 e.tensor_copy(_d, _s),
                 waits=[(ctx.es["gpsimd"], max(cl, cr))], inc=ctx.es["vector"])
    ctx.barrier()
    conv = pr.din("b1_conv", (3, 128, 24, NCH, 128))
    for oc in range(24):
        banks = [(oc % 2) * 2, (oc % 2) * 2 + 1]
        last = None
        for k in range(3):
            sl = pr.r_wg.next()
            c = dma(ctx, "gpsimd", sl.ap[:, 0: NCH * 128],
                    conv.ap()[k][:, oc].rearrange("p c j -> p (c j)"),
                    slot_sem=sl.sem, waits=sl.free_waits)
            for cc in range(NCH):
                tile = sl.ap[:, cc * 128:(cc + 1) * 128]
                for b in range(2):
                    bw = []
                    if cc == 0 and b == 0:
                        bw = [(sl.sem, c)]
                        if k == 0:
                            bw = bw + list(pr.bank_free[banks[0]]) + \
                                list(pr.bank_free[banks[1]])
                    rhs = pr.Y[:, cc * YST + k + b * 514: cc * YST + k + b * 514 + 512]
                    last = ctx.emit(
                        "tensor",
                        lambda e, _p=pr.PS[banks[b]][:, :], _t=tile, _r=rhs,
                               _s=(k == 0 and cc == 0), _e=(k == 2 and cc == NCH - 1):
                        e.matmul(_p, _t, _r, start=_s, stop=_e),
                        waits=bw, inc=ctx.es["tensor"])
            sl.free_waits = [(ctx.es["tensor"], last)]
        for b in range(2):
            dcols = pr.BIG[:, oc * 1024 + b * 512: oc * 1024 + b * 512 + 512]
            ce = ctx.emit("vector",
                          lambda e, _d=dcols, _p=pr.PS[banks[b]][:, :],
                                 _bc=pr.biasC["b1_convb"][:, oc:oc + 1]:
                          e.tensor_scalar(_d, _p, _bc, None, ALU.add),
                          waits=[(ctx.es["tensor"], last)], inc=ctx.es["vector"])
            pr.bank_free[banks[b]] = [(ctx.es["vector"], ce)]
    ctx.barrier()
    def qkv_rhs(base):
        return lambda cc, th: pr.BIG[:, (base + cc) * 1024 + th * 512:
                                     (base + cc) * 1024 + th * 512 + 512]
    stream_proj(pr, "b1_wq", (128, 8, 8, 128), 8, 8, qkv_rhs(0),
                delta_writer(pr, pr.Y, bias=pr.biasC["b1_bq"]), bank_n=4)
    stream_proj(pr, "b1_wk", (128, 8, 8, 128), 8, 8, qkv_rhs(8),
                delta_writer(pr, pr.Y, bias=pr.biasC["b1_bk"], col_off=8192),
                bank_n=4, bank_base=4)
    ctx.barrier()
    wv = pr.din("b1_wv_rhs", (128, 8, 1024))
    bvr = pr.din("b1_bv_row", (1, 1024))
    semv = ctx.new_sem("b1_wv")
    dma(ctx, "sync", pr.ROWS[0:1, :], bvr.ap(), slot_sem=semv)
    dma(ctx, "gpsimd", pr.AUX[:, 0:8192], wv.ap().rearrange("p c j -> p (c j)"),
        slot_sem=semv)
    prev = 0
    for th in range(2):
        cb = ctx.emit("tensor",
                      lambda e, _r=pr.ROWS[0:1, th * 512:(th + 1) * 512]:
                      e.matmul(pr.PS[6][:, :], pr.ones_f[0:1, :], _r,
                               start=True, stop=True),
                      waits=(pr.bank_free[6] + [(semv, 32)] if th == 0 else
                             [(ctx.es["vector"], prev)]),
                      inc=ctx.es["tensor"])
        prev = ctx.emit("vector",
                        lambda e, _d=pr.NB[:, th * 512:(th + 1) * 512]:
                        e.tensor_copy(_d, pr.PS[6][:, :]),
                        waits=[(ctx.es["tensor"], cb)], inc=ctx.es["vector"])
    cbv_done = prev
    pr.bank_free[6] = [(ctx.es["vector"], cbv_done)]
    for tt in range(8):
        banks = [(tt % 2) * 2, (tt % 2) * 2 + 1]
        last = None
        for wc in range(8):
            tile = pr.BIG[:, (16 + wc) * 1024 + tt * 128: (16 + wc) * 1024 + tt * 128 + 128]
            for th in range(2):
                bw = []
                if wc == 0:
                    bw = list(pr.bank_free[banks[th]])
                    if th == 0 and tt == 0:
                        bw = bw + [(semv, 32)]
                last = ctx.emit(
                    "tensor",
                    lambda e, _p=pr.PS[banks[th]][:, :], _t=tile,
                           _r=pr.AUX[:, wc * 1024 + th * 512: wc * 1024 + th * 512 + 512],
                           _s=(wc == 0), _e=(wc == 7):
                    e.matmul(_p, _t, _r, start=_s, stop=_e),
                    waits=bw, inc=ctx.es["tensor"])
        for th in range(2):
            dcols = pr.Y[:, 16384 + tt * 1024 + th * 512:
                         16384 + tt * 1024 + th * 512 + 512]
            cvv = ctx.emit("vector",
                           lambda e, _d=dcols, _p=pr.PS[banks[th]][:, :],
                                  _b=pr.NB[:, th * 512:(th + 1) * 512]:
                           e.tensor_tensor(_d, _p, _b, ALU.add),
                           waits=[(ctx.es["tensor"], last),
                                  (ctx.es["vector"], cbv_done)],
                           inc=ctx.es["vector"])
            pr.bank_free[banks[th]] = [(ctx.es["vector"], cvv)]
    ctx.barrier()
    ctx = pr.ctx
    gcol = pr.STAT[:, 0:1]
    rmax = pr.ROWS[0:1, 512:513]
    rtmp = pr.ROWS[0:1, 513:514]
    cz = ctx.emit("gpsimd", lambda e: e.memset(rmax, 0.0), inc=ctx.es["gpsimd"])
    cprev = 0
    sqf = []
    for idx in range(16):
        for th in range(2):
            ysl = pr.Y[:, idx * 1024 + th * 512: idx * 1024 + th * 512 + 512]
            csq = ctx.emit("scalar",
                           lambda e, _y=ysl: e.activation(pr.PTB[0][:, :], _y,
                                                          AF.Square),
                           waits=sqf, inc=ctx.es["scalar"])
            cqn = ctx.emit("tensor",
                           lambda e: e.matmul(pr.PS[4][0:1, :], pr.onescol_b[:, 0:1],
                                              pr.PTB[0][:, :], start=True, stop=True),
                           waits=[(ctx.es["scalar"], csq)] + pr.bank_free[4],
                           inc=ctx.es["tensor"])
            sqf = [(ctx.es["tensor"], cqn)]
            cr = ctx.emit("vector",
                          lambda e: e.tensor_reduce(rtmp, pr.PS[4][0:1, :], AX.X,
                                                    ALU.max),
                          waits=[(ctx.es["tensor"], cqn), (ctx.es["vector"], cprev),
                                 (ctx.es["gpsimd"], cz)],
                          inc=ctx.es["vector"])
            pr.bank_free[4] = [(ctx.es["vector"], cr)]
            cprev = ctx.emit("vector",
                             lambda e: e.tensor_tensor(rmax, rmax, rtmp, ALU.max),
                             waits=[(ctx.es["vector"], cr)], inc=ctx.es["vector"])
    cbu = ctx.emit("vector",
                   lambda e: e.tensor_scalar(rmax, rmax, -RS64, None, ALU.mult),
                   waits=[(ctx.es["vector"], cprev)], inc=ctx.es["vector"])
    cbm = ctx.emit("tensor",
                   lambda e: e.matmul(pr.PS[4][:, 0:1], pr.ones_f[0:1, :], rmax,
                                      start=True, stop=True),
                   waits=[(ctx.es["vector"], cbu)] + pr.bank_free[4],
                   inc=ctx.es["tensor"])
    cbe = ctx.emit("vector", lambda e: e.tensor_copy(gcol, pr.PS[4][:, 0:1]),
                   waits=[(ctx.es["tensor"], cbm)], inc=ctx.es["vector"])
    pr.bank_free[4] = [(ctx.es["vector"], cbe)]
    bias_ready = (ctx.es["vector"], cbe)
    SBK = [0, 1, 6, 7]
    scr_lo = {i: [] for i in range(3)}
    scr_hi = {i: [] for i in range(3)}
    pt_free = {i: [] for i in range(8)}
    row_free = {0: [], 1: []}
    ui = 0
    for b in range(2):
        for blk in range(4):
            tti = b * 4 + blk
            for h in range(16):
                hc = h // 2
                pb = (h % 2) * 64
                base_q = hc * 1024 + b * 512 + blk * 128
                qt = pr.Y[pb:pb + 64, base_q: base_q + 128]
                kt = pr.Y[pb:pb + 64, 8192 + base_q: 8192 + base_q + 128]
                vt = pr.Y[:, 16384 + tti * 1024 + h * 64:
                          16384 + tti * 1024 + h * 64 + 64]
                sb = SBK[ui % 4]
                otb = 2 if ui % 2 == 0 else 5
                csb = 3 if ui % 2 == 0 else 4
                ptc = pr.AUX[:, (ui % 8) * 128: (ui % 8) * 128 + 128]
                ss = ui % 3
                rrow = pr.ROWS[0:1, (ui % 2) * 128: (ui % 2) * 128 + 128]
                cmm = ctx.emit("tensor",
                               lambda e, _p=pr.PS[sb][:, 0:128], _k=kt, _q=qt:
                               e.matmul(_p, _k, _q, start=True, stop=True),
                               waits=pr.bank_free[sb], inc=ctx.es["tensor"])
                cadd = ctx.emit("vector",
                                lambda e, _d=pr.SCR[ss][:, 0:128],
                                       _p=pr.PS[sb][:, 0:128]:
                                e.tensor_tensor(_d, _p, pr.mask_diag[:, :], ALU.add),
                                waits=[(ctx.es["tensor"], cmm)] + scr_lo[ss],
                                inc=ctx.es["vector"])
                pr.bank_free[sb] = [(ctx.es["vector"], cadd)]
                cexp = ctx.emit("scalar",
                                lambda e, _d=ptc, _s=pr.SCR[ss][:, 0:128]:
                                e.activation(_d, _s, AF.Exp, bias=gcol, scale=RS64),
                                waits=[(ctx.es["vector"], cadd), bias_ready] +
                                      pt_free[ui % 8],
                                inc=ctx.es["scalar"])
                scr_lo[ss] = [(ctx.es["scalar"], cexp)]
                cpv = ctx.emit("tensor",
                               lambda e, _o=pr.PS[otb][0:64, 0:128], _v=vt, _r=ptc:
                               e.matmul(_o, _v, _r, start=True, stop=True),
                               waits=[(ctx.es["scalar"], cexp)] + pr.bank_free[otb],
                               inc=ctx.es["tensor"])
                ccs = ctx.emit("tensor",
                               lambda e, _p=pr.PS[csb][0:1, 0:128], _r=ptc:
                               e.matmul(_p, pr.onescol_b[:, 0:1], _r,
                                        start=True, stop=True),
                               waits=pr.bank_free[csb], inc=ctx.es["tensor"])
                pt_free[ui % 8] = [(ctx.es["tensor"], ccs)]
                ce1 = ctx.emit("vector",
                               lambda e, _r=rrow, _p=pr.PS[csb][0:1, 0:128]:
                               e.tensor_copy(_r, _p),
                               waits=[(ctx.es["tensor"], ccs)] + row_free[ui % 2],
                               inc=ctx.es["vector"])
                cb3 = ctx.emit("tensor",
                               lambda e, _p=pr.PS[csb][:, 0:128], _r=rrow:
                               e.matmul(_p, pr.ones_f[0:1, :], _r,
                                        start=True, stop=True),
                               waits=[(ctx.es["vector"], ce1)], inc=ctx.es["tensor"])
                row_free[ui % 2] = [(ctx.es["tensor"], cb3)]
                cln = ctx.emit("scalar",
                               lambda e, _d=pr.SCR[ss][0:64, 128:256],
                                      _p=pr.PS[csb][0:64, 0:128]:
                               e.activation(_d, _p, AF.Ln),
                               waits=[(ctx.es["tensor"], cb3)] + scr_hi[ss],
                               inc=ctx.es["scalar"])
                pr.bank_free[csb] = [(ctx.es["scalar"], cln)]
                crcp = ctx.emit("scalar",
                                lambda e, _d=pr.SCR[ss][0:64, 256:384],
                                       _s=pr.SCR[ss][0:64, 128:256]:
                                e.activation(_d, _s, AF.Exp, scale=-1.0),
                                inc=ctx.es["scalar"])
                ono = pr.BIG[pb:pb + 64, base_q: base_q + 128]
                cdv = ctx.emit("vector",
                               lambda e, _d=ono, _o=pr.PS[otb][0:64, 0:128],
                                      _r=pr.SCR[ss][0:64, 256:384]:
                               e.tensor_tensor(_d, _o, _r, ALU.mult),
                               waits=[(ctx.es["scalar"], crcp),
                                      (ctx.es["tensor"], cpv)],
                               inc=ctx.es["vector"])
                pr.bank_free[otb] = [(ctx.es["vector"], cdv)]
                scr_hi[ss] = [(ctx.es["vector"], cdv)]
                ui += 1
    ctx.barrier()
    stream_proj(pr, "b1_wo", (128, 8, 8, 128), 8, 8, big_rhs(pr),
                delta_writer(pr, pr.AUX, bias=pr.biasC["b1_bo"]), bank_n=4)
    ctx.barrier()
    stream_proj(pr, "b1_proj", (128, NCH, 8, 128), NCH, 8,
                lambda cc, th: pr.AUX[:, cc * 1024 + th * 512: cc * 1024 + th * 512 + 512],
                delta_writer(pr, pr.Y, bias=pr.biasC["b1_projb"]))
    ctx.barrier()



PHASES = {1: "attn_a", 2: "ffn0", 3: "b0", 4: "ffn0",
          5: "a1", 6: "ffn1", 7: "b1", 8: "ffn1"}


def emit_sublayer(pr, k, first, stats_next):
    kind = PHASES[k]
    if first:
        pr.pass_A(padded=(kind == "b1"))
    else:
        pr.pass_A_tail()
    pr.pass_B(padded=(kind == "b1"))
    if kind == "attn_a":
        ph_attn_a(pr, stats_next)
        return
    if kind == "ffn0":
        ph_ffn0(pr)
        pr.pass_D(pr.BIG, stats_next=stats_next)
    elif kind == "ffn1":
        ph_ffn1(pr)
        pr.pass_D(pr.BIG, stats_next=stats_next)
    elif kind == "b0":
        ph_b0(pr)
        pr.pass_D(pr.Y, stats_next=stats_next)
    elif kind == "a1":
        ph_a1(pr)
        pr.pass_D(pr.Y, stats_next=stats_next)
    elif kind == "b1":
        ph_b1(pr)
        pr.pass_D(pr.Y, stats_next=stats_next)


def build_launch(sublayers, mode):
    nc = bass.Bass("TRN2", target_bir_lowering=False, debug=False)
    with ExitStack() as stack:
        pr = Prog(nc, stack)
        x_in = pr.din("x_in", (NCH, 128, TLOC))
        x_out = nc.dram_tensor("x_out", [NCH, 128, TLOC], F32, kind="ExternalOutput")
        pr.load_consts()
        if mode == "fused" and len(sublayers) > 1:
            xm = pr.dint("x_master", (NCH, 128, TLOC), F32)
            dma(pr.ctx, "sync", xm.ap()[:, :, :], x_in.ap()[:, :, :],
                slot_sem=pr.s_store)
            pr.ctx.barrier()
            for i, k in enumerate(sublayers):
                pr.x_src = xm
                pr.x_dst = xm if i < len(sublayers) - 1 else x_out
                nxt = PHASES[sublayers[i + 1]] if i + 1 < len(sublayers) else None
                emit_sublayer(pr, k, i == 0, nxt)
        else:
            for k in sublayers:
                pr.x_src = x_in
                pr.x_dst = x_out
                emit_sublayer(pr, k, True, None)
        for e in ENGS:
            pr.ctx.emit(e, None, waits=[(pr.s_store, pr.s_store.n)])
        block = stack.enter_context(nc.Block())
        pr.ctx.finalize(block)
        names = set(pr.input_names)
    return nc, names


def kernel(**inputs):
    sh, per_core, inv = prep_host(inputs)
    groups = [SUBLAYERS] if MODE == "fused" else [[k] for k in SUBLAYERS]
    xcur = [pc["x_in"] for pc in per_core]
    for group in groups:
        nc, input_names = build_launch(group, MODE)
        in_maps = []
        for c in range(N_CORES):
            m = {}
            for nm in input_names:
                if nm == "x_in":
                    m[nm] = xcur[c]
                elif nm in per_core[c]:
                    m[nm] = per_core[c][nm]
                else:
                    m[nm] = sh[nm]
            in_maps.append(m)
        res = run_bass_kernel_spmd(nc, in_maps, core_ids=list(range(N_CORES)))
        for key in res.results[0]:
            if key.startswith("dbg"):
                DEBUG_RES[key] = [np.asarray(r[key]) for r in res.results]
        xcur = [r["x_out"] for r in res.results]

    out = np.empty((B, SEQ, D), np.float32)
    for c in range(N_CORES):
        xf = xcur[c].reshape(D, B, SLOC)
        out[:, SLOC * c: SLOC * (c + 1), :] = xf.transpose(1, 2, 0)
    return np.ascontiguousarray(out[:, :, inv])


# version 17
# speedup vs baseline: 1.0407x; 1.0057x over previous
import math
from contextlib import ExitStack

import numpy as np
import ml_dtypes

import concourse.bass as bass
import concourse.mybir as mybir
from concourse.bass_utils import run_bass_kernel_spmd

F32 = mybir.dt.float32
BF16 = mybir.dt.bfloat16
AF = mybir.ActivationFunctionType
ALU = mybir.AluOpType
AX = mybir.AxisListType

N_CORES = 8
B = 2
SEQ = 4096
D = 4096
NCH = 32
SLOC = 512
TLOC = 1024
YST = 1032
NEG = -1e9
RS128 = 1.0 / math.sqrt(128.0)
RS64 = 0.125

SUBLAYERS = list(range(1, 9))
MODE = "fused"
DEBUG_TAP = None
DEBUG_RES = {}


def _tile_lhsT(WT):
    Dk, O = WT.shape
    return np.ascontiguousarray(
        WT.reshape(Dk // 128, 128, O // 128, 128).transpose(1, 2, 0, 3))


def _tile_rhs(W):
    Dk, O = W.shape
    return np.ascontiguousarray(W.reshape(Dk // 128, 128, O).transpose(1, 0, 2))


def _bias_col(b):
    return np.ascontiguousarray(np.asarray(b, np.float32).reshape(-1, 128).T)


def prep_host(inputs):
    f32 = np.float32
    perm = np.asarray(inputs["L0_ffn_perm"]).astype(np.int64)
    inv = np.argsort(perm)
    sh = {}

    x = np.asarray(inputs["L0_attn_a_x"], dtype=f32)
    xp = np.ascontiguousarray(x[:, :, perm])
    per_core = []
    for c in range(N_CORES):
        xc = xp[:, SLOC * c: SLOC * (c + 1), :]
        xf = np.ascontiguousarray(xc.transpose(2, 0, 1).reshape(NCH, 128, TLOC))
        per_core.append({"x_in": xf})

    sh["c_ones_f"] = np.ones((128, 128), f32)
    sh["c_negones"] = np.full((1, 128), -1.0, f32)
    sh["c_onescol"] = np.ones((128, 1), f32)
    sh["c_ident"] = np.eye(128, dtype=f32)
    idx = np.arange(128)
    sh["c_mask_diag"] = np.where(idx[:, None] <= idx[None, :], 0.0, NEG).astype(f32)
    sh["c_mask_tail"] = np.where(idx[:, None] > idx[None, :], 0.0, NEG).astype(f32)
    PT = np.zeros((128, 128), f32)
    for e in range(64):
        PT[e + 64, e] = -1.0
        PT[e, e + 64] = 1.0
    sh["c_rotPT"] = PT
    E = np.zeros((8, 128), f32)
    for e in range(8):
        E[e, 16 * e: 16 * (e + 1)] = 1.0
    sh["c_expE"] = E

    half = np.arange(0, 128, 2, dtype=f32)
    inv_freq = 1.0 / (10000.0 ** (half / 128.0))
    t = np.arange(SEQ, dtype=f32)
    emb = np.concatenate([np.outer(t, inv_freq)] * 2, axis=1)
    cosT = np.cos(emb).T.astype(f32)
    sinT = np.sin(emb).T.astype(f32)
    for c in range(N_CORES):
        per_core[c]["c_cosT"] = np.ascontiguousarray(cosT[:, SLOC * c: SLOC * (c + 1)])
        per_core[c]["c_sinT"] = np.ascontiguousarray(sinT[:, SLOC * c: SLOC * (c + 1)])
        vcol = np.zeros((128, 12), f32)
        for j in range(12):
            if c + j // 4 - 2 < 0:
                vcol[:, j] = NEG
        per_core[c]["b0_vcol"] = vcol
        hv = np.zeros((128, 2), f32)
        hv[:, 0] = 1.0 if c > 0 else 0.0
        hv[:, 1] = 1.0 if c < N_CORES - 1 else 0.0
        per_core[c]["b1_hv"] = hv

    for nm in ("Wq", "Wk", "Wv", "Wo"):
        W = np.asarray(inputs[f"L0_attn_b_{nm}"], f32)
        WT = np.ascontiguousarray(W[perm, :].T) if nm == "Wo" else \
            np.ascontiguousarray(W[:, perm].T)
        sh[f"b0_{nm}T"] = _tile_lhsT(WT)
    up = np.asarray(inputs["L0_ffn_up"], f32)
    down = np.asarray(inputs["L0_ffn_down"], f32)
    sh["f0_up"] = np.ascontiguousarray(
        up.reshape(16, 2, 128, 8, 128).transpose(0, 2, 1, 3, 4).reshape(16, 128, 2048))
    sh["f0_down"] = np.ascontiguousarray(
        down.reshape(16, 8, 128, 2, 128).transpose(0, 2, 1, 3, 4).reshape(16, 128, 2048))
    for nm in ("Wq", "Wk", "Wv", "Wo"):
        W = np.asarray(inputs[f"L1_attn_a_{nm}"], f32)
        WT = np.ascontiguousarray(W[perm, :].T) if nm == "Wo" else \
            np.ascontiguousarray(W[:, perm].T)
        sh[f"a1_{nm}T"] = _tile_lhsT(WT)
    Wb = np.asarray(inputs["L1_ffn_W_base"], f32)
    sh["f1_WbT"] = _tile_lhsT(np.ascontiguousarray(Wb[perm][:, perm].T))
    sh["f1_bias"] = _bias_col(np.asarray(inputs["L1_ffn_b_base"], f32)[perm])
    Wr = np.asarray(inputs["L1_ffn_W_router"], f32)
    WrT = np.ascontiguousarray(Wr[:, perm].T)
    sh["f1_WrT"] = np.ascontiguousarray(WrT.reshape(NCH, 128, 8).transpose(1, 0, 2))
    A = np.asarray(inputs["L1_ffn_lora_A"], f32)
    A2T = np.ascontiguousarray(A.transpose(2, 0, 1).reshape(D, 128))[perm, :]
    sh["f1_A2T"] = np.ascontiguousarray(A2T.reshape(NCH, 128, 128).transpose(1, 0, 2))
    Bm = np.asarray(inputs["L1_ffn_lora_B"], f32)
    B2 = np.ascontiguousarray(Bm.transpose(0, 2, 1).reshape(128, D))[:, perm]
    sh["f1_B2"] = np.ascontiguousarray(B2.reshape(128, NCH, 128))
    cw = np.asarray(inputs["L1_attn_b_conv_w"], f32)[:, perm, :]
    sh["b1_conv"] = np.ascontiguousarray(
        cw.reshape(3, NCH, 128, 24, 128).transpose(0, 2, 3, 1, 4))
    sh["b1_convb"] = _bias_col(inputs["L1_attn_b_conv_b"])
    for nm in ("wq", "wk", "wv"):
        W2 = np.asarray(inputs[f"L1_attn_b_{nm}"], f32).reshape(1024, 1024)
        if nm == "wv":
            sh["b1_wv_rhs"] = _tile_rhs(W2)
        else:
            sh[f"b1_{nm}"] = _tile_lhsT(W2)
        sh[f"b1_b{nm[1]}"] = _bias_col(
            np.asarray(inputs[f"L1_attn_b_b{nm[1]}"], f32).reshape(1024))
    sh["b1_bv_row"] = np.ascontiguousarray(
        np.asarray(inputs["L1_attn_b_bv"], f32).reshape(1, 1024))
    wo = np.asarray(inputs["L1_attn_b_wo"], f32).reshape(1024, 1024)
    sh["b1_wo"] = _tile_lhsT(wo)
    sh["b1_bo"] = _bias_col(inputs["L1_attn_b_bo"])
    pw = np.asarray(inputs["L1_attn_b_proj_w"], f32)
    sh["b1_proj"] = _tile_lhsT(np.ascontiguousarray(pw[:, perm]))
    sh["b1_projb"] = _bias_col(np.asarray(inputs["L1_attn_b_proj_b"], f32)[perm])
    return sh, per_core, inv



ENGS = ("tensor", "vector", "scalar", "gpsimd", "sync")


class SemT:
    __slots__ = ("h", "n", "name")

    def __init__(self, h, name):
        self.h, self.n, self.name = h, 0, name


class Ctx:
    def __init__(self, nc, stack):
        self.nc = nc
        self.stack = stack
        self.ops = {e: [] for e in ENGS}
        self.sems = []
        self._semi = 0
        self.es = {e: self.new_sem("es_" + e) for e in ENGS}
        self.pid = {}
        self.need_pid = set()

    def new_sem(self, name=None):
        name = f"{name}_{self._semi}" if name else f"s{self._semi}"
        self._semi += 1
        h = self.stack.enter_context(self.nc.semaphore(name))
        s = SemT(h, name)
        self.sems.append(s)
        return s

    def emit(self, eng, build, waits=(), inc=None, inc_by=1):
        waits = [(s.h, v) for (s, v) in waits if v > 0]
        after = None
        ih = None
        if inc is not None:
            inc.n += inc_by
            after = inc.n
            ih = inc.h

        def op(e, _w=waits, _b=build, _ih=ih, _iv=inc_by):
            for (h, v) in _w:
                e.wait_ge(h, v)
            ins = _b(e) if _b is not None else None
            if _ih is not None:
                if ins is None:
                    ins = e.nop()
                ins.then_inc(_ih, _iv)
        self.ops[eng].append(op)
        return after

    def vN(self):
        return self.es["vector"].n

    def barrier(self):
        vals = [(s, s.n) for s in self.sems if s.n > 0]
        for e in ENGS:
            self.emit(e, None, waits=vals)

    def finalize(self, block):
        for e in ENGS:
            ops = self.ops[e]
            sec = getattr(block, e)
            need = e in self.need_pid

            def runner(eng, _ops=ops, _e=e, _need=need):
                if _need:
                    self.pid[_e] = self.nc.partition_id()
                for f in _ops:
                    f(eng)
            sec(runner)


class Slot:
    __slots__ = ("ap", "sem", "free_waits")

    def __init__(self, ap, sem):
        self.ap, self.sem, self.free_waits = ap, sem, []


class Ring:
    def __init__(self, ctx, aps, name):
        self.slots = [Slot(ap, ctx.new_sem(f"{name}{i}")) for i, ap in enumerate(aps)]
        self.i = 0

    def next(self):
        s = self.slots[self.i % len(self.slots)]
        self.i += 1
        return s


def dma(ctx, queue, dst, src, slot_sem=None, waits=()):
    if slot_sem is not None:
        return ctx.emit(queue, lambda e, _d=dst, _s=src: e.dma_start(out=_d, in_=_s),
                        waits=waits, inc=slot_sem, inc_by=16)
    ctx.emit(queue, lambda e, _d=dst, _s=src: e.dma_start(out=_d, in_=_s), waits=waits)
    return None



class Prog:
    def __init__(self, nc, stack):
        self.nc = nc
        self.stack = stack
        self.ctx = Ctx(nc, stack)
        self.dram = {}
        self.input_names = set()

        def sb(name, shape, dt):
            return stack.enter_context(nc.sbuf_tensor(name, shape, dt))
        self.Y = sb("Y", [128, NCH * YST], BF16)
        self.BIG = sb("BIGB", [128, NCH * 1024], BF16)
        self.AUX = sb("AUX", [128, 8192], BF16)
        self.NB = sb("NB", [128, 1024], BF16)
        self.BRr = sb("BRr", [128, 1024], BF16)
        self.XS = [sb(f"XS{i}", [128, 1024], F32) for i in range(2)]
        self.WG = [sb(f"WG{i}", [128, 4096], BF16) for i in range(2)]
        self.SCR = [sb(f"SCR{i}", [128, 512], F32) for i in range(3)]
        self.PTB = [sb(f"PTB{i}", [128, 512], BF16) for i in range(2)]
        self.KSLB = sb("KSLB", [128, 4096], BF16)
        self.VTB = sb("VTB", [128, 4096], BF16)
        self.ROWS = sb("ROWS", [128, 1024], F32)
        self.STAT = sb("STAT", [128, 160], F32)
        self.WR = sb("WR", [128, 256], BF16)
        self.KNM = sb("KNM", [1, 128], F32)
        self.ones_f = sb("ones_f", [128, 128], F32)
        self.ones_b = sb("ones_b", [1, 128], BF16)
        self.negones_b = sb("negones_b", [1, 128], BF16)
        self.onescol_f = sb("onescol_f", [128, 1], F32)
        self.onescol_b = sb("onescol_b", [128, 1], BF16)
        self.ident_f = sb("ident_f", [128, 128], F32)
        self.mask_diag = sb("mask_diag", [128, 128], BF16)
        self.mask_tail = sb("mask_tail", [128, 128], BF16)
        self.rotPT = sb("rotPT", [128, 128], BF16)
        self.expE = sb("expE", [8, 128], BF16)
        self.cosT = sb("cosT", [128, 512], BF16)
        self.sinT = sb("sinT", [128, 512], BF16)
        self.vcolB = sb("vcolB", [128, 12], F32)
        self.hvB = sb("hvB", [128, 2], F32)
        self.biasC = {}
        for nm, w in (("f1_bias", 32), ("b1_convb", 24), ("b1_bq", 8),
                      ("b1_bk", 8), ("b1_bo", 8), ("b1_projb", 32)):
            self.biasC[nm] = sb("bc_" + nm, [128, w], F32)
        self.W8T = self.KSLB
        self.HSB = self.VTB
        self.HSB2 = self.VTB
        self.PS = [stack.enter_context(nc.psum_tensor(f"PS{i}", [128, 512], F32))
                   for i in range(8)]
        self.bank_free = [[] for _ in range(8)]

        self.r_x = Ring(self.ctx, [t[:, :] for t in self.XS], "rx")
        self.r_wg = Ring(self.ctx, [t[:, :] for t in self.WG], "rwg")
        self.s_store = self.ctx.new_sem("st")
        self.x_src = None
        self.x_dst = None

    def din(self, name, shape, dt=F32):
        if name not in self.dram:
            self.dram[name] = self.nc.dram_tensor(name, list(shape), dt,
                                                  kind="ExternalInput")
            self.input_names.add(name)
        return self.dram[name]

    def dint(self, name, shape, dt=BF16):
        if name not in self.dram:
            self.dram[name] = self.nc.dram_tensor(name, list(shape), dt, kind="Internal")
        return self.dram[name]

    def load_consts(self):
        ctx = self.ctx
        sem = ctx.new_sem("consts")
        pairs = [
            ("c_ones_f", self.ones_f[:, :], (128, 128)),
            ("c_onescol", self.onescol_f[:, :], (128, 1)),
            ("c_ident", self.ident_f[:, :], (128, 128)),
            ("c_mask_diag", self.mask_diag[:, :], (128, 128)),
            ("c_mask_tail", self.mask_tail[:, :], (128, 128)),
            ("c_rotPT", self.rotPT[:, :], (128, 128)),
            ("c_expE", self.expE[0:8, :], (8, 128)),
            ("c_cosT", self.cosT[:, :], (128, 512)),
            ("c_sinT", self.sinT[:, :], (128, 512)),
            ("b0_vcol", self.vcolB[:, :], (128, 12)),
            ("b1_hv", self.hvB[:, :], (128, 2)),
            ("c_negones", self.negones_b[0:1, :], (1, 128)),
        ]
        for nm, dst, shp in pairs:
            t = self.din(nm, shp)
            dma(ctx, "gpsimd", dst, t.ap(), slot_sem=sem)
        for nm, buf in self.biasC.items():
            t = self.din(nm, (128, buf.shape[1]))
            dma(ctx, "gpsimd", buf[:, :], t.ap(), slot_sem=sem)
        ctx.emit("vector", lambda e: e.tensor_copy(self.ones_b[0:1, :], self.ones_f[0:1, :]),
                 waits=[(sem, sem.n)], inc=ctx.es["vector"])
        ctx.emit("vector", lambda e: e.tensor_copy(self.onescol_b[:, :], self.onescol_f[:, :]),
                 inc=ctx.es["vector"])
        ctx.barrier()

    @staticmethod
    def col_n(ch, b):
        return ch * 1024 + b * 512

    @staticmethod
    def col_p(ch, b):
        return ch * YST + 1 + b * 514

    def pass_A(self, padded=False):
        ctx = self.ctx
        col = self.col_p if padded else self.col_n
        sq_free = [[], []]
        for ch in range(NCH):
            sl = self.r_x.next()
            c = dma(ctx, "sync", sl.ap, self.x_src.ap()[ch], slot_sem=sl.sem,
                    waits=sl.free_waits)
            casts = []
            for b in range(2):
                ycols = self.Y[:, col(ch, b): col(ch, b) + 512]
                cv = ctx.emit("scalar",
                              lambda e, _y=ycols, _x=sl.ap[:, b * 512:(b + 1) * 512]:
                              e.copy(_y, _x),
                              waits=[(sl.sem, c)] if b == 0 else [],
                              inc=ctx.es["scalar"])
                casts.append((ycols, cv))
            sl.free_waits = [(ctx.es["scalar"], casts[-1][1])]
            for b in range(2):
                ycols, cv = casts[b]
                si = (2 * ch + b) % 2
                sqa = self.AUX[:, si * 512: si * 512 + 512]
                cs = ctx.emit("scalar",
                              lambda e, _o=sqa, _y=ycols: e.activation(_o, _y, AF.Square),
                              waits=sq_free[si], inc=ctx.es["scalar"])
                ctx.emit("tensor",
                         lambda e, _p=self.PS[b][0:1, :], _r=ycols, _s=(ch == 0),
                                _e=(ch == NCH - 1):
                         e.matmul(_p, self.onescol_b[:, 0:1], _r, start=_s, stop=_e),
                         waits=[(ctx.es["scalar"], cv)], inc=ctx.es["tensor"])
                last = ctx.emit("tensor",
                                lambda e, _p=self.PS[2 + b][0:1, :], _r=sqa, _s=(ch == 0),
                                       _e=(ch == NCH - 1):
                                e.matmul(_p, self.onescol_b[:, 0:1], _r, start=_s, stop=_e),
                                waits=[(ctx.es["scalar"], cs)], inc=ctx.es["tensor"])
                sq_free[si] = [(ctx.es["tensor"], last)]
        self._rows_and_broadcasts()
        ctx.barrier()

    def _rows_and_broadcasts(self):
        ctx = self.ctx
        R = self.ROWS
        S0 = self.SCR[0]
        tN = ctx.es["tensor"].n
        prevv = 0
        for th in range(2):
            c = ctx.emit("vector", lambda e, _p=self.PS[th][0:1, :]:
                     e.tensor_copy(R[0:1, 0:512], _p),
                     waits=[(ctx.es["tensor"], tN), (ctx.es["vector"], prevv)],
                     inc=ctx.es["vector"])
            c = ctx.emit("vector", lambda e, _p=self.PS[2 + th][0:1, :]:
                     e.tensor_copy(S0[0:1, 0:512], _p),
                     waits=[(ctx.es["vector"], c)], inc=ctx.es["vector"])
            c = ctx.emit("vector", lambda e: e.tensor_scalar(R[0:1, 0:512], R[0:1, 0:512],
                                                             1.0 / D, None, ALU.mult),
                     waits=[(ctx.es["vector"], c)], inc=ctx.es["vector"])
            c = ctx.emit("vector", lambda e: e.tensor_tensor(R[0:1, 512:1024], R[0:1, 0:512],
                                                             R[0:1, 0:512], ALU.mult),
                     waits=[(ctx.es["vector"], c)], inc=ctx.es["vector"])
            c = ctx.emit("vector", lambda e: e.tensor_scalar(S0[0:1, 0:512], S0[0:1, 0:512],
                                                             1.0 / D, None, ALU.mult),
                     waits=[(ctx.es["vector"], c)], inc=ctx.es["vector"])
            c = ctx.emit("vector", lambda e: e.tensor_tensor(S0[0:1, 0:512], S0[0:1, 0:512],
                                                             R[0:1, 512:1024], ALU.subtract),
                     waits=[(ctx.es["vector"], c)], inc=ctx.es["vector"])
            cvar = ctx.emit("vector", lambda e: e.tensor_scalar(S0[0:1, 0:512],
                                                                S0[0:1, 0:512],
                                                                1e-5, None, ALU.add),
                            waits=[(ctx.es["vector"], c)], inc=ctx.es["vector"])
            csd = ctx.emit("scalar", lambda e: e.activation(R[0:1, 512:1024],
                                                            S0[0:1, 0:512], AF.Sqrt),
                           waits=[(ctx.es["vector"], cvar)], inc=ctx.es["scalar"])
            crs = ctx.emit("vector", lambda e: e.reciprocal(S0[0:1, 0:512],
                                                            R[0:1, 512:1024]),
                           waits=[(ctx.es["scalar"], csd)], inc=ctx.es["vector"])
            cnm = ctx.emit("vector", lambda e: e.tensor_scalar(R[0:1, 0:512],
                                                               R[0:1, 0:512], -1.0,
                                                               None, ALU.mult),
                           waits=[(ctx.es["vector"], crs)], inc=ctx.es["vector"])
            m1 = ctx.emit("tensor",
                          lambda e: e.matmul(self.PS[4][:, :], self.ones_f[0:1, :],
                                             R[0:1, 0:512], start=True, stop=True),
                          waits=[(ctx.es["vector"], cnm)] + self.bank_free[4],
                          inc=ctx.es["tensor"])
            c1 = ctx.emit("vector",
                          lambda e, _d=self.NB[:, th * 512:(th + 1) * 512]:
                          e.tensor_copy(_d, self.PS[4][:, :]),
                          waits=[(ctx.es["tensor"], m1)], inc=ctx.es["vector"])
            self.bank_free[4] = [(ctx.es["vector"], c1)]
            m2 = ctx.emit("tensor",
                          lambda e: e.matmul(self.PS[5][:, :], self.ones_f[0:1, :],
                                             S0[0:1, 0:512], start=True, stop=True),
                          waits=[(ctx.es["vector"], crs)] + self.bank_free[5],
                          inc=ctx.es["tensor"])
            c2 = ctx.emit("vector",
                          lambda e, _d=self.BRr[:, th * 512:(th + 1) * 512]:
                          e.tensor_copy(_d, self.PS[5][:, :]),
                          waits=[(ctx.es["tensor"], m2)], inc=ctx.es["vector"])
            self.bank_free[5] = [(ctx.es["vector"], c2)]
            prevv = ctx.es["vector"].n

    def pass_B(self, padded=False, barrier=True):
        ctx = self.ctx
        col = self.col_p if padded else self.col_n
        slot_w = {0: [], 1: []}
        self.y_ready = [None] * NCH
        for ch in range(NCH):
            for b in range(2):
                ycols = self.Y[:, col(ch, b): col(ch, b) + 512]
                si = (2 * ch + b) % 2
                tt = self.SCR[si][:, :]
                ca = ctx.emit("gpsimd",
                              lambda e, _t=tt, _s=ycols, _n=self.NB[:, b * 512:(b + 1) * 512]:
                              e.tensor_tensor(_t, _s, _n, ALU.add),
                              waits=slot_w[si], inc=ctx.es["gpsimd"])
                cv = ctx.emit("vector",
                              lambda e, _d=ycols, _t=tt, _r=self.BRr[:, b * 512:(b + 1) * 512]:
                              e.tensor_tensor(_d, _t, _r, ALU.mult),
                              waits=[(ctx.es["gpsimd"], ca)], inc=ctx.es["vector"])
                slot_w[si] = [(ctx.es["vector"], cv)]
            self.y_ready[ch] = (ctx.es["vector"], slot_w[(2 * ch + 1) % 2][0][1])
        if barrier:
            ctx.barrier()

    def emit_chunk_stats(self, ch, slot_ap, dep, padded_next):
        ctx = self.ctx
        col = self.col_p if padded_next else self.col_n
        casts = []
        for b in range(2):
            ycols = self.Y[:, col(ch, b): col(ch, b) + 512]
            cv = ctx.emit("scalar",
                          lambda e, _y=ycols, _x=slot_ap[:, b * 512:(b + 1) * 512]:
                          e.copy(_y, _x),
                          waits=[dep] if b == 0 else [],
                          inc=ctx.es["scalar"])
            casts.append((ycols, cv))
        for b in range(2):
            ycols, cv = casts[b]
            si = (2 * ch + b) % 2
            sqa = self.AUX[:, si * 512: si * 512 + 512]
            cs = ctx.emit("scalar",
                          lambda e, _o=sqa, _y=ycols: e.activation(_o, _y, AF.Square),
                          waits=self._sq_free[si], inc=ctx.es["scalar"])
            ctx.emit("tensor",
                     lambda e, _p=self.PS[b][0:1, :], _r=ycols, _s=(ch == 0),
                            _e=(ch == NCH - 1):
                     e.matmul(_p, self.onescol_b[:, 0:1], _r, start=_s, stop=_e),
                     waits=[(ctx.es["scalar"], cv)] +
                           (self.bank_free[b] if ch == 0 else []),
                     inc=ctx.es["tensor"])
            last = ctx.emit("tensor",
                            lambda e, _p=self.PS[2 + b][0:1, :], _r=sqa, _s=(ch == 0),
                                   _e=(ch == NCH - 1):
                            e.matmul(_p, self.onescol_b[:, 0:1], _r, start=_s, stop=_e),
                            waits=[(ctx.es["scalar"], cs)] +
                                  (self.bank_free[2 + b] if ch == 0 else []),
                            inc=ctx.es["tensor"])
            self._sq_free[si] = [(ctx.es["tensor"], last)]
            if ch == NCH - 1:
                self.bank_free[b] = []
                self.bank_free[2 + b] = []
        return (ctx.es["scalar"], casts[-1][1])

    def pass_D(self, delta_buf, dstride=1024, stats_next=None):
        ctx = self.ctx
        self._sq_free = [[], []]
        for ch in range(NCH):
            sl = self.r_x.next()
            c = dma(ctx, "sync", sl.ap, self.x_src.ap()[ch], slot_sem=sl.sem,
                    waits=sl.free_waits)
            dcols = delta_buf[:, ch * dstride: ch * dstride + 1024]
            cv = ctx.emit("vector",
                          lambda e, _x=sl.ap, _d=dcols: e.tensor_tensor(_x, _x, _d, ALU.add),
                          waits=[(sl.sem, c)], inc=ctx.es["vector"])
            cs = dma(ctx, "sync", self.x_dst.ap()[ch], sl.ap, slot_sem=self.s_store,
                     waits=[(ctx.es["vector"], cv)])
            sl.free_waits = [(self.s_store, cs)]
            if stats_next is not None:
                cst = self.emit_chunk_stats(ch, sl.ap, (ctx.es["vector"], cv),
                                            stats_next == "b1")
                sl.free_waits = sl.free_waits + [cst]
        ctx.barrier()

    def pass_A_tail(self):
        self._rows_and_broadcasts()
        self.ctx.barrier()



def stream_proj(pr, wname, wshape, n_oc, n_cc, rhs_fn, out_write,
                bank_base=0, bank_n=8, extra_accum=None, gran_sel=None,
                rhs_ready=None):
    ctx = pr.ctx
    w = pr.din(wname, wshape)
    ppr = bank_n // 2
    for oc in range(n_oc):
        sl = pr.r_wg.next()
        src = w.ap()[:, oc] if gran_sel is None else gran_sel(w, oc)
        c = dma(ctx, "gpsimd", sl.ap[:, 0: n_cc * 128],
                src.rearrange("p c j -> p (c j)"), slot_sem=sl.sem,
                waits=sl.free_waits)
        banks = [bank_base + (oc % ppr) * 2 + th for th in range(2)]
        last = None
        for cc in range(n_cc):
            tile = sl.ap[:, cc * 128:(cc + 1) * 128]
            for th in range(2):
                bw = []
                if cc == 0:
                    bw = list(pr.bank_free[banks[th]])
                    if th == 0:
                        bw = bw + [(sl.sem, c)]
                if oc == 0 and th == 0 and rhs_ready is not None:
                    bw = bw + [rhs_ready(cc)]
                last = ctx.emit(
                    "tensor",
                    lambda e, _p=pr.PS[banks[th]][:, :], _t=tile, _r=rhs_fn(cc, th),
                           _s=(cc == 0), _e=(cc == n_cc - 1 and extra_accum is None):
                    e.matmul(_p, _t, _r, start=_s, stop=_e),
                    waits=bw, inc=ctx.es["tensor"])
        sl.free_waits = [(ctx.es["tensor"], last)]
        for th in range(2):
            cnt = last
            if extra_accum is not None:
                cnt = extra_accum(oc, th, pr.PS[banks[th]], cnt)
        for th in range(2):
            cnt = ctx.es["tensor"].n
            pr.bank_free[banks[th]] = [out_write(oc, th, pr.PS[banks[th]], cnt)]


def delta_writer(pr, buf, bias=None, stride=1024, col_off=0):
    ctx = pr.ctx

    def wr(oc, th, psum, dep):
        cols = buf[:, col_off + oc * stride + th * 512:
                   col_off + oc * stride + th * 512 + 512]
        if bias is not None:
            cc = ctx.emit("vector",
                          lambda e, _d=cols, _p=psum[:, :], _b=bias[:, oc:oc + 1]:
                          e.tensor_scalar(_d, _p, _b, None, ALU.add),
                          waits=[(ctx.es["tensor"], dep)], inc=ctx.es["vector"])
            return (ctx.es["vector"], cc)
        cc = ctx.emit("vector",
                      lambda e, _d=cols, _p=psum[:, :]: e.tensor_copy(_d, _p),
                      waits=[(ctx.es["tensor"], dep)], inc=ctx.es["vector"])
        return (ctx.es["vector"], cc)
    return wr


def y_rhs(pr):
    return lambda cc, th: pr.Y[:, cc * 1024 + th * 512: cc * 1024 + th * 512 + 512]


def big_rhs(pr):
    return lambda cc, th: pr.BIG[:, cc * 1024 + th * 512: cc * 1024 + th * 512 + 512]



def ph_attn_a(pr, stats_next=None):
    ctx = pr.ctx
    ar_in = pr.dint("aa_in", [128, 96], F32)
    ar_out = pr.dint("aa_out", [128, 96], F32)
    ST = pr.STAT
    red_done = {0: [], 1: [], 2: []}
    for ch in range(NCH):
        y0 = pr.Y[:, ch * 1024: ch * 1024 + 512]
        y1 = pr.Y[:, ch * 1024 + 512: ch * 1024 + 1024]
        for pi, (a, b_, colo) in enumerate(((y0, y0, 0), (y0, y1, 32), (y1, y1, 64))):
            prod = pr.SCR[pi][:, :]
            cg = ctx.emit("gpsimd",
                          lambda e, _p=prod, _a=a, _b=b_:
                          e.tensor_tensor(_p, _a, _b, ALU.mult),
                          waits=red_done[pi], inc=ctx.es["gpsimd"])
            cv = ctx.emit("vector",
                          lambda e, _d=ST[:, colo + ch: colo + ch + 1], _p=prod:
                          e.tensor_reduce(_d, _p, AX.X, ALU.add),
                          waits=[(ctx.es["gpsimd"], cg)], inc=ctx.es["vector"])
            red_done[pi] = [(ctx.es["vector"], cv)]
    cv = ctx.es["vector"].n
    cst = dma(ctx, "sync", ar_in.ap(), ST[:, 0:96], slot_sem=pr.s_store,
              waits=[(ctx.es["vector"], cv)])
    sc = ctx.new_sem("aa_cc")
    ctx.emit("gpsimd",
             lambda e: e.collective_compute(
                 "AllReduce", ALU.add, replica_groups=[list(range(N_CORES))],
                 ins=[ar_in.ap()], outs=[ar_out.ap()]),
             waits=[(pr.s_store, cst)], inc=sc)
    sld = ctx.new_sem("aa_ld")
    cld = dma(ctx, "gpsimd", ST[:, 0:96], ar_out.ap(), slot_sem=sld, waits=[(sc, 1)])
    ctx.emit("vector", lambda e: e.tensor_tensor(ST[:, 96:128], ST[:, 0:32],
                                                 ST[:, 32:64], ALU.subtract),
             waits=[(sld, cld)], inc=ctx.es["vector"])
    c2 = ctx.emit("vector", lambda e: e.tensor_tensor(ST[:, 128:160], ST[:, 32:64],
                                                      ST[:, 64:96], ALU.subtract),
                  inc=ctx.es["vector"])
    ctx.emit("scalar", lambda e: e.activation(ST[:, 96:128], ST[:, 96:128], AF.Sigmoid,
                                              scale=1.0 / 64.0),
             waits=[(ctx.es["vector"], c2)], inc=ctx.es["scalar"])
    ctx.emit("scalar", lambda e: e.activation(ST[:, 128:160], ST[:, 128:160], AF.Sigmoid,
                                              scale=1.0 / 64.0), inc=ctx.es["scalar"])
    ctx.barrier()
    q0 = ST[:, 96:128]
    q1 = ST[:, 128:160]
    pr._sq_free = [[], []]
    gprev = []
    for ch in range(NCH):
        sl = pr.r_x.next()
        c = dma(ctx, "sync", sl.ap, pr.x_src.ap()[ch], slot_sem=sl.sem,
                waits=sl.free_waits)
        y0 = pr.Y[:, ch * 1024: ch * 1024 + 512]
        y1 = pr.Y[:, ch * 1024 + 512: ch * 1024 + 1024]
        x0 = sl.ap[:, 0:512]
        x1 = sl.ap[:, 512:1024]
        t0, t1, t2 = pr.SCR[0][:, :], pr.SCR[1][:, :], pr.SCR[2][:, :]
        p0c = q0[:, ch: ch + 1]
        p1c = q1[:, ch: ch + 1]
        a1 = ctx.emit("vector", lambda e, _t=t0, _y=y0, _p=p0c:
                      e.tensor_scalar(_t, _y, _p, None, ALU.mult), waits=gprev,
                      inc=ctx.es["vector"])
        a2 = ctx.emit("vector", lambda e, _t=t1, _y=y1, _p=p0c:
                      e.tensor_scalar(_t, _y, _p, -1.0, ALU.mult, op1=ALU.mult),
                      inc=ctx.es["vector"])
        a3 = ctx.emit("vector", lambda e, _t=t2, _y=y0, _p=p1c:
                      e.tensor_scalar(_t, _y, _p, None, ALU.mult), inc=ctx.es["vector"])
        ctx.emit("vector", lambda e, _x=x0, _y=y0: e.tensor_tensor(_x, _x, _y, ALU.add),
                 waits=[(sl.sem, c)], inc=ctx.es["vector"])
        ctx.emit("vector", lambda e, _x=x0, _y=y1: e.tensor_tensor(_x, _x, _y, ALU.add),
                 inc=ctx.es["vector"])
        c3 = ctx.emit("vector", lambda e, _x=x0, _t=t0: e.tensor_tensor(_x, _x, _t, ALU.add),
                      inc=ctx.es["vector"])
        ctx.emit("vector", lambda e, _x=x0, _t=t1: e.tensor_tensor(_x, _x, _t, ALU.add),
                 inc=ctx.es["vector"])
        cg1 = ctx.emit("gpsimd", lambda e, _x=x1, _y=y1:
                       e.tensor_scalar(_x, _y, 2.0, None, ALU.mult, accum_out=None)
                       if False else e.tensor_tensor(_x, _x, _y, ALU.add),
                       waits=[(sl.sem, c)], inc=ctx.es["gpsimd"])
        cg2 = ctx.emit("gpsimd", lambda e, _x=x1, _y=y1: e.tensor_tensor(_x, _x, _y, ALU.add),
                       inc=ctx.es["gpsimd"])
        c7 = ctx.emit("vector", lambda e, _x=x1, _t=t2: e.tensor_tensor(_x, _x, _t, ALU.add),
                      waits=[(ctx.es["gpsimd"], cg2)], inc=ctx.es["vector"])
        a4 = ctx.emit("vector", lambda e, _t=t0, _y=y1, _p=p1c:
                      e.tensor_scalar(_t, _y, _p, None, ALU.mult),
                      waits=[(ctx.es["vector"], c3)], inc=ctx.es["vector"])
        cl = ctx.emit("vector", lambda e, _x=x1, _t=t0:
                      e.tensor_tensor(_x, _x, _t, ALU.subtract),
                      waits=[(ctx.es["vector"], a4)], inc=ctx.es["vector"])
        gprev = [(ctx.es["vector"], cl)]
        cs = dma(ctx, "sync", pr.x_dst.ap()[ch], sl.ap, slot_sem=pr.s_store,
                 waits=[(ctx.es["vector"], cl)])
        sl.free_waits = [(pr.s_store, cs)]
        if stats_next is not None:
            cst = pr.emit_chunk_stats(ch, sl.ap, (ctx.es["vector"], cl),
                                      stats_next == "b1")
            sl.free_waits = sl.free_waits + [cst]
    ctx.barrier()



def ph_ffn0(pr):
    ctx = pr.ctx
    up = pr.din("f0_up", (16, 128, 2048))
    down = pr.din("f0_down", (16, 128, 2048))
    for h in range(16):
        slu = pr.r_wg.next()
        cu = dma(ctx, "gpsimd", slu.ap[:, 0:2048], up.ap()[h], slot_sem=slu.sem,
                 waits=slu.free_waits)
        sld = pr.r_wg.next()
        cd = dma(ctx, "gpsimd", sld.ap[:, 0:2048], down.ap()[h], slot_sem=sld.sem,
                 waits=sld.free_waits)
        last_up = None
        zlast = None
        for mc in range(8):
            bks = [(mc % 2) * 2, (mc % 2) * 2 + 1]
            for jc in range(2):
                tile = slu.ap[:, jc * 1024 + mc * 128: jc * 1024 + mc * 128 + 128]
                for th in range(2):
                    bw = []
                    if jc == 0:
                        bw = list(pr.bank_free[bks[th]])
                        if th == 0 and mc == 0:
                            bw = bw + [(slu.sem, cu)]
                    if mc == 0 and th == 0:
                        bw = bw + [pr.y_ready[2 * h + jc]]
                    ych = 2 * h + jc
                    last_up = ctx.emit(
                        "tensor",
                        lambda e, _p=pr.PS[bks[th]][:, :], _t=tile,
                               _r=pr.Y[:, ych * 1024 + th * 512: ych * 1024 + th * 512 + 512],
                               _s=(jc == 0), _e=(jc == 1):
                        e.matmul(_p, _t, _r, start=_s, stop=_e),
                        waits=bw, inc=ctx.es["tensor"])
            for th in range(2):
                zc = pr.AUX[:, mc * 1024 + th * 512: mc * 1024 + th * 512 + 512]
                cz = ctx.emit("scalar",
                              lambda e, _z=zc, _p=pr.PS[bks[th]][:, :]:
                              e.activation(_z, _p, AF.Gelu),
                              waits=[(ctx.es["tensor"], last_up)], inc=ctx.es["scalar"])
                zlast = cz
                pr.bank_free[bks[th]] = [(ctx.es["scalar"], cz)]
        last_dn = None
        for djc in range(2):
            bks = [4 + djc * 2, 5 + djc * 2]
            for mc in range(8):
                tile = sld.ap[:, mc * 256 + djc * 128: mc * 256 + djc * 128 + 128]
                for th in range(2):
                    bw = []
                    if mc == 0:
                        bw = list(pr.bank_free[bks[th]])
                        if th == 0:
                            bw = bw + [(ctx.es["scalar"], zlast)]
                            if djc == 0:
                                bw = bw + [(sld.sem, cd)]
                    last_dn = ctx.emit(
                        "tensor",
                        lambda e, _p=pr.PS[bks[th]][:, :], _t=tile,
                               _r=pr.AUX[:, mc * 1024 + th * 512: mc * 1024 + th * 512 + 512],
                               _s=(mc == 0), _e=(mc == 7):
                        e.matmul(_p, _t, _r, start=_s, stop=_e),
                        waits=bw, inc=ctx.es["tensor"])
            for th in range(2):
                och = 2 * h + djc
                dcols = pr.BIG[:, och * 1024 + th * 512: och * 1024 + th * 512 + 512]
                cvv = ctx.emit("vector",
                               lambda e, _d=dcols, _p=pr.PS[bks[th]][:, :]:
                               e.tensor_copy(_d, _p),
                               waits=[(ctx.es["tensor"], last_dn)], inc=ctx.es["vector"])
                pr.bank_free[bks[th]] = [(ctx.es["vector"], cvv)]
        slu.free_waits = [(ctx.es["tensor"], last_up)]
        sld.free_waits = [(ctx.es["tensor"], last_dn)]
    ctx.barrier()



def ph_ffn1(pr):
    ctx = pr.ctx
    wr = pr.din("f1_WrT", (128, NCH, 8))
    a2 = pr.din("f1_A2T", (128, NCH, 128))
    b2 = pr.din("f1_B2", (128, NCH, 128))
    sem_w = ctx.new_sem("f1w")
    dma(ctx, "gpsimd", pr.WR[:, 0:256], wr.ap().rearrange("p c j -> p (c j)"),
        slot_sem=sem_w)
    dma(ctx, "gpsimd", pr.AUX[:, 0:4096], a2.ap().rearrange("p c j -> p (c j)"),
        slot_sem=sem_w)
    dma(ctx, "gpsimd", pr.AUX[:, 4096:8192], b2.ap().rearrange("p c j -> p (c j)"),
        slot_sem=sem_w)
    prev_ev = 0
    for th in range(2):
        for cc in range(NCH):
            bw = []
            if cc == 0:
                bw = [(sem_w, 16)] if th == 0 else [(ctx.es["vector"], prev_ev)]
            if th == 0:
                bw = bw + [pr.y_ready[cc]]
            ctx.emit("tensor",
                     lambda e, _t=pr.WR[:, cc * 8:(cc + 1) * 8],
                            _r=pr.Y[:, cc * 1024 + th * 512: cc * 1024 + th * 512 + 512],
                            _s=(cc == 0), _e=(cc == NCH - 1):
                     e.matmul(pr.PS[6][0:8, :], _t, _r, start=_s, stop=_e),
                     waits=bw, inc=ctx.es["tensor"])
        prev_ev = ctx.emit("vector",
                           lambda e, _d=pr.ROWS[0:8, th * 512:(th + 1) * 512]:
                           e.tensor_copy(_d, pr.PS[6][0:8, :]),
                           waits=[(ctx.es["tensor"], ctx.es["tensor"].n)],
                           inc=ctx.es["vector"])
    lg_ev = prev_ev
    h_mm = None
    for th in range(2):
        for cc in range(NCH):
            bw = []
            if cc == 0:
                bw = list(pr.bank_free[3 + th])
                if th == 0:
                    bw = bw + [(sem_w, 32)]
            h_mm = ctx.emit("tensor",
                            lambda e, _p=pr.PS[3 + th][:, :],
                                   _t=pr.AUX[:, cc * 128:(cc + 1) * 128],
                                   _r=pr.Y[:, cc * 1024 + th * 512: cc * 1024 + th * 512 + 512],
                                   _s=(cc == 0), _e=(cc == NCH - 1):
                            e.matmul(_p, _t, _r, start=_s, stop=_e),
                            waits=bw, inc=ctx.es["tensor"])
    ps7_free = []
    for tt in range(8):
        ctr = ctx.emit("tensor",
                       lambda e, _i=pr.ROWS[0:8, tt * 128:(tt + 1) * 128]:
                       e.transpose(pr.PS[7][:, 0:8], _i, pr.ident_f[0:8, 0:8]),
                       waits=[(ctx.es["vector"], lg_ev)] + ps7_free,
                       inc=ctx.es["tensor"])
        lg = pr.STAT[:, tt * 8: tt * 8 + 8]
        cv0 = ctx.emit("vector", lambda e, _d=lg: e.tensor_copy(_d, pr.PS[7][:, 0:8]),
                       waits=[(ctx.es["tensor"], ctr)], inc=ctx.es["vector"])
        nm1 = pr.STAT[:, 64 + tt * 2: 64 + tt * 2 + 1]
        nm2 = pr.STAT[:, 64 + tt * 2 + 1: 64 + tt * 2 + 2]
        g1 = pr.SCR[0][:, 0:8]
        g2 = pr.SCR[1][:, 0:8]
        g3 = pr.SCR[2][:, 0:8]
        c = ctx.emit("vector", lambda e, _d=nm1, _l=lg:
                     e.tensor_reduce(_d, _l, AX.X, ALU.max, negate=True),
                     waits=[(ctx.es["vector"], cv0)], inc=ctx.es["vector"])
        c = ctx.emit("vector", lambda e, _d=g1, _l=lg, _n=nm1:
                     e.tensor_scalar(_d, _l, _n, None, ALU.add),
                     waits=[(ctx.es["vector"], c)], inc=ctx.es["vector"])
        c = ctx.emit("vector", lambda e, _d=g2, _g=g1:
                     e.tensor_scalar(_d, _g, 0.0, NEG, ALU.is_ge, op1=ALU.mult),
                     waits=[(ctx.es["vector"], c)], inc=ctx.es["vector"])
        c = ctx.emit("vector", lambda e, _d=g3, _l=lg, _g=g2:
                     e.tensor_tensor(_d, _l, _g, ALU.add),
                     waits=[(ctx.es["vector"], c)], inc=ctx.es["vector"])
        c = ctx.emit("vector", lambda e, _d=nm2, _g=g3:
                     e.tensor_reduce(_d, _g, AX.X, ALU.max, negate=True),
                     waits=[(ctx.es["vector"], c)], inc=ctx.es["vector"])
        cp8 = ctx.emit("scalar", lambda e, _d=g3, _g=g1: e.activation(_d, _g, AF.Exp),
                       waits=[(ctx.es["vector"], c)], inc=ctx.es["scalar"])
        c = ctx.emit("vector", lambda e, _d=g1, _l=lg, _n=nm2:
                     e.tensor_scalar(_d, _l, _n, None, ALU.add),
                     waits=[(ctx.es["vector"], c), (ctx.es["scalar"], cp8)],
                     inc=ctx.es["vector"])
        c = ctx.emit("vector", lambda e, _d=g1:
                     e.tensor_scalar(_d, _d, 0.0, None, ALU.is_ge),
                     waits=[(ctx.es["vector"], c)], inc=ctx.es["vector"])
        c = ctx.emit("vector", lambda e, _d=g1, _p=g3:
                     e.tensor_tensor(_d, _p, _d, ALU.mult),
                     waits=[(ctx.es["vector"], c)], inc=ctx.es["vector"])
        c = ctx.emit("vector", lambda e, _d=nm1, _g=g1:
                     e.tensor_reduce(_d, _g, AX.X, ALU.add),
                     waits=[(ctx.es["vector"], c)], inc=ctx.es["vector"])
        c = ctx.emit("vector", lambda e, _d=nm2, _n=nm1: e.reciprocal(_d, _n),
                     waits=[(ctx.es["vector"], c)], inc=ctx.es["vector"])
        cwf = ctx.emit("vector", lambda e, _d=g2, _g=g1, _n=nm2:
                       e.tensor_scalar(_d, _g, _n, 2.0, ALU.mult, op1=ALU.mult),
                       waits=[(ctx.es["vector"], c)], inc=ctx.es["vector"])
        ctr2 = ctx.emit("tensor",
                        lambda e, _i=g2: e.transpose(pr.PS[7][0:8, 0:128], _i,
                                                     pr.ident_f[:, :]),
                        waits=[(ctx.es["vector"], cwf)], inc=ctx.es["tensor"])
        cwe = ctx.emit("vector",
                       lambda e, _d=pr.KSLB[0:8, 1024 + tt * 128: 1024 + (tt + 1) * 128]:
                       e.tensor_copy(_d, pr.PS[7][0:8, 0:128]),
                       waits=[(ctx.es["tensor"], ctr2)], inc=ctx.es["vector"])
        ps7_free = [(ctx.es["vector"], cwe)]
    wt_done = ctx.es["vector"].n
    prev = 0
    for th in range(2):
        cm = ctx.emit("tensor",
                      lambda e, _r=pr.KSLB[0:8, 1024 + th * 512: 1024 + (th + 1) * 512]:
                      e.matmul(pr.PS[5][:, :], pr.expE[0:8, :], _r, start=True, stop=True),
                      waits=[(ctx.es["vector"], wt_done if th == 0 else prev)] +
                            (pr.bank_free[5] if th == 0 else []),
                      inc=ctx.es["tensor"])
        cev = ctx.emit("vector",
                       lambda e, _d=pr.PTB[th][:, :]: e.tensor_copy(_d, pr.PS[5][:, :]),
                       waits=[(ctx.es["tensor"], cm)], inc=ctx.es["vector"])
        prev = cev
        cg = ctx.emit("vector",
                      lambda e, _d=pr.KSLB[:, th * 512:(th + 1) * 512],
                             _h=pr.PS[3 + th][:, :], _w=pr.PTB[th][:, :]:
                      e.tensor_tensor(_d, _h, _w, ALU.mult),
                      waits=[(ctx.es["tensor"], h_mm), (ctx.es["vector"], cev)],
                      inc=ctx.es["vector"])
        pr.bank_free[3 + th] = [(ctx.es["vector"], cg)]
        pr.bank_free[5] = [(ctx.es["vector"], cev)]
    ctx.barrier()
    if DEBUG_TAP == "f1_gates":
        dbg1 = pr.nc.dram_tensor("dbg_stat", [128, 160], F32, kind="ExternalOutput")
        dbg2 = pr.nc.dram_tensor("dbg_kslb", [128, 2048], BF16, kind="ExternalOutput")
        dbg3 = pr.nc.dram_tensor("dbg_rows", [8, 1024], F32, kind="ExternalOutput")
        dma(ctx, "sync", dbg1.ap(), pr.STAT[:, :], slot_sem=pr.s_store)
        dma(ctx, "sync", dbg2.ap(), pr.KSLB[:, 0:2048], slot_sem=pr.s_store)
        dma(ctx, "sync", dbg3.ap(), pr.ROWS[0:8, :], slot_sem=pr.s_store)
        ctx.barrier()

    def extra(oc, th, psum, cnt):
        return ctx.emit("tensor",
                        lambda e, _p=psum[:, :],
                               _t=pr.AUX[:, 4096 + oc * 128: 4096 + oc * 128 + 128],
                               _r=pr.KSLB[:, th * 512:(th + 1) * 512]:
                        e.matmul(_p, _t, _r, start=False, stop=True),
                        inc=ctx.es["tensor"])
    stream_proj(pr, "f1_WbT", (128, NCH, NCH, 128), NCH, NCH, y_rhs(pr),
                delta_writer(pr, pr.BIG, bias=pr.biasC["f1_bias"]),
                extra_accum=extra)
    ctx.barrier()



def k_writer(pr, kv_loc, rot):
    ctx = pr.ctx
    st = {"ktmp_free": {0: [], 1: []}, "sq_free": {0: [], 1: []},
          "krot_free": [], "scr_free": []}

    def wr(oc, th, psum, dep):
        ktmp = pr.AUX[:, th * 512: th * 512 + 512]
        sq = pr.AUX[:, 1024 + th * 512: 1024 + th * 512 + 512]
        c1 = ctx.emit("scalar", lambda e, _d=ktmp, _p=psum[:, :]: e.copy(_d, _p),
                      waits=[(ctx.es["tensor"], dep)] + st["ktmp_free"][th],
                      inc=ctx.es["scalar"])
        c2 = ctx.emit("scalar", lambda e, _d=sq, _p=psum[:, :]:
                      e.activation(_d, _p, AF.Square),
                      waits=st["sq_free"][th], inc=ctx.es["scalar"])
        bank = 6 + th
        cm = ctx.emit("tensor",
                      lambda e, _p=pr.PS[bank][0:1, :], _r=sq:
                      e.matmul(_p, pr.onescol_b[:, 0:1], _r, start=True, stop=True),
                      waits=[(ctx.es["scalar"], c2)] + pr.bank_free[bank],
                      inc=ctx.es["tensor"])
        st["sq_free"][th] = [(ctx.es["tensor"], cm)]
        cr = ctx.emit("vector",
                      lambda e, _d=pr.KNM[0:1, oc * 2 + th: oc * 2 + th + 1],
                             _p=pr.PS[bank][0:1, :]:
                      e.tensor_reduce(_d, _p, AX.X, ALU.max),
                      waits=[(ctx.es["tensor"], cm)], inc=ctx.es["vector"])
        pr.bank_free[bank] = [(ctx.es["vector"], cr)]
        if th == 0:
            st["c1a"] = c1
            return (ctx.es["scalar"], c1)
        if not rot:
            cs = dma(ctx, "sync", kv_loc.ap()[oc * 128:(oc + 1) * 128, :],
                     pr.AUX[:, 0:1024], slot_sem=pr.s_store,
                     waits=[(ctx.es["scalar"], max(c1, st["c1a"]))])
            st["ktmp_free"][0] = [(pr.s_store, cs)]
            st["ktmp_free"][1] = [(pr.s_store, cs)]
            return (ctx.es["scalar"], c1)
        lastv = None
        kt_fr = {0: [], 1: []}
        for b in range(2):
            cmm = ctx.emit("tensor",
                           lambda e, _r=pr.AUX[:, b * 512: b * 512 + 512]:
                           e.matmul(pr.PS[5][:, :], pr.rotPT[:, :], _r,
                                    start=True, stop=True),
                           waits=[(ctx.es["scalar"], max(c1, st["c1a"]))] +
                                 pr.bank_free[5],
                           inc=ctx.es["tensor"])
            cg = ctx.emit("gpsimd",
                          lambda e, _a=pr.AUX[:, b * 512: b * 512 + 512]:
                          e.tensor_tensor(pr.SCR[0][:, :], _a, pr.cosT[:, :], ALU.mult),
                          waits=st["scr_free"] + [(ctx.es["scalar"], max(c1, st["c1a"]))],
                          inc=ctx.es["gpsimd"])
            cv2 = ctx.emit("vector",
                           lambda e: e.tensor_tensor(pr.SCR[1][:, :], pr.PS[5][:, :],
                                                     pr.sinT[:, :], ALU.mult),
                           waits=[(ctx.es["tensor"], cmm)] + st["scr_free"],
                           inc=ctx.es["vector"])
            pr.bank_free[5] = [(ctx.es["vector"], cv2)]
            lastv = ctx.emit("vector",
                             lambda e, _d=pr.AUX[:, 2048 + b * 512: 2048 + b * 512 + 512]:
                             e.tensor_tensor(_d, pr.SCR[0][:, :], pr.SCR[1][:, :],
                                             ALU.add),
                             waits=[(ctx.es["gpsimd"], cg), (ctx.es["vector"], cv2)] +
                                   st["krot_free"],
                             inc=ctx.es["vector"])
            st["scr_free"] = [(ctx.es["vector"], lastv)]
            kt_fr[b] = [(ctx.es["tensor"], cmm), (ctx.es["gpsimd"], cg)]
        cs = dma(ctx, "sync", kv_loc.ap()[oc * 128:(oc + 1) * 128, :],
                 pr.AUX[:, 2048:3072], slot_sem=pr.s_store,
                 waits=[(ctx.es["vector"], lastv)])
        st["krot_free"] = [(pr.s_store, cs)]
        st["ktmp_free"][0] = kt_fr[0]
        st["ktmp_free"][1] = kt_fr[1]
        return (ctx.es["vector"], lastv)
    return wr


def v_writer(pr, kv_loc, row_base):
    ctx = pr.ctx
    st = {"aux_free": []}

    def wr(oc, th, psum, dep):
        vst = pr.AUX[:, 3072 + th * 512: 3072 + th * 512 + 512]
        c1 = ctx.emit("scalar", lambda e, _d=vst, _p=psum[:, :]: e.copy(_d, _p),
                      waits=[(ctx.es["tensor"], dep)] + st["aux_free"],
                      inc=ctx.es["scalar"])
        if th == 0:
            st["c1a"] = c1
        else:
            cs = dma(ctx, "sync",
                     kv_loc.ap()[row_base + oc * 128: row_base + (oc + 1) * 128, :],
                     pr.AUX[:, 3072:4096], slot_sem=pr.s_store,
                     waits=[(ctx.es["scalar"], max(c1, st["c1a"]))])
            st["aux_free"] = [(pr.s_store, cs)]
        return (ctx.es["scalar"], c1)
    return wr



def attn_units(pr, kv_all, n_kt, k_rows_fn, v_rows_fn, q_cols_fn, mtab_fn,
               knm_idx_fn, out_cols_fn, scale, units, group_fn=None,
               k_ready=(), v_ready=()):
    ctx = pr.ctx
    sem_k = ctx.new_sem("atk")
    sem_v = ctx.new_sem("atv")
    n_sl = (n_kt + 3) // 4
    LA = 4 if mtab_fn is None else 3
    SB = [0, 1, 6, 7]
    n_pt = 6 if mtab_fn is None else 4
    ctx.emit("vector", lambda e: e.tensor_scalar(pr.KNM[0:1, 64:128],
                                                 pr.KNM[0:1, 0:64], 0.5, None, ALU.mult),
             waits=[(ctx.es["vector"], ctx.vN())] + list(k_ready),
             inc=ctx.es["vector"])
    bias_done = []
    ptb0_free = []
    for u_i, u in enumerate(units):
        qc = q_cols_fn(u)
        csq = ctx.emit("scalar",
                       lambda e, _q=qc: e.activation(pr.PTB[0][:, :], _q, AF.Square),
                       waits=ptb0_free, inc=ctx.es["scalar"])
        cqn = ctx.emit("tensor",
                       lambda e: e.matmul(pr.PS[4][0:1, :], pr.onescol_b[:, 0:1],
                                          pr.PTB[0][:, :], start=True, stop=True),
                       waits=[(ctx.es["scalar"], csq)] + pr.bank_free[4],
                       inc=ctx.es["tensor"])
        ptb0_free = [(ctx.es["tensor"], cqn)]
        c = ctx.emit("vector",
                     lambda e: e.tensor_reduce(pr.ROWS[0:1, 512:513], pr.PS[4][0:1, :],
                                               AX.X, ALU.max),
                     waits=[(ctx.es["tensor"], cqn), (ctx.es["vector"], ctx.vN())],
                     inc=ctx.es["vector"])
        ki = knm_idx_fn(u)
        c = ctx.emit("vector",
                     lambda e: e.tensor_scalar(pr.ROWS[0:1, 512:513],
                                               pr.ROWS[0:1, 512:513], 0.5, None, ALU.mult),
                     waits=[(ctx.es["vector"], c)], inc=ctx.es["vector"])
        c = ctx.emit("vector",
                     lambda e, _k=pr.KNM[0:1, 64 + ki: 64 + ki + 1]:
                     e.tensor_scalar(pr.ROWS[0:1, 512:513], pr.ROWS[0:1, 512:513],
                                     _k, None, ALU.add),
                     waits=[(ctx.es["vector"], c)], inc=ctx.es["vector"])
        cbu = ctx.emit("vector",
                       lambda e, _s=-scale:
                       e.tensor_scalar(pr.ROWS[0:1, 512:513], pr.ROWS[0:1, 512:513],
                                       _s, None, ALU.mult),
                       waits=[(ctx.es["vector"], c)], inc=ctx.es["vector"])
        cbm = ctx.emit("tensor",
                       lambda e: e.matmul(pr.PS[4][:, 0:1], pr.ones_f[0:1, :],
                                          pr.ROWS[0:1, 512:513], start=True, stop=True),
                       waits=[(ctx.es["vector"], cbu)], inc=ctx.es["tensor"])
        cbe = ctx.emit("vector",
                       lambda e, _d=pr.STAT[:, u_i: u_i + 1]:
                       e.tensor_copy(_d, pr.PS[4][:, 0:1]),
                       waits=[(ctx.es["tensor"], cbm)], inc=ctx.es["vector"])
        pr.bank_free[4] = [(ctx.es["vector"], cbe)]
        bias_done.append((ctx.es["vector"], cbe))
    scr_free = {i: [] for i in range(3)}
    ptb_free = {i: [] for i in range(n_pt)}
    ks_mm = {}
    vt_pv = {}
    ck_all = cv_all = 0
    for gi, u in enumerate(units):
        new_group = (group_fn is None or gi == 0 or
                     group_fn(u) != group_fn(units[gi - 1]))
        if new_group:
            for sl in range(n_sl):
                def mk_k(e, _u=u, _sl=sl):
                    r, c0 = k_rows_fn(_u, _sl)
                    kt_ = kv_all[0] if isinstance(kv_all, tuple) else kv_all
                    return e.dma_start(out=pr.KSLB[:, _sl * 512:(_sl + 1) * 512],
                                       in_=kt_.ap()[bass.ds(r, 128), c0:c0 + 512])
                ctx.emit("sync", mk_k,
                         waits=[(ctx.es["tensor"], ks_mm.get(sl, 0))] + list(k_ready),
                         inc=sem_k, inc_by=16)
            for j in range(n_kt):
                def mk_v(e, _u=u, _j=j):
                    r, c0 = v_rows_fn(_u, _j)
                    vt_ = kv_all[1] if isinstance(kv_all, tuple) else kv_all
                    return e.dma_start(out=pr.VTB[:, _j * 128:(_j + 1) * 128],
                                       in_=vt_.ap()[bass.ds(r, 128), c0:c0 + 128],
                                       transpose=True)
                ctx.emit("sync", mk_v,
                         waits=[(ctx.es["tensor"], vt_pv.get(j, 0))] + list(v_ready),
                         inc=sem_v, inc_by=16)
            ck_all = sem_k.n
            cv_all = sem_v.n
        qc = q_cols_fn(u)
        bcol = pr.STAT[:, gi: gi + 1]
        bwait = bias_done[gi]
        otb = 2 if gi % 2 == 0 else 5
        exp_of = {}
        pv_last = None
        cs_last = None

        def pt_ap(j):
            sl_ = j % n_pt
            return pr.AUX[:, sl_ * 512: sl_ * 512 + 512]

        def emit_pv(j):
            nonlocal pv_last, cs_last
            pv_last = ctx.emit("tensor",
                               lambda e, _t=pr.VTB[:, j * 128:(j + 1) * 128],
                                      _r=pt_ap(j), _s=(j == 0), _e=(j == n_kt - 1),
                                      _o=pr.PS[otb][:, :]:
                               e.matmul(_o, _t, _r, start=_s, stop=_e),
                               waits=[exp_of[j]] +
                                     ([(sem_v, cv_all)] + pr.bank_free[otb]
                                      if j == 0 else []),
                               inc=ctx.es["tensor"])
            cs_last = ctx.emit("tensor",
                               lambda e, _r=pt_ap(j), _s=(j == 0), _e=(j == n_kt - 1):
                               e.matmul(pr.PS[3][0:1, :], pr.onescol_b[:, 0:1], _r,
                                        start=_s, stop=_e),
                               waits=(pr.bank_free[3] if j == 0 else []),
                               inc=ctx.es["tensor"])
            ptb_free[j % n_pt] = [(ctx.es["tensor"], cs_last)]
            vt_pv[j] = pv_last

        for j in range(n_kt):
            bk = SB[j % 4]
            cmm = ctx.emit("tensor",
                           lambda e, _pp=pr.PS[bk][:, :],
                                  _t=pr.KSLB[:, (j // 4) * 512 + (j % 4) * 128:
                                             (j // 4) * 512 + (j % 4) * 128 + 128],
                                  _q=qc:
                           e.matmul(_pp, _t, _q, start=True, stop=True),
                           waits=([(sem_k, ck_all)] if j == 0 else []) + pr.bank_free[bk],
                           inc=ctx.es["tensor"])
            ks_mm[j // 4] = cmm
            mcols = mtab_fn(u, j) if mtab_fn is not None else None
            if mcols is not None:
                sslot = j % 3
                cadd = ctx.emit("vector",
                                lambda e, _d=pr.SCR[sslot][:, :], _p=pr.PS[bk][:, :],
                                       _m=mcols:
                                e.tensor_tensor(_d, _p, _m, ALU.add),
                                waits=[(ctx.es["tensor"], cmm)] + scr_free[sslot],
                                inc=ctx.es["vector"])
                pr.bank_free[bk] = [(ctx.es["vector"], cadd)]
                cexp = ctx.emit("scalar",
                                lambda e, _d=pt_ap(j), _s=pr.SCR[sslot][:, :], _b=bcol:
                                e.activation(_d, _s, AF.Exp, bias=_b, scale=scale),
                                waits=[(ctx.es["vector"], cadd)] +
                                      ([bwait] if j == 0 else []) + ptb_free[j % n_pt],
                                inc=ctx.es["scalar"])
                scr_free[sslot] = [(ctx.es["scalar"], cexp)]
            else:
                cexp = ctx.emit("scalar",
                                lambda e, _d=pt_ap(j), _p=pr.PS[bk][:, :], _b=bcol:
                                e.activation(_d, _p, AF.Exp, bias=_b, scale=scale),
                                waits=[(ctx.es["tensor"], cmm)] +
                                      ([bwait] if j == 0 else []) + ptb_free[j % n_pt],
                                inc=ctx.es["scalar"])
                pr.bank_free[bk] = [(ctx.es["scalar"], cexp)]
            exp_of[j] = (ctx.es["scalar"], cexp)
            if j >= LA:
                emit_pv(j - LA)
        for j in range(max(0, n_kt - LA), n_kt):
            emit_pv(j)
        ce1 = ctx.emit("vector",
                       lambda e: e.tensor_copy(pr.ROWS[0:1, 0:512], pr.PS[3][0:1, :]),
                       waits=[(ctx.es["tensor"], cs_last)], inc=ctx.es["vector"])
        cb2 = ctx.emit("tensor",
                       lambda e: e.matmul(pr.PS[3][:, :], pr.ones_f[0:1, :],
                                          pr.ROWS[0:1, 0:512], start=True, stop=True),
                       waits=[(ctx.es["vector"], ce1)], inc=ctx.es["tensor"])
        ce2 = ctx.emit("vector",
                       lambda e: e.tensor_copy(pr.SCR[0][:, :], pr.PS[3][:, :]),
                       waits=[(ctx.es["tensor"], cb2)] + scr_free[0],
                       inc=ctx.es["vector"])
        pr.bank_free[3] = [(ctx.es["vector"], ce2)]
        cr1 = ctx.emit("vector",
                       lambda e: e.reciprocal(pr.SCR[1][:, :], pr.SCR[0][:, :]),
                       waits=[(ctx.es["vector"], ce2)] + scr_free[1],
                       inc=ctx.es["vector"])
        cno = ctx.emit("vector",
                       lambda e, _d=out_cols_fn(u), _o=pr.PS[otb][:, :]:
                       e.tensor_tensor(_d, _o, pr.SCR[1][:, :], ALU.mult),
                       waits=[(ctx.es["vector"], cr1)], inc=ctx.es["vector"])
        pr.bank_free[otb] = [(ctx.es["vector"], cno)]
        scr_free[0] = [(ctx.es["vector"], cr1)]
        scr_free[1] = [(ctx.es["vector"], cno)]



def ph_b0(pr):
    ctx = pr.ctx
    kv_loc = pr.dint("b0_kvloc", [8192, 1024], BF16)
    kv_allK = pr.dint("b0_kvallk", [32768, 1024], BF16)
    kv_allV = pr.dint("b0_kvallv", [32768, 1024], BF16)
    kv_win = pr.dint("b0_kvwin", [24576, 1024], BF16)
    knm_in = pr.dint("b0_knin", [1, 64], F32)
    knm_out = pr.dint("b0_knout", [1, 64], F32)
    ctx.need_pid.add("sync")

    stream_proj(pr, "b0_WqT", (128, NCH, NCH, 128), NCH, NCH, y_rhs(pr),
                delta_writer(pr, pr.BIG), rhs_ready=lambda cc: pr.y_ready[cc])
    ctx.barrier()
    scr_free = []
    for ch in range(NCH):
        for b in range(2):
            bk = 6 + (2 * ch + b) % 2
            qcols = pr.BIG[:, ch * 1024 + b * 512: ch * 1024 + b * 512 + 512]
            cmm = ctx.emit("tensor",
                           lambda e, _p=pr.PS[bk][:, :], _q=qcols:
                           e.matmul(_p, pr.rotPT[:, :], _q, start=True, stop=True),
                           waits=pr.bank_free[bk], inc=ctx.es["tensor"])
            cg = ctx.emit("gpsimd",
                          lambda e, _q=qcols:
                          e.tensor_tensor(pr.SCR[0][:, :], _q, pr.cosT[:, :], ALU.mult),
                          waits=scr_free, inc=ctx.es["gpsimd"])
            cv2 = ctx.emit("vector",
                           lambda e, _p=pr.PS[bk][:, :]:
                           e.tensor_tensor(pr.SCR[1][:, :], _p, pr.sinT[:, :], ALU.mult),
                           waits=[(ctx.es["tensor"], cmm)] + scr_free,
                           inc=ctx.es["vector"])
            pr.bank_free[bk] = [(ctx.es["vector"], cv2)]
            cvw = ctx.emit("vector",
                           lambda e, _d=qcols:
                           e.tensor_tensor(_d, pr.SCR[0][:, :], pr.SCR[1][:, :], ALU.add),
                           waits=[(ctx.es["gpsimd"], cg)], inc=ctx.es["vector"])
            scr_free = [(ctx.es["vector"], cvw)]
    ctx.barrier()
    stream_proj(pr, "b0_WkT", (128, NCH, NCH, 128), NCH, NCH, y_rhs(pr),
                k_writer(pr, kv_loc, rot=True), bank_n=4)
    ctx.barrier()
    dma(ctx, "sync", knm_in.ap(), pr.KNM[0:1, 0:64], slot_sem=pr.s_store)
    sc1 = ctx.new_sem("b0_agk")
    ctx.emit("gpsimd",
             lambda e: e.collective_compute(
                 "AllGather", ALU.bypass, replica_groups=[list(range(N_CORES))],
                 ins=[kv_loc.ap()[0:4096, :]], outs=[kv_allK.ap()]),
             waits=[(pr.s_store, pr.s_store.n)], inc=sc1)
    sc2 = ctx.new_sem("b0_knm")
    ctx.emit("gpsimd",
             lambda e: e.collective_compute(
                 "AllReduce", ALU.max, replica_groups=[list(range(N_CORES))],
                 ins=[knm_in.ap()], outs=[knm_out.ap()]),
             inc=sc2)
    sld = ctx.new_sem("b0_knl")
    dma(ctx, "gpsimd", pr.KNM[0:1, 0:64], knm_out.ap(), slot_sem=sld, waits=[(sc2, 1)])
    kv8k = kv_allK.ap().rearrange("(c r) w -> c r w", c=8)
    swin = ctx.new_sem("b0_wink")
    for dlt in range(3):
        def mk_win(e, _d=dlt):
            src = kv8k[bass.ds((ctx.pid["sync"] + (_d + 6)) % 8, 1), :, :]
            return e.dma_start(out=kv_win.ap()[_d * 4096:(_d + 1) * 4096, :],
                               in_=src.rearrange("o r w -> (o r) w"))
        ctx.emit("sync", mk_win, waits=[(sc1, 1)] if dlt == 0 else [],
                 inc=swin, inc_by=16)
    for j in range(12):
        for i in range(4):
            d = i + 8 - j
            cols = pr.AUX[:, 2048 + j * 512 + i * 128: 2048 + j * 512 + i * 128 + 128]
            if d == 0:
                ctx.emit("gpsimd", lambda e, _d=cols: e.tensor_copy(_d, pr.mask_diag[:, :]),
                         inc=ctx.es["gpsimd"])
            elif d == 8:
                ctx.emit("gpsimd", lambda e, _d=cols: e.tensor_copy(_d, pr.mask_tail[:, :]),
                         inc=ctx.es["gpsimd"])
            elif 1 <= d <= 7:
                ctx.emit("gpsimd", lambda e, _d=cols: e.memset(_d, 0.0),
                         inc=ctx.es["gpsimd"])
            else:
                ctx.emit("gpsimd", lambda e, _d=cols: e.memset(_d, NEG),
                         inc=ctx.es["gpsimd"])
        ctx.emit("gpsimd",
                 lambda e, _d=pr.AUX[:, 2048 + j * 512: 2048 + j * 512 + 512],
                        _v=pr.vcolB[:, j: j + 1]:
                 e.tensor_scalar(_d, _d, _v, None, ALU.add),
                 inc=ctx.es["gpsimd"])
    stream_proj(pr, "b0_WvT", (128, NCH, NCH, 128), NCH, NCH, y_rhs(pr),
                v_writer(pr, kv_loc, 4096), bank_n=4)
    ctx.barrier()
    scv = ctx.new_sem("b0_agv")
    ctx.emit("gpsimd",
             lambda e: e.collective_compute(
                 "AllGather", ALU.bypass, replica_groups=[list(range(N_CORES))],
                 ins=[kv_loc.ap()[4096:8192, :]], outs=[kv_allV.ap()]),
             waits=[(pr.s_store, pr.s_store.n)], inc=scv)
    kv8v = kv_allV.ap().rearrange("(c r) w -> c r w", c=8)
    for dlt in range(3):
        def mk_winv(e, _d=dlt):
            src = kv8v[bass.ds((ctx.pid["sync"] + (_d + 6)) % 8, 1), :, :]
            return e.dma_start(out=kv_win.ap()[12288 + _d * 4096: 12288 + (_d + 1) * 4096, :],
                               in_=src.rearrange("o r w -> (o r) w"))
        ctx.emit("sync", mk_winv, waits=[(scv, 1)] if dlt == 0 else [],
                 inc=swin, inc_by=16)
    win_k_ready = [(swin, 48), (sld, 16)]
    win_v_ready = [(swin, 96)]

    units = [(b, h) for b in range(2) for h in range(NCH)]

    def k_rows(u, sl):
        return (sl * 4096 + u[1] * 128, u[0] * 512)

    def v_rows(u, j):
        return (12288 + (j // 4) * 4096 + u[1] * 128,
                u[0] * 512 + (j % 4) * 128)

    attn_units(pr, kv_win, 12, k_rows, v_rows,
               q_cols_fn=lambda u: pr.BIG[:, u[1] * 1024 + u[0] * 512:
                                          u[1] * 1024 + u[0] * 512 + 512],
               mtab_fn=lambda u, j: pr.AUX[:, 2048 + j * 512: 2048 + j * 512 + 512],
               knm_idx_fn=lambda u: u[1] * 2 + u[0],
               out_cols_fn=lambda u: pr.BIG[:, u[1] * 1024 + u[0] * 512:
                                            u[1] * 1024 + u[0] * 512 + 512],
               scale=RS128, units=units,
               k_ready=win_k_ready, v_ready=win_v_ready)
    ctx.barrier()
    stream_proj(pr, "b0_WoT", (128, NCH, NCH, 128), NCH, NCH, big_rhs(pr),
                delta_writer(pr, pr.Y))
    ctx.barrier()



def ph_a1(pr):
    ctx = pr.ctx
    kv_loc = pr.dint("a1_kvloc", [2048, 1024], BF16)
    kv_allK = pr.dint("a1_kvallk", [8192, 1024], BF16)
    kv_allV = pr.dint("a1_kvallv", [8192, 1024], BF16)
    knm_in = pr.dint("a1_knin", [1, 64], F32)
    knm_out = pr.dint("a1_knout", [1, 64], F32)

    stream_proj(pr, "a1_WqT", (128, NCH, NCH, 128), NCH, NCH, y_rhs(pr),
                delta_writer(pr, pr.BIG), rhs_ready=lambda cc: pr.y_ready[cc])
    ctx.barrier()
    stream_proj(pr, "a1_WkT", (128, 8, NCH, 128), 8, NCH, y_rhs(pr),
                k_writer(pr, kv_loc, rot=False), bank_n=4)
    ctx.barrier()
    dma(ctx, "sync", knm_in.ap(), pr.KNM[0:1, 0:64], slot_sem=pr.s_store)
    sc1 = ctx.new_sem("a1_agk")
    ctx.emit("gpsimd",
             lambda e: e.collective_compute(
                 "AllGather", ALU.bypass, replica_groups=[list(range(N_CORES))],
                 ins=[kv_loc.ap()[0:1024, :]], outs=[kv_allK.ap()]),
             waits=[(pr.s_store, pr.s_store.n)], inc=sc1)
    sc2 = ctx.new_sem("a1_knm")
    ctx.emit("gpsimd",
             lambda e: e.collective_compute(
                 "AllReduce", ALU.max, replica_groups=[list(range(N_CORES))],
                 ins=[knm_in.ap()], outs=[knm_out.ap()]),
             inc=sc2)
    sld = ctx.new_sem("a1_knl")
    dma(ctx, "gpsimd", pr.KNM[0:1, 0:64], knm_out.ap(), slot_sem=sld, waits=[(sc2, 1)])
    stream_proj(pr, "a1_WvT", (128, 8, NCH, 128), 8, NCH, y_rhs(pr),
                v_writer(pr, kv_loc, 1024), bank_n=4)
    ctx.barrier()
    scv = ctx.new_sem("a1_agv")
    ctx.emit("gpsimd",
             lambda e: e.collective_compute(
                 "AllGather", ALU.bypass, replica_groups=[list(range(N_CORES))],
                 ins=[kv_loc.ap()[1024:2048, :]], outs=[kv_allV.ap()]),
             waits=[(pr.s_store, pr.s_store.n)], inc=scv)

    units = [(b, h) for b in range(2) for h in range(NCH)]
    attn_units(pr, (kv_allK, kv_allV), 32,
               k_rows_fn=lambda u, sl: (sl * 1024 + (u[1] // 4) * 128, u[0] * 512),
               v_rows_fn=lambda u, j: ((j // 4) * 1024 + (u[1] // 4) * 128,
                                       u[0] * 512 + (j % 4) * 128),
               q_cols_fn=lambda u: pr.BIG[:, u[1] * 1024 + u[0] * 512:
                                          u[1] * 1024 + u[0] * 512 + 512],
               mtab_fn=None,
               knm_idx_fn=lambda u: (u[1] // 4) * 2 + u[0],
               out_cols_fn=lambda u: pr.BIG[:, u[1] * 1024 + u[0] * 512:
                                            u[1] * 1024 + u[0] * 512 + 512],
               scale=RS128, units=units, group_fn=lambda u: (u[0], u[1] // 4),
               k_ready=[(sc1, 1), (sld, 16)], v_ready=[(scv, 1)])
    ctx.barrier()
    if DEBUG_TAP == "a1_attn":
        dbg = pr.nc.dram_tensor("dbg_big", [128, NCH * 1024], BF16,
                                kind="ExternalOutput")
        dma(ctx, "sync", dbg.ap(), pr.BIG[:, :], slot_sem=pr.s_store)
        ctx.barrier()
    if DEBUG_TAP == "a1_q":
        dbg = pr.nc.dram_tensor("dbg_big", [128, NCH * 1024], BF16,
                                kind="ExternalOutput")
        dma(ctx, "sync", dbg.ap(), pr.BIG[:, :], slot_sem=pr.s_store)
        ctx.barrier()
    stream_proj(pr, "a1_WoT", (128, NCH, NCH, 128), NCH, NCH, big_rhs(pr),
                delta_writer(pr, pr.Y))
    ctx.barrier()



def ph_b1(pr):
    ctx = pr.ctx
    hl_loc = pr.dint("b1_hloc", [128, 128], BF16)
    hl_all = pr.dint("b1_hall", [1024, 128], BF16)
    ctx.need_pid.add("sync")
    ecols = [(1,), (512,), (515,), (1026,)]
    cvh = None
    for e_i, (off,) in enumerate(ecols):
        src = pr.Y[:, off: off + (NCH - 1) * YST + 1: YST]
        cvh = ctx.emit("vector",
                       lambda e, _d=pr.VTB[:, 128 + e_i * 32: 128 + (e_i + 1) * 32], _s=src:
                       e.tensor_copy(_d, _s), inc=ctx.es["vector"])
    dma(ctx, "sync", hl_loc.ap(), pr.VTB[:, 128:256], slot_sem=pr.s_store,
        waits=[(ctx.es["vector"], cvh)])
    sc1 = ctx.new_sem("b1_ag")
    ctx.emit("gpsimd",
             lambda e: e.collective_compute(
                 "AllGather", ALU.bypass, replica_groups=[list(range(N_CORES))],
                 ins=[hl_loc.ap()], outs=[hl_all.ap()]),
             waits=[(pr.s_store, pr.s_store.n)], inc=sc1)
    sem_h = ctx.new_sem("b1_hl")
    ch1 = ctx.emit("sync",
                   lambda e: e.dma_start(
                       out=pr.VTB[:, 128:256],
                       in_=hl_all.ap()[bass.ds((ctx.pid["sync"] + 7) % 8 * 128, 128), :]),
                   waits=[(sc1, 1)], inc=sem_h, inc_by=16)
    ch2 = ctx.emit("sync",
                   lambda e: e.dma_start(
                       out=pr.VTB[:, 256:384],
                       in_=hl_all.ap()[bass.ds((ctx.pid["sync"] + 1) % 8 * 128, 128), :]),
                   inc=sem_h, inc_by=16)
    cl = ctx.emit("gpsimd",
                  lambda e: e.tensor_scalar(pr.VTB[:, 128:256], pr.VTB[:, 128:256],
                                            pr.hvB[:, 0:1], None, ALU.mult),
                  waits=[(sem_h, ch1)], inc=ctx.es["gpsimd"])
    cr = ctx.emit("gpsimd",
                  lambda e: e.tensor_scalar(pr.VTB[:, 256:384], pr.VTB[:, 256:384],
                                            pr.hvB[:, 1:2], None, ALU.mult),
                  waits=[(sem_h, ch2)], inc=ctx.es["gpsimd"])
    fills = [(128, 1, 0), (128, 3, 514), (256, 0, 513), (256, 2, 1027)]
    for bo, eidx, coloff in fills:
        ctx.emit("vector",
                 lambda e, _s=pr.VTB[:, bo + eidx * 32: bo + (eidx + 1) * 32],
                        _d=pr.Y[:, coloff: coloff + (NCH - 1) * YST + 1: YST]:
                 e.tensor_copy(_d, _s),
                 waits=[(ctx.es["gpsimd"], max(cl, cr))], inc=ctx.es["vector"])
    ctx.barrier()
    conv = pr.din("b1_conv", (3, 128, 24, NCH, 128))
    for oc in range(24):
        banks = [(oc % 2) * 2, (oc % 2) * 2 + 1]
        last = None
        for k in range(3):
            sl = pr.r_wg.next()
            c = dma(ctx, "gpsimd", sl.ap[:, 0: NCH * 128],
                    conv.ap()[k][:, oc].rearrange("p c j -> p (c j)"),
                    slot_sem=sl.sem, waits=sl.free_waits)
            for cc in range(NCH):
                tile = sl.ap[:, cc * 128:(cc + 1) * 128]
                for b in range(2):
                    bw = []
                    if cc == 0 and b == 0:
                        bw = [(sl.sem, c)]
                        if k == 0:
                            bw = bw + list(pr.bank_free[banks[0]]) + \
                                list(pr.bank_free[banks[1]])
                    rhs = pr.Y[:, cc * YST + k + b * 514: cc * YST + k + b * 514 + 512]
                    last = ctx.emit(
                        "tensor",
                        lambda e, _p=pr.PS[banks[b]][:, :], _t=tile, _r=rhs,
                               _s=(k == 0 and cc == 0), _e=(k == 2 and cc == NCH - 1):
                        e.matmul(_p, _t, _r, start=_s, stop=_e),
                        waits=bw, inc=ctx.es["tensor"])
            sl.free_waits = [(ctx.es["tensor"], last)]
        for b in range(2):
            dcols = pr.BIG[:, oc * 1024 + b * 512: oc * 1024 + b * 512 + 512]
            ce = ctx.emit("vector",
                          lambda e, _d=dcols, _p=pr.PS[banks[b]][:, :],
                                 _bc=pr.biasC["b1_convb"][:, oc:oc + 1]:
                          e.tensor_scalar(_d, _p, _bc, None, ALU.add),
                          waits=[(ctx.es["tensor"], last)], inc=ctx.es["vector"])
            pr.bank_free[banks[b]] = [(ctx.es["vector"], ce)]
    ctx.barrier()
    def qkv_rhs(base):
        return lambda cc, th: pr.BIG[:, (base + cc) * 1024 + th * 512:
                                     (base + cc) * 1024 + th * 512 + 512]
    stream_proj(pr, "b1_wq", (128, 8, 8, 128), 8, 8, qkv_rhs(0),
                delta_writer(pr, pr.Y, bias=pr.biasC["b1_bq"]), bank_n=4)
    stream_proj(pr, "b1_wk", (128, 8, 8, 128), 8, 8, qkv_rhs(8),
                delta_writer(pr, pr.Y, bias=pr.biasC["b1_bk"], col_off=8192),
                bank_n=4, bank_base=4)
    ctx.barrier()
    wv = pr.din("b1_wv_rhs", (128, 8, 1024))
    bvr = pr.din("b1_bv_row", (1, 1024))
    semv = ctx.new_sem("b1_wv")
    dma(ctx, "sync", pr.ROWS[0:1, :], bvr.ap(), slot_sem=semv)
    dma(ctx, "gpsimd", pr.AUX[:, 0:8192], wv.ap().rearrange("p c j -> p (c j)"),
        slot_sem=semv)
    prev = 0
    for th in range(2):
        cb = ctx.emit("tensor",
                      lambda e, _r=pr.ROWS[0:1, th * 512:(th + 1) * 512]:
                      e.matmul(pr.PS[6][:, :], pr.ones_f[0:1, :], _r,
                               start=True, stop=True),
                      waits=(pr.bank_free[6] + [(semv, 32)] if th == 0 else
                             [(ctx.es["vector"], prev)]),
                      inc=ctx.es["tensor"])
        prev = ctx.emit("vector",
                        lambda e, _d=pr.NB[:, th * 512:(th + 1) * 512]:
                        e.tensor_copy(_d, pr.PS[6][:, :]),
                        waits=[(ctx.es["tensor"], cb)], inc=ctx.es["vector"])
    cbv_done = prev
    pr.bank_free[6] = [(ctx.es["vector"], cbv_done)]
    for tt in range(8):
        banks = [(tt % 2) * 2, (tt % 2) * 2 + 1]
        last = None
        for wc in range(8):
            tile = pr.BIG[:, (16 + wc) * 1024 + tt * 128: (16 + wc) * 1024 + tt * 128 + 128]
            for th in range(2):
                bw = []
                if wc == 0:
                    bw = list(pr.bank_free[banks[th]])
                    if th == 0 and tt == 0:
                        bw = bw + [(semv, 32)]
                last = ctx.emit(
                    "tensor",
                    lambda e, _p=pr.PS[banks[th]][:, :], _t=tile,
                           _r=pr.AUX[:, wc * 1024 + th * 512: wc * 1024 + th * 512 + 512],
                           _s=(wc == 0), _e=(wc == 7):
                    e.matmul(_p, _t, _r, start=_s, stop=_e),
                    waits=bw, inc=ctx.es["tensor"])
        for th in range(2):
            dcols = pr.Y[:, 16384 + tt * 1024 + th * 512:
                         16384 + tt * 1024 + th * 512 + 512]
            cvv = ctx.emit("vector",
                           lambda e, _d=dcols, _p=pr.PS[banks[th]][:, :],
                                  _b=pr.NB[:, th * 512:(th + 1) * 512]:
                           e.tensor_tensor(_d, _p, _b, ALU.add),
                           waits=[(ctx.es["tensor"], last),
                                  (ctx.es["vector"], cbv_done)],
                           inc=ctx.es["vector"])
            pr.bank_free[banks[th]] = [(ctx.es["vector"], cvv)]
    ctx.barrier()
    ctx = pr.ctx
    gcol = pr.STAT[:, 0:1]
    rmax = pr.ROWS[0:1, 512:513]
    rtmp = pr.ROWS[0:1, 513:514]
    cz = ctx.emit("gpsimd", lambda e: e.memset(rmax, 0.0), inc=ctx.es["gpsimd"])
    cprev = 0
    sqf = []
    for idx in range(16):
        for th in range(2):
            ysl = pr.Y[:, idx * 1024 + th * 512: idx * 1024 + th * 512 + 512]
            csq = ctx.emit("scalar",
                           lambda e, _y=ysl: e.activation(pr.PTB[0][:, :], _y,
                                                          AF.Square),
                           waits=sqf, inc=ctx.es["scalar"])
            cqn = ctx.emit("tensor",
                           lambda e: e.matmul(pr.PS[4][0:1, :], pr.onescol_b[:, 0:1],
                                              pr.PTB[0][:, :], start=True, stop=True),
                           waits=[(ctx.es["scalar"], csq)] + pr.bank_free[4],
                           inc=ctx.es["tensor"])
            sqf = [(ctx.es["tensor"], cqn)]
            cr = ctx.emit("vector",
                          lambda e: e.tensor_reduce(rtmp, pr.PS[4][0:1, :], AX.X,
                                                    ALU.max),
                          waits=[(ctx.es["tensor"], cqn), (ctx.es["vector"], cprev),
                                 (ctx.es["gpsimd"], cz)],
                          inc=ctx.es["vector"])
            pr.bank_free[4] = [(ctx.es["vector"], cr)]
            cprev = ctx.emit("vector",
                             lambda e: e.tensor_tensor(rmax, rmax, rtmp, ALU.max),
                             waits=[(ctx.es["vector"], cr)], inc=ctx.es["vector"])
    cbu = ctx.emit("vector",
                   lambda e: e.tensor_scalar(rmax, rmax, -RS64, None, ALU.mult),
                   waits=[(ctx.es["vector"], cprev)], inc=ctx.es["vector"])
    cbm = ctx.emit("tensor",
                   lambda e: e.matmul(pr.PS[4][:, 0:1], pr.ones_f[0:1, :], rmax,
                                      start=True, stop=True),
                   waits=[(ctx.es["vector"], cbu)] + pr.bank_free[4],
                   inc=ctx.es["tensor"])
    cbe = ctx.emit("vector", lambda e: e.tensor_copy(gcol, pr.PS[4][:, 0:1]),
                   waits=[(ctx.es["tensor"], cbm)], inc=ctx.es["vector"])
    pr.bank_free[4] = [(ctx.es["vector"], cbe)]
    bias_ready = (ctx.es["vector"], cbe)
    SBK = [0, 1, 6, 7]
    scr_lo = {i: [] for i in range(3)}
    scr_hi = {i: [] for i in range(3)}
    pt_free = {i: [] for i in range(8)}
    row_free = {0: [], 1: []}
    ui = 0
    for b in range(2):
        for blk in range(4):
            tti = b * 4 + blk
            for h in range(16):
                hc = h // 2
                pb = (h % 2) * 64
                base_q = hc * 1024 + b * 512 + blk * 128
                qt = pr.Y[pb:pb + 64, base_q: base_q + 128]
                kt = pr.Y[pb:pb + 64, 8192 + base_q: 8192 + base_q + 128]
                vt = pr.Y[:, 16384 + tti * 1024 + h * 64:
                          16384 + tti * 1024 + h * 64 + 64]
                sb = SBK[ui % 4]
                otb = 2 if ui % 2 == 0 else 5
                csb = 3 if ui % 2 == 0 else 4
                ptc = pr.AUX[:, (ui % 8) * 128: (ui % 8) * 128 + 128]
                ss = ui % 3
                rrow = pr.ROWS[0:1, (ui % 2) * 128: (ui % 2) * 128 + 128]
                cmm = ctx.emit("tensor",
                               lambda e, _p=pr.PS[sb][:, 0:128], _k=kt, _q=qt:
                               e.matmul(_p, _k, _q, start=True, stop=True),
                               waits=pr.bank_free[sb], inc=ctx.es["tensor"])
                cadd = ctx.emit("vector",
                                lambda e, _d=pr.SCR[ss][:, 0:128],
                                       _p=pr.PS[sb][:, 0:128]:
                                e.tensor_tensor(_d, _p, pr.mask_diag[:, :], ALU.add),
                                waits=[(ctx.es["tensor"], cmm)] + scr_lo[ss],
                                inc=ctx.es["vector"])
                pr.bank_free[sb] = [(ctx.es["vector"], cadd)]
                cexp = ctx.emit("scalar",
                                lambda e, _d=ptc, _s=pr.SCR[ss][:, 0:128]:
                                e.activation(_d, _s, AF.Exp, bias=gcol, scale=RS64),
                                waits=[(ctx.es["vector"], cadd), bias_ready] +
                                      pt_free[ui % 8],
                                inc=ctx.es["scalar"])
                scr_lo[ss] = [(ctx.es["scalar"], cexp)]
                cpv = ctx.emit("tensor",
                               lambda e, _o=pr.PS[otb][0:64, 0:128], _v=vt, _r=ptc:
                               e.matmul(_o, _v, _r, start=True, stop=True),
                               waits=[(ctx.es["scalar"], cexp)] + pr.bank_free[otb],
                               inc=ctx.es["tensor"])
                ccs = ctx.emit("tensor",
                               lambda e, _p=pr.PS[csb][0:1, 0:128], _r=ptc:
                               e.matmul(_p, pr.onescol_b[:, 0:1], _r,
                                        start=True, stop=True),
                               waits=pr.bank_free[csb], inc=ctx.es["tensor"])
                pt_free[ui % 8] = [(ctx.es["tensor"], ccs)]
                ce1 = ctx.emit("vector",
                               lambda e, _r=rrow, _p=pr.PS[csb][0:1, 0:128]:
                               e.tensor_copy(_r, _p),
                               waits=[(ctx.es["tensor"], ccs)] + row_free[ui % 2],
                               inc=ctx.es["vector"])
                cb3 = ctx.emit("tensor",
                               lambda e, _p=pr.PS[csb][:, 0:128], _r=rrow:
                               e.matmul(_p, pr.ones_f[0:1, :], _r,
                                        start=True, stop=True),
                               waits=[(ctx.es["vector"], ce1)], inc=ctx.es["tensor"])
                row_free[ui % 2] = [(ctx.es["tensor"], cb3)]
                cln = ctx.emit("scalar",
                               lambda e, _d=pr.SCR[ss][0:64, 128:256],
                                      _p=pr.PS[csb][0:64, 0:128]:
                               e.activation(_d, _p, AF.Ln),
                               waits=[(ctx.es["tensor"], cb3)] + scr_hi[ss],
                               inc=ctx.es["scalar"])
                pr.bank_free[csb] = [(ctx.es["scalar"], cln)]
                crcp = ctx.emit("scalar",
                                lambda e, _d=pr.SCR[ss][0:64, 256:384],
                                       _s=pr.SCR[ss][0:64, 128:256]:
                                e.activation(_d, _s, AF.Exp, scale=-1.0),
                                inc=ctx.es["scalar"])
                ono = pr.BIG[pb:pb + 64, base_q: base_q + 128]
                cdv = ctx.emit("vector",
                               lambda e, _d=ono, _o=pr.PS[otb][0:64, 0:128],
                                      _r=pr.SCR[ss][0:64, 256:384]:
                               e.tensor_tensor(_d, _o, _r, ALU.mult),
                               waits=[(ctx.es["scalar"], crcp),
                                      (ctx.es["tensor"], cpv)],
                               inc=ctx.es["vector"])
                pr.bank_free[otb] = [(ctx.es["vector"], cdv)]
                scr_hi[ss] = [(ctx.es["vector"], cdv)]
                ui += 1
    ctx.barrier()
    stream_proj(pr, "b1_wo", (128, 8, 8, 128), 8, 8, big_rhs(pr),
                delta_writer(pr, pr.AUX, bias=pr.biasC["b1_bo"]), bank_n=4)
    ctx.barrier()
    stream_proj(pr, "b1_proj", (128, NCH, 8, 128), NCH, 8,
                lambda cc, th: pr.AUX[:, cc * 1024 + th * 512: cc * 1024 + th * 512 + 512],
                delta_writer(pr, pr.Y, bias=pr.biasC["b1_projb"]))
    ctx.barrier()



PHASES = {1: "attn_a", 2: "ffn0", 3: "b0", 4: "ffn0",
          5: "a1", 6: "ffn1", 7: "b1", 8: "ffn1"}


def emit_sublayer(pr, k, first, stats_next):
    kind = PHASES[k]
    if first:
        pr.pass_A(padded=(kind == "b1"))
    else:
        pr.pass_A_tail()
    pr.pass_B(padded=(kind == "b1"),
              barrier=(kind not in ("ffn0", "b0", "a1", "ffn1")))
    if kind == "attn_a":
        ph_attn_a(pr, stats_next)
        return
    if kind == "ffn0":
        ph_ffn0(pr)
        pr.pass_D(pr.BIG, stats_next=stats_next)
    elif kind == "ffn1":
        ph_ffn1(pr)
        pr.pass_D(pr.BIG, stats_next=stats_next)
    elif kind == "b0":
        ph_b0(pr)
        pr.pass_D(pr.Y, stats_next=stats_next)
    elif kind == "a1":
        ph_a1(pr)
        pr.pass_D(pr.Y, stats_next=stats_next)
    elif kind == "b1":
        ph_b1(pr)
        pr.pass_D(pr.Y, stats_next=stats_next)


def build_launch(sublayers, mode):
    nc = bass.Bass("TRN2", target_bir_lowering=False, debug=False)
    with ExitStack() as stack:
        pr = Prog(nc, stack)
        x_in = pr.din("x_in", (NCH, 128, TLOC))
        x_out = nc.dram_tensor("x_out", [NCH, 128, TLOC], F32, kind="ExternalOutput")
        pr.load_consts()
        if mode == "fused" and len(sublayers) > 1:
            xm = pr.dint("x_master", (NCH, 128, TLOC), F32)
            dma(pr.ctx, "sync", xm.ap()[:, :, :], x_in.ap()[:, :, :],
                slot_sem=pr.s_store)
            pr.ctx.barrier()
            for i, k in enumerate(sublayers):
                pr.x_src = xm
                pr.x_dst = xm if i < len(sublayers) - 1 else x_out
                nxt = PHASES[sublayers[i + 1]] if i + 1 < len(sublayers) else None
                emit_sublayer(pr, k, i == 0, nxt)
        else:
            for k in sublayers:
                pr.x_src = x_in
                pr.x_dst = x_out
                emit_sublayer(pr, k, True, None)
        for e in ENGS:
            pr.ctx.emit(e, None, waits=[(pr.s_store, pr.s_store.n)])
        block = stack.enter_context(nc.Block())
        pr.ctx.finalize(block)
        names = set(pr.input_names)
    return nc, names


def kernel(**inputs):
    sh, per_core, inv = prep_host(inputs)
    groups = [SUBLAYERS] if MODE == "fused" else [[k] for k in SUBLAYERS]
    xcur = [pc["x_in"] for pc in per_core]
    for group in groups:
        nc, input_names = build_launch(group, MODE)
        in_maps = []
        for c in range(N_CORES):
            m = {}
            for nm in input_names:
                if nm == "x_in":
                    m[nm] = xcur[c]
                elif nm in per_core[c]:
                    m[nm] = per_core[c][nm]
                else:
                    m[nm] = sh[nm]
            in_maps.append(m)
        res = run_bass_kernel_spmd(nc, in_maps, core_ids=list(range(N_CORES)))
        for key in res.results[0]:
            if key.startswith("dbg"):
                DEBUG_RES[key] = [np.asarray(r[key]) for r in res.results]
        xcur = [r["x_out"] for r in res.results]

    out = np.empty((B, SEQ, D), np.float32)
    for c in range(N_CORES):
        xf = xcur[c].reshape(D, B, SLOC)
        out[:, SLOC * c: SLOC * (c + 1), :] = xf.transpose(1, 2, 0)
    return np.ascontiguousarray(out[:, :, inv])
